# version 05
# baseline (speedup 1.0000x reference)
import math
import numpy as np
from contextlib import ExitStack

import concourse.bass as bass
import concourse.mybir as mybir
from concourse.bass_utils import run_bass_kernel_spmd

F32 = mybir.dt.float32
BF16 = mybir.dt.bfloat16
I32 = mybir.dt.int32
AF = mybir.ActivationFunctionType
ALU = mybir.AluOpType
AX = mybir.AxisListType

B, S, D = 4, 2048, 4096
NH, HD = 32, 128
INTER = 11008
NI = INTER // 128
NKC = D // 128
TOK = 1024
SCALE = HD ** -0.5
EPS = 1e-5
MARGIN = 4.0
SLOPES = [2.0 ** (-0.25 * (h + 1)) for h in range(NH)]
NEG = -1e30
NO_COLVEC_H = 11

ENGS = ("pe", "act", "dve", "pool", "sync")
ENG_ATTR = {"pe": "tensor", "act": "scalar", "dve": "vector", "pool": "gpsimd", "sync": "sync"}
DMA_ENGS = ("sync", "act")
NLANES = 12


class Prog:

    def __init__(self, nc, ctx, mark=None):
        import bisect
        self._bisect = bisect
        self.nc = nc
        self.collect = mark is None
        self.mark = mark
        self.sems = {}
        for e in ENGS:
            self.sems[e] = ctx.enter_context(nc.semaphore(f"s_{e}"))
        for e in DMA_ENGS:
            for ln in range(NLANES):
                self.sems[f"{e}_dma{ln}"] = ctx.enter_context(nc.semaphore(f"s_{e}d{ln}"))
        self.dma_rr = {e: 0 for e in DMA_ENGS}
        self.cnt = {s: 0 for s in self.sems}
        self.last_w = {}
        self.readers = {}
        self.waited = {e: {} for e in ENGS}
        self.used = {s: set() for s in self.sems}
        self.phase_ops = {e: [] for e in ENGS}

    def _remap_val(self, sem, val):
        if "_dma" in sem:
            return val
        lst = self.mark[sem]
        i = self._bisect.bisect_left(lst, val)
        assert i < len(lst), f"remap miss {sem} {val}"
        return i + 1

    def _is_marked(self, sem, val):
        if "_dma" in sem:
            return True
        lst = self.mark[sem]
        i = self._bisect.bisect_left(lst, val)
        return i < len(lst) and lst[i] == val

    def op(self, eng, fn, reads=(), writes=(), dma=False, extra=()):
        if dma:
            ln = self.dma_rr[eng] % NLANES
            self.dma_rr[eng] += 1
            sem = f"{eng}_dma{ln}"
            extra = list(extra) + [(sem, self.cnt[sem])]
        else:
            sem = eng
        toks = {}

        def add(tok):
            if tok is None:
                return
            s, v = tok
            if toks.get(s, 0) < v:
                toks[s] = v

        for k in reads:
            add(self.last_w.get(k))
        for k in writes:
            add(self.last_w.get(k))
            for t in self.readers.get(k, ()):
                add(t)
        for t in extra:
            add(t)

        waits = []
        for s, v in sorted(toks.items()):
            if v <= 0:
                continue
            if s == "pe" and eng == "pe" and not dma:
                continue
            if self.waited[eng].get(s, 0) >= v:
                continue
            self.waited[eng][s] = v
            waits.append((s, v))
            if self.collect:
                self.used[s].add(v)

        self.cnt[sem] += 16 if dma else 1
        tok = (sem, self.cnt[sem])
        for k in reads:
            self.readers.setdefault(k, []).append(tok)
        for k in writes:
            self.last_w[k] = tok
            self.readers[k] = []

        if not self.collect:
            do_inc = self._is_marked(sem, self.cnt[sem])
            rw = [(self.sems[s], self._remap_val(s, v)) for s, v in waits]
            my_sem = self.sems[sem]
            inc_amt = 16 if dma else 1

            def emit(engine, rw=rw, do_inc=do_inc, fn=fn, my_sem=my_sem, inc_amt=inc_amt):
                for sh, v in rw:
                    engine.wait_ge(sh, v)
                instr = fn(engine)
                if do_inc:
                    instr.then_inc(my_sem, inc_amt)

            self.phase_ops[eng].append(emit)
        return tok

    def barrier(self):
        snap = [(s, self.cnt[s]) for s in self.sems if self.cnt[s] > 0]
        for e in ENGS:
            ex = [(s, v) for s, v in snap if s != e]
            self.op(e, lambda eng: eng.nop(), extra=ex)

    def flush(self, block):
        if self.collect or block is None:
            for e in ENGS:
                self.phase_ops[e] = []
            return
        for e in ENGS:
            ops = self.phase_ops[e]
            if not ops:
                continue
            self.phase_ops[e] = []
            section = getattr(block, ENG_ATTR[e])

            def body(engine, ops=ops):
                for o in ops:
                    o(engine)

            section(body)


def build_core_program(mark=None, debug=False):
    nc = bass.Bass("TRN2", target_bir_lowering=False)

    xt = nc.dram_tensor("xt", [D, S], F32, kind="ExternalInput")
    wqkv = nc.dram_tensor("wqkv", [D, 4352], F32, kind="ExternalInput")
    wo = nc.dram_tensor("wo", [D, D], F32, kind="ExternalInput")
    wgt = nc.dram_tensor("wgt", [D, INTER], F32, kind="ExternalInput")
    wut = nc.dram_tensor("wut", [D, INTER], F32, kind="ExternalInput")
    wdt = nc.dram_tensor("wdt", [INTER, D], F32, kind="ExternalInput")
    ag_in = nc.dram_tensor("ag", [128, NI], F32, kind="ExternalInput")
    au_in = nc.dram_tensor("au", [128, NI], F32, kind="ExternalInput")
    ad_in = nc.dram_tensor("ad", [128, NKC], F32, kind="ExternalInput")
    alibi_in = nc.dram_tensor("alibi", [128, NH * 2 * 16], F32, kind="ExternalInput")
    out = nc.dram_tensor("out", [TOK, D], F32, kind="ExternalOutput")

    spill_kind = "ExternalOutput" if debug else "Internal"
    attn_spill = nc.dram_tensor("attn_spill", [D, TOK], BF16, kind=spill_kind)
    rbuf = nc.dram_tensor("rbuf", [D, TOK], F32, kind=spill_kind)
    hbuf = nc.dram_tensor("hbuf", [INTER, TOK], BF16, kind=spill_kind)
    r2buf = nc.dram_tensor("r2buf", [D, TOK], F32, kind=spill_kind)
    if debug:
        dbg_kmax = nc.dram_tensor("dbg_kmax", [128, 1], F32, kind="ExternalOutput")
        dbg_kfm = nc.dram_tensor("dbg_kfm", [128, 2048], BF16, kind="ExternalOutput")
        dbg_bias = nc.dram_tensor("dbg_bias", [128, 64], F32, kind="ExternalOutput")
        dbg_cvec = nc.dram_tensor("dbg_cvec", [128, 4], F32, kind="ExternalOutput")
        dbg_qn2 = nc.dram_tensor("dbg_qn2", [128, 4], F32, kind="ExternalOutput")
        dbg_qsb = nc.dram_tensor("dbg_qsb", [128, 2048], BF16, kind="ExternalOutput")
        dbg_acc = nc.dram_tensor("dbg_acc", [128, 2048], F32, kind="ExternalOutput")
        dbg_dqf = nc.dram_tensor("dbg_dqf", [128, 512], F32, kind="ExternalOutput")
        dbg_xown = nc.dram_tensor("dbg_xown", [128, NKC * 1024], BF16, kind="ExternalOutput")

    with ExitStack() as top:
        prog = Prog(nc, top, mark=mark)
        p = prog.op
        block = None if mark is None else top.enter_context(nc.Block())

        def sb(stack, name, shape, dtype):
            return stack.enter_context(nc.sbuf_tensor(name, shape, dtype))

        ps = [top.enter_context(nc.psum_tensor(f"ps{i}", [128, 512], F32)) for i in range(7)]
        ps_t16 = top.enter_context(nc.psum_tensor("ps_t16", [128, 128], BF16))

        eps_t = sb(top, "eps_t", [128, 1], F32)
        ones16 = sb(top, "ones16", [128, 128], BF16)
        ones32f = sb(top, "ones32f", [128, 128], F32)
        ident16 = sb(top, "ident16", [128, 128], BF16)
        ident32 = sb(top, "ident32", [128, 128], F32)
        dq_i = sb(top, "dq_i", [128, 512], I32)
        dq_f = sb(top, "dq_f", [128, 512], F32)
        alibi_t = sb(top, "alibi_t", [128, NH * 2 * 16], F32)
        ag_t = sb(top, "ag_t", [128, NI], F32)
        au_t = sb(top, "au_t", [128, NI], F32)
        ad_t = sb(top, "ad_t", [128, NKC], F32)
        kmax2 = sb(top, "kmax2", [128, 1], F32)
        kn_tmp = sb(top, "kn_tmp", [128, 1], F32)
        stage_w = sb(top, "stage_w", [128, 4, 512], F32)
        stage_b = sb(top, "stage_b", [128, 2, 1024], F32)
        mean_bc = sb(top, "mean_bc", [128, 1024], F32)
        rstd_bc = sb(top, "rstd_bc", [128, 1024], F32)
        m2_raw = sb(top, "m2_raw", [128, 1024], F32)
        msq2_raw = sb(top, "msq2_raw", [128, 1024], F32)
        bias_all = sb(top, "bias_all", [128, 4, 16], F32)
        cvec = sb(top, "cvec", [128, 4], F32)
        maskb = sb(top, "maskb", [128, 4, 512], F32)
        qn2 = sb(top, "qn2", [128, 4], F32)

        def conv(eng, dst_ap_fn, src_ap_fn, reads, writes):
            if eng == "act":
                p("act", lambda e: e.copy(out=dst_ap_fn(), in_=src_ap_fn()),
                  reads=reads, writes=writes)
            else:
                p("dve", lambda e: e.tensor_copy(out=dst_ap_fn(), in_=src_ap_fn()),
                  reads=reads, writes=writes)

        p("pool", lambda g: g.memset(eps_t[:, :], EPS), writes=[("eps",)])
        p("pool", lambda g: g.memset(ones16[:, :], 1.0), writes=[("ones16",)])
        p("pool", lambda g: g.memset(ones32f[:, :], 1.0), writes=[("ones32f",)])
        p("pool", lambda g: g.memset(ident32[:, :], 0.0), writes=[("id32",)])
        p("pool", lambda g: g.affine_select(out=ident32[:, :], in_=ident32[:, :],
                                            compare_op=ALU.not_equal, fill=1.0, base=0,
                                            pattern=[[-1, 128]], channel_multiplier=1),
          reads=[("id32",)], writes=[("id32",)])
        p("act", lambda e: e.copy(out=ident16[:, :], in_=ident32[:, :]),
          reads=[("id32",)], writes=[("id16",)])
        p("pool", lambda g: g.iota(out=dq_i[:, :], pattern=[[1, 512]], base=0,
                                   channel_multiplier=0), writes=[("dqi",)])
        for pi in range(4):
            p("pool", lambda g, pi=pi: g.memset(maskb[:, pi, :], 0.0), writes=[("mask", pi)])
            p("pool", lambda g, pi=pi: g.affine_select(out=maskb[:, pi, :], in_=maskb[:, pi, :],
                                                       compare_op=ALU.is_ge, fill=NEG,
                                                       base=-128 * pi, pattern=[[1, 512]],
                                                       channel_multiplier=-1),
              reads=[("mask", pi)], writes=[("mask", pi)])
        p("dve", lambda v: v.tensor_copy(out=dq_f[:, :], in_=dq_i[:, :]),
          reads=[("dqi",)], writes=[("dqf",)])
        p("sync", lambda s: s.dma_start(out=alibi_t[:, :], in_=alibi_in[:, :]),
          writes=[("alibi",)], dma=True)
        p("sync", lambda s: s.dma_start(out=ag_t[:, :], in_=ag_in[:, :]), writes=[("ag",)], dma=True)
        p("sync", lambda s: s.dma_start(out=au_t[:, :], in_=au_in[:, :]), writes=[("au",)], dma=True)
        p("sync", lambda s: s.dma_start(out=ad_t[:, :], in_=ad_in[:, :]), writes=[("ad",)], dma=True)
        prog.flush(block)

        with ExitStack() as sA:
            x_own = sb(sA, "x_own", [128, NKC, 1024], BF16)
            k_fm = sb(sA, "k_fm", [128, 2048], BF16)
            v_tok = sb(sA, "v_tok", [128, 2048], BF16)

            with ExitStack() as sA1:
                x_oth = sb(sA1, "x_oth", [128, NKC, 1024], BF16)
                v_fm = sb(sA1, "v_fm", [128, 2048], BF16)
                wkv_bf = sb(sA1, "wkv_bf", [128, 4, 128], BF16)
                ksq = sb(sA1, "ksq", [128, 512], BF16)

                for j in range(2 * NKC):
                    kc, half = j % NKC, j // NKC
                    dst = x_own if half == 0 else x_oth
                    slot = j % 2
                    p("sync", lambda s, kc=kc, half=half, slot=slot:
                        s.dma_start(out=stage_b[:, slot, :],
                                    in_=xt[kc * 128:(kc + 1) * 128,
                                           half * 1024:(half + 1) * 1024]),
                      writes=[("stb", slot)], dma=True)
                    conv("act" if j % 4 == 0 else "dve",
                         lambda dst=dst, kc=kc: dst[:, kc, :],
                         lambda slot=slot: stage_b[:, slot, :],
                         reads=[("stb", slot)], writes=[("x", half, kc)])

                for ocp in range(2):
                    for kc in range(NKC):
                        slot = kc % 4
                        p("sync", lambda s, ocp=ocp, kc=kc, slot=slot:
                            s.dma_start(out=stage_w[:, slot, 0:128],
                                        in_=wqkv[kc * 128:(kc + 1) * 128,
                                                 4096 + ocp * 128:4096 + (ocp + 1) * 128]),
                          writes=[("stw", slot)], dma=True)
                        conv("act" if kc % 4 == 3 else "dve",
                             lambda slot=slot: wkv_bf[:, slot, :],
                             lambda slot=slot: stage_w[:, slot, 0:128],
                             reads=[("stw", slot)], writes=[("wkv", slot)])
                        for tq in range(4):
                            half, col = tq // 2, (tq % 2) * 512
                            src = x_own if half == 0 else x_oth
                            p("pe", lambda t, kc=kc, slot=slot, tq=tq, src=src, col=col:
                                t.matmul(ps[tq][:, :], wkv_bf[:, slot, :],
                                         src[:, kc, col:col + 512],
                                         start=(kc == 0), stop=(kc == NKC - 1)),
                              reads=[("wkv", slot), ("x", half, kc)], writes=[("ps", tq)])
                    dstt = k_fm if ocp == 0 else v_fm
                    for tq in range(4):
                        p("act", lambda e, tq=tq, dstt=dstt:
                            e.copy(out=dstt[:, tq * 512:(tq + 1) * 512], in_=ps[tq][:, :]),
                          reads=[("ps", tq)], writes=[("kv", ocp, tq)])
                        if ocp == 0:
                            p("act", lambda e, tq=tq:
                                e.square(out=ksq[:, :], in_=k_fm[:, tq * 512:(tq + 1) * 512]),
                              reads=[("kv", 0, tq)], writes=[("ksq",)])
                            p("pe", lambda t: t.matmul(ps[4][:, :], ones16[:, :], ksq[:, :],
                                                       start=True, stop=True),
                              reads=[("ksq",), ("ones16",)], writes=[("ps", 4)])
                            if tq == 0:
                                p("dve", lambda v: v.reduce_max(out=kmax2[:, :], in_=ps[4][:, :],
                                                                axis=AX.X),
                                  reads=[("ps", 4)], writes=[("kmax2",)])
                            else:
                                p("dve", lambda v: v.reduce_max(out=kn_tmp[:, :], in_=ps[4][:, :],
                                                                axis=AX.X),
                                  reads=[("ps", 4)], writes=[("kntmp",)])
                                p("dve", lambda v: v.tensor_tensor(out=kmax2[:, :], in0=kmax2[:, :],
                                                                   in1=kn_tmp[:, :], op=ALU.max),
                                  reads=[("kntmp",), ("kmax2",)], writes=[("kmax2",)])
                if debug:
                    p("sync", lambda s: s.dma_start(out=dbg_xown[:, :],
                                                    in_=x_own[:, :, :].rearrange("p a b -> p (a b)")),
                      reads=[("x", 0, 0), ("x", 0, 31)], dma=True)
                    p("sync", lambda s: s.dma_start(out=dbg_kmax[:, :], in_=kmax2[:, :]),
                      reads=[("kmax2",)], dma=True)
                    p("sync", lambda s: s.dma_start(out=dbg_kfm[:, :], in_=k_fm[:, :]),
                      reads=[("kv", 0, 0), ("kv", 0, 1), ("kv", 0, 2), ("kv", 0, 3)], dma=True)
                    p("sync", lambda s: s.dma_start(out=dbg_dqf[:, :], in_=dq_f[:, :]),
                      reads=[("dqf",)], dma=True)
                for kt in range(16):
                    p("pe", lambda t, kt=kt: t.transpose(ps_t16[:, :],
                                                         v_fm[:, kt * 128:(kt + 1) * 128],
                                                         ident16[:, :]),
                      reads=[("kv", 1, kt // 4), ("id16",)], writes=[("pst16",)])
                    p("act", lambda e, kt=kt: e.copy(out=v_tok[:, kt * 128:(kt + 1) * 128],
                                                     in_=ps_t16[:, :]),
                      reads=[("pst16",)], writes=[("vtok", kt)])
                prog.flush(block)
            prog.barrier()
            prog.flush(block)

            with ExitStack() as sA2:
                qblk = sb(sA2, "qblk", [128, 2, 16, 512], BF16)
                q_sb = sb(sA2, "q_sb", [128, 4, 512], BF16)
                qsq = sb(sA2, "qsq", [128, 512], BF16)
                colvec = sb(sA2, "colvec", [128, 4, 512], F32)
                acc_e = sb(sA2, "acc_e", [128, 4, 512], F32)
                sc_b = sb(sA2, "sc_b", [128, 4, 512], F32)
                e_b = sb(sA2, "e_b", [128, 8, 512], BF16)
                linv = sb(sA2, "linv", [128, 2, 512], F32)
                attn_o = sb(sA2, "attn_o", [128, 4, 512], BF16)

                for hg in range(8):
                    for half in range(2):
                        for kk in range(16):
                            kc = half * 16 + kk
                            slot = kc % 4
                            p("sync", lambda s, kc=kc, slot=slot, hg=hg:
                                s.dma_start(out=stage_w[:, slot, :],
                                            in_=wqkv[kc * 128:(kc + 1) * 128,
                                                     hg * 512:(hg + 1) * 512]),
                              writes=[("stw", slot)], dma=True)
                            conv("act" if kc % 4 == 3 else "dve",
                                 lambda half=half, kk=kk: qblk[:, half, kk, :],
                                 lambda slot=slot: stage_w[:, slot, :],
                                 reads=[("stw", slot)], writes=[("qblk", half, kk)])
                    for hp in range(4):
                        h = hg * 4 + hp
                        if h < NO_COLVEC_H:
                            p("dve", lambda v, hp=hp, h=h:
                                v.tensor_scalar_mul(colvec[:, hp, :], dq_f[:, :], -SLOPES[h]),
                              reads=[("dqf",)], writes=[("colvec", hp)])
                    for qt in range(2):
                        for hp in range(4):
                            for kc in range(NKC):
                                half, kk = kc // 16, kc % 16
                                rd = ([("qblk", hf, k2) for hf in range(2) for k2 in range(16)]
                                      if kc == 0 else [("qblk", half, kk)])
                                p("pe", lambda t, hp=hp, kc=kc, half=half, kk=kk, qt=qt:
                                    t.matmul(ps[2 + hp][:, :],
                                             qblk[:, half, kk, hp * 128:(hp + 1) * 128],
                                             x_own[:, kc, qt * 512:(qt + 1) * 512],
                                             start=(kc == 0), stop=(kc == NKC - 1)),
                                  reads=rd + [("x", 0, kc)],
                                  writes=[("ps", 2 + hp)])
                            p("act", lambda e, hp=hp:
                                e.activation(out=q_sb[:, hp, :], in_=ps[2 + hp][:, :],
                                             func=AF.Copy, bias=0.0, scale=SCALE),
                              reads=[("ps", 2 + hp)], writes=[("qsb", hp)])
                            p("act", lambda e, hp=hp: e.square(out=qsq[:, :], in_=q_sb[:, hp, :]),
                              reads=[("qsb", hp)], writes=[("qsq",)])
                            p("pe", lambda t, hp=hp: t.matmul(ps[hp % 2][:, :], ones16[:, :],
                                                              qsq[:, :], start=True, stop=True),
                              reads=[("qsq",), ("ones16",)], writes=[("ps", hp % 2)])
                            p("dve", lambda v, hp=hp:
                                v.reduce_max(out=qn2[:, hp:hp + 1], in_=ps[hp % 2][:, :], axis=AX.X),
                              reads=[("ps", hp % 2)], writes=[("qn2", hp)])
                            p("dve", lambda v, hp=hp:
                                v.tensor_tensor(out=cvec[:, hp:hp + 1], in0=qn2[:, hp:hp + 1],
                                                in1=kmax2[:, :], op=ALU.mult),
                              reads=[("qn2", hp), ("kmax2",)], writes=[("cv", hp)])
                            p("act", lambda e, hp=hp:
                                e.activation(out=cvec[:, hp:hp + 1], in_=cvec[:, hp:hp + 1],
                                             func=AF.Sqrt, bias=0.0, scale=1.0),
                              reads=[("cv", hp)], writes=[("cv", hp)])
                            p("dve", lambda v, hp=hp:
                                v.tensor_scalar_add(cvec[:, hp:hp + 1], cvec[:, hp:hp + 1], MARGIN),
                              reads=[("cv", hp)], writes=[("cv", hp)])
                            h = hg * 4 + hp
                            col0 = (h * 2 + qt) * 16
                            p("dve", lambda v, hp=hp, col0=col0:
                                v.tensor_scalar(out=bias_all[:, hp, :],
                                                in0=alibi_t[:, col0:col0 + 16],
                                                scalar1=cvec[:, hp:hp + 1], scalar2=0.0,
                                                op0=ALU.subtract, op1=ALU.subtract),
                              reads=[("alibi",), ("cv", hp)], writes=[("bias", hp)])
                        if debug and hg == 0 and qt == 0:
                            p("sync", lambda s: s.dma_start(out=dbg_qsb[:, :],
                                                            in_=q_sb[:, :, :].rearrange("p a b -> p (a b)")),
                              reads=[("qsb", 0), ("qsb", 1), ("qsb", 2), ("qsb", 3)], dma=True)
                            p("sync", lambda s: s.dma_start(out=dbg_qn2[:, :], in_=qn2[:, :]),
                              reads=[("qn2", 0), ("qn2", 1), ("qn2", 2), ("qn2", 3)], dma=True)
                            p("sync", lambda s: s.dma_start(out=dbg_cvec[:, :], in_=cvec[:, :]),
                              reads=[("cv", 0), ("cv", 1), ("cv", 2), ("cv", 3)], dma=True)
                            p("sync", lambda s: s.dma_start(out=dbg_bias[:, :],
                                                            in_=bias_all[:, :, :].rearrange("p a b -> p (a b)")),
                              reads=[("bias", 0), ("bias", 1), ("bias", 2), ("bias", 3)], dma=True)
                        for kt in range(16):
                            partial_i = kt - 4 * qt
                            for hp in range(4):
                                h = hg * 4 + hp
                                spx = [0, 1, 6][(kt * 4 + hp) % 3]
                                scs = (kt * 4 + hp) % 4
                                ebs = (kt % 2) * 4 + hp
                                p("pe", lambda t, kt=kt, hp=hp, spx=spx:
                                    t.matmul(ps[spx][:, :], k_fm[:, kt * 128:(kt + 1) * 128],
                                             q_sb[:, hp, :], start=True, stop=True),
                                  reads=[("kv", 0, kt // 4), ("qsb", hp)], writes=[("ps", spx)])
                                partial = 0 <= partial_i < 4
                                if h < NO_COLVEC_H:
                                    p("dve", lambda v, spx=spx, scs=scs, hp=hp:
                                        v.tensor_tensor(out=sc_b[:, scs, :], in0=ps[spx][:, :],
                                                        in1=colvec[:, hp, :], op=ALU.add),
                                      reads=[("ps", spx), ("colvec", hp)], writes=[("sc", scs)])
                                    if partial:
                                        p("dve", lambda v, scs=scs, pi=partial_i:
                                            v.tensor_tensor(out=sc_b[:, scs, :], in0=sc_b[:, scs, :],
                                                            in1=maskb[:, pi, :], op=ALU.add),
                                          reads=[("sc", scs), ("mask", pi)], writes=[("sc", scs)])
                                    p("act", lambda e, scs=scs, ebs=ebs, hp=hp, kt=kt:
                                        e.activation(out=e_b[:, ebs, :], in_=sc_b[:, scs, :],
                                                     func=AF.Exp, bias=bias_all[:, hp, kt:kt + 1],
                                                     scale=1.0),
                                      reads=[("sc", scs), ("bias", hp)], writes=[("eb", ebs)])
                                elif partial:
                                    p("dve", lambda v, spx=spx, scs=scs, pi=partial_i:
                                        v.tensor_tensor(out=sc_b[:, scs, :], in0=ps[spx][:, :],
                                                        in1=maskb[:, pi, :], op=ALU.add),
                                      reads=[("ps", spx), ("mask", pi)], writes=[("sc", scs)])
                                    p("act", lambda e, scs=scs, ebs=ebs, hp=hp, kt=kt:
                                        e.activation(out=e_b[:, ebs, :], in_=sc_b[:, scs, :],
                                                     func=AF.Exp, bias=bias_all[:, hp, kt:kt + 1],
                                                     scale=1.0),
                                      reads=[("sc", scs), ("bias", hp)], writes=[("eb", ebs)])
                                else:
                                    p("act", lambda e, spx=spx, ebs=ebs, hp=hp, kt=kt:
                                        e.activation(out=e_b[:, ebs, :], in_=ps[spx][:, :],
                                                     func=AF.Exp, bias=bias_all[:, hp, kt:kt + 1],
                                                     scale=1.0),
                                      reads=[("ps", spx), ("bias", hp)], writes=[("eb", ebs)])
                                if kt == 0:
                                    p("dve", lambda v, ebs=ebs, hp=hp:
                                        v.tensor_copy(out=acc_e[:, hp, :], in_=e_b[:, ebs, :]),
                                      reads=[("eb", ebs)], writes=[("acc", hp)])
                                else:
                                    p("dve", lambda v, ebs=ebs, hp=hp:
                                        v.tensor_tensor(out=acc_e[:, hp, :], in0=acc_e[:, hp, :],
                                                        in1=e_b[:, ebs, :], op=ALU.add),
                                      reads=[("eb", ebs), ("acc", hp)], writes=[("acc", hp)])
                                p("pe", lambda t, kt=kt, ebs=ebs, hp=hp:
                                    t.matmul(ps[2 + hp][:, :], v_tok[:, kt * 128:(kt + 1) * 128],
                                             e_b[:, ebs, :], start=(kt == 0), stop=(kt == 15)),
                                  reads=[("vtok", kt), ("eb", ebs)], writes=[("ps", 2 + hp)])
                        if debug and hg == 0 and qt == 0:
                            p("sync", lambda s: s.dma_start(out=dbg_acc[:, :],
                                                            in_=acc_e[:, :, :].rearrange("p a b -> p (a b)")),
                              reads=[("acc", 0), ("acc", 1), ("acc", 2), ("acc", 3)], dma=True)
                        for hp in range(4):
                            lp = hp % 2
                            p("pe", lambda t, hp=hp, lp=lp:
                                t.matmul(ps[lp][:, :], ones32f[:, :], acc_e[:, hp, :],
                                         start=True, stop=True),
                              reads=[("acc", hp), ("ones32f",)], writes=[("ps", lp)])
                            p("dve", lambda v, lp=lp:
                                v.reciprocal(out=linv[:, lp, :], in_=ps[lp][:, :]),
                              reads=[("ps", lp)], writes=[("linv", lp)])
                            p("dve", lambda v, hp=hp, lp=lp:
                                v.tensor_tensor(out=attn_o[:, hp, :], in0=ps[2 + hp][:, :],
                                                in1=linv[:, lp, :], op=ALU.mult),
                              reads=[("ps", 2 + hp), ("linv", lp)], writes=[("ao", hp)])
                            h = hg * 4 + hp
                            p("act", lambda s, h=h, hp=hp, qt=qt:
                                s.dma_start(out=attn_spill[h * 128:(h + 1) * 128,
                                                           qt * 512:(qt + 1) * 512],
                                            in_=attn_o[:, hp, :]),
                              reads=[("ao", hp)], dma=True)
                prog.flush(block)
        prog.barrier()
        prog.flush(block)

        with ExitStack() as sB:
            attn_sb = sb(sB, "attn_sb", [128, NKC, 1024], BF16)
            woblk = sb(sB, "woblk", [128, 2, NKC, 256], BF16)
            rb = sb(sB, "rb", [128, 2, 512], BF16)
            sqb = sb(sB, "sqb", [128, 2, 512], BF16)
            rc = sb(sB, "rc", [128, 4, 512], F32)
            xs = sb(sB, "xs", [128, 2, 512], F32)
            for h in range(NKC):
                p("sync", lambda s, h=h: s.dma_start(out=attn_sb[:, h, :],
                                                     in_=attn_spill[h * 128:(h + 1) * 128, :]),
                  writes=[("at", h)], dma=True)
            for fp in range(16):
                par = fp % 2
                for kc in range(NKC):
                    slot = kc % 4
                    p("sync", lambda s, fp=fp, kc=kc, slot=slot:
                        s.dma_start(out=stage_w[:, slot, 0:256],
                                    in_=wo[kc * 128:(kc + 1) * 128, fp * 256:(fp + 1) * 256]),
                      writes=[("stw", slot)], dma=True)
                    conv("act" if kc % 4 == 3 else "dve",
                         lambda par=par, kc=kc: woblk[:, par, kc, :],
                         lambda slot=slot: stage_w[:, slot, 0:256],
                         reads=[("stw", slot)], writes=[("wob", par, kc)])
                for tq in range(2):
                    for j in range(2):
                        foc = fp * 2 + j
                        for kc in range(NKC):
                            rd = ([("wob", par, k2) for k2 in range(NKC)] if kc == 0
                                  else [("wob", par, kc)])
                            p("pe", lambda t, par=par, kc=kc, j=j, tq=tq:
                                t.matmul(ps[j][:, :], woblk[:, par, kc, j * 128:(j + 1) * 128],
                                         attn_sb[:, kc, tq * 512:(tq + 1) * 512],
                                         start=(kc == 0), stop=(kc == NKC - 1)),
                              reads=rd + [("at", kc)], writes=[("ps", j)])
                        p("sync", lambda s, foc=foc, tq=tq, j=j:
                            s.dma_start(out=xs[:, j, :],
                                        in_=xt[foc * 128:(foc + 1) * 128,
                                               tq * 512:(tq + 1) * 512]),
                          writes=[("xs", j)], dma=True)
                        rslot = (foc * 2 + tq) % 4
                        p("dve", lambda v, j=j, rslot=rslot:
                            v.tensor_tensor(out=rc[:, rslot, :], in0=ps[j][:, :],
                                            in1=xs[:, j, :], op=ALU.add),
                          reads=[("ps", j), ("xs", j)], writes=[("rc", rslot)])
                        p("act", lambda s, foc=foc, tq=tq, rslot=rslot:
                            s.dma_start(out=rbuf[foc * 128:(foc + 1) * 128,
                                                 tq * 512:(tq + 1) * 512],
                                        in_=rc[:, rslot, :]),
                          reads=[("rc", rslot)], dma=True)
                        p("act", lambda e, rslot=rslot, j=j:
                            e.copy(out=rb[:, j, :], in_=rc[:, rslot, :]),
                          reads=[("rc", rslot)], writes=[("rb", j)])
                        p("pe", lambda t, j=j, tq=tq, foc=foc:
                            t.matmul(ps[4 + tq][:, :], ones16[:, :], rb[:, j, :],
                                     start=(foc == 0), stop=(foc == NKC - 1)),
                          reads=[("rb", j), ("ones16",)], writes=[("ps", 4 + tq)])
                        p("act", lambda e, j=j: e.square(out=sqb[:, j, :], in_=rb[:, j, :]),
                          reads=[("rb", j)], writes=[("sqb", j)])
                        p("pe", lambda t, j=j, tq=tq, foc=foc:
                            t.matmul(ps[2 + tq][:, :], ones16[:, :], sqb[:, j, :],
                                     start=(foc == 0), stop=(foc == NKC - 1)),
                          reads=[("sqb", j), ("ones16",)], writes=[("ps", 2 + tq)])
            prog.flush(block)
        prog.barrier()
        prog.flush(block)

        with ExitStack() as sC:
            y1 = sb(sC, "y1", [128, NKC, 1024], BF16)

            with ExitStack() as sC1:
                varb = sb(sC1, "varb", [128, 1024], F32)
                for tq in range(2):
                    p("act", lambda e, tq=tq:
                        e.activation(out=mean_bc[:, tq * 512:(tq + 1) * 512],
                                     in_=ps[4 + tq][:, :], func=AF.Copy, bias=0.0, scale=1.0 / D),
                      reads=[("ps", 4 + tq)], writes=[("meanbc",)])
                    p("act", lambda e, tq=tq:
                        e.activation(out=varb[:, tq * 512:(tq + 1) * 512],
                                     in_=ps[2 + tq][:, :], func=AF.Copy, bias=0.0, scale=1.0 / D),
                      reads=[("ps", 2 + tq)], writes=[("varb",)])
                p("dve", lambda v: v.tensor_tensor(out=rstd_bc[:, :], in0=mean_bc[:, :],
                                                   in1=mean_bc[:, :], op=ALU.mult),
                  reads=[("meanbc",)], writes=[("rstdbc",)])
                p("dve", lambda v: v.tensor_tensor(out=varb[:, :], in0=varb[:, :],
                                                   in1=rstd_bc[:, :], op=ALU.subtract),
                  reads=[("varb",), ("rstdbc",)], writes=[("varb",)])
                p("act", lambda e: e.activation(out=varb[:, :], in_=varb[:, :], func=AF.Sqrt,
                                                bias=eps_t[:, :], scale=1.0),
                  reads=[("varb",), ("eps",)], writes=[("varb",)])
                p("dve", lambda v: v.reciprocal(out=rstd_bc[:, :], in_=varb[:, :]),
                  reads=[("varb",)], writes=[("rstdbc",)])
                for kc in range(NKC):
                    slot = kc % 2
                    p("sync", lambda s, kc=kc, slot=slot:
                        s.dma_start(out=stage_b[:, slot, :],
                                    in_=rbuf[kc * 128:(kc + 1) * 128, :]),
                      writes=[("stb", slot)], dma=True)
                    p("dve", lambda v, slot=slot:
                        v.tensor_tensor(out=stage_b[:, slot, :], in0=stage_b[:, slot, :],
                                        in1=mean_bc[:, :], op=ALU.subtract),
                      reads=[("stb", slot), ("meanbc",)], writes=[("stb", slot)])
                    p("dve", lambda v, kc=kc, slot=slot:
                        v.tensor_tensor(out=y1[:, kc, :], in0=stage_b[:, slot, :],
                                        in1=rstd_bc[:, :], op=ALU.mult),
                      reads=[("stb", slot), ("rstdbc",)], writes=[("y1", kc)])
                prog.flush(block)
            prog.barrier()
            prog.flush(block)

            with ExitStack() as sC2:
                wgblk = sb(sC2, "wgblk", [128, 2, NKC, 256], BF16)
                wublk = sb(sC2, "wublk", [128, 2, NKC, 256], BF16)
                sgb = sb(sC2, "sgb", [128, 2, 512], F32)
                sub_ = sb(sC2, "sub", [128, 2, 512], F32)
                hb = sb(sC2, "hb", [128, 4, 512], BF16)
                for ip in range(43):
                    par = ip % 2
                    for kc in range(NKC):
                        slot = kc % 4
                        p("sync", lambda s, ip=ip, kc=kc, slot=slot:
                            s.dma_start(out=stage_w[:, slot, 0:256],
                                        in_=wgt[kc * 128:(kc + 1) * 128,
                                                ip * 256:(ip + 1) * 256]),
                          writes=[("stw", slot)], dma=True)
                        conv("act" if kc % 4 == 3 else "dve",
                             lambda par=par, kc=kc: wgblk[:, par, kc, :],
                             lambda slot=slot: stage_w[:, slot, 0:256],
                             reads=[("stw", slot)], writes=[("wgb", par, kc)])
                        p("sync", lambda s, ip=ip, kc=kc, slot=slot:
                            s.dma_start(out=stage_w[:, slot, 256:512],
                                        in_=wut[kc * 128:(kc + 1) * 128,
                                                ip * 256:(ip + 1) * 256]),
                          writes=[("stw2", slot)], dma=True)
                        conv("dve" if kc % 4 != 1 else "act",
                             lambda par=par, kc=kc: wublk[:, par, kc, :],
                             lambda slot=slot: stage_w[:, slot, 256:512],
                             reads=[("stw2", slot)], writes=[("wub", par, kc)])
                    for jj in range(2):
                        ioc = ip * 2 + jj
                        for tq in range(2):
                            pg = 2 * ((jj + tq) % 2)
                            pu = pg + 1
                            for kc in range(NKC):
                                rdg = ([("wgb", par, k2) for k2 in range(NKC)] if kc == 0
                                       else [("wgb", par, kc)])
                                rdu = ([("wub", par, k2) for k2 in range(NKC)] if kc == 0
                                       else [("wub", par, kc)])
                                p("pe", lambda t, par=par, kc=kc, jj=jj, tq=tq, pg=pg:
                                    t.matmul(ps[pg][:, :],
                                             wgblk[:, par, kc, jj * 128:(jj + 1) * 128],
                                             y1[:, kc, tq * 512:(tq + 1) * 512],
                                             start=(kc == 0), stop=(kc == NKC - 1)),
                                  reads=rdg + [("y1", kc)], writes=[("ps", pg)])
                                p("pe", lambda t, par=par, kc=kc, jj=jj, tq=tq, pu=pu:
                                    t.matmul(ps[pu][:, :],
                                             wublk[:, par, kc, jj * 128:(jj + 1) * 128],
                                             y1[:, kc, tq * 512:(tq + 1) * 512],
                                             start=(kc == 0), stop=(kc == NKC - 1)),
                                  reads=rdu + [("y1", kc)], writes=[("ps", pu)])
                            p("act", lambda e, pg=pg, tq=tq, ioc=ioc:
                                e.activation(out=sgb[:, tq, :], in_=ps[pg][:, :], func=AF.Silu,
                                             bias=0.0, scale=ag_t[:, ioc:ioc + 1]),
                              reads=[("ps", pg), ("ag",)], writes=[("sg", tq)])
                            p("act", lambda e, pu=pu, tq=tq, ioc=ioc:
                                e.activation(out=sub_[:, tq, :], in_=ps[pu][:, :], func=AF.Copy,
                                             bias=0.0, scale=au_t[:, ioc:ioc + 1]),
                              reads=[("ps", pu), ("au",)], writes=[("su", tq)])
                            hs = (ioc * 2 + tq) % 4
                            p("dve", lambda v, tq=tq, hs=hs:
                                v.tensor_tensor(out=hb[:, hs, :], in0=sgb[:, tq, :],
                                                in1=sub_[:, tq, :], op=ALU.mult),
                              reads=[("sg", tq), ("su", tq)], writes=[("hb", hs)])
                            p("act", lambda s, ioc=ioc, tq=tq, hs=hs:
                                s.dma_start(out=hbuf[ioc * 128:(ioc + 1) * 128,
                                                     tq * 512:(tq + 1) * 512],
                                            in_=hb[:, hs, :]),
                              reads=[("hb", hs)], dma=True)
                prog.flush(block)
        prog.barrier()
        prog.flush(block)

        with ExitStack() as sD:
            h_sb = sb(sD, "h_sb", [128, NI, 512], BF16)
            wdb = sb(sD, "wdb", [128, 4, 256], BF16)
            fb = sb(sD, "fb", [128, 2, 512], F32)
            tb = sb(sD, "tb", [128, 2, 512], F32)
            r2c = sb(sD, "r2c", [128, 4, 512], F32)
            r2b = sb(sD, "r2b", [128, 2, 512], BF16)
            sq2 = sb(sD, "sq2", [128, 2, 512], BF16)
            for tq in range(2):
                for ic in range(NI):
                    p("sync", lambda s, ic=ic, tq=tq:
                        s.dma_start(out=h_sb[:, ic, :],
                                    in_=hbuf[ic * 128:(ic + 1) * 128,
                                             tq * 512:(tq + 1) * 512]),
                      writes=[("hs", ic)], dma=True)
                for fp in range(16):
                    for ic in range(NI):
                        slot = ic % 4
                        p("sync", lambda s, fp=fp, ic=ic, slot=slot:
                            s.dma_start(out=stage_w[:, slot, 0:256],
                                        in_=wdt[ic * 128:(ic + 1) * 128,
                                                fp * 256:(fp + 1) * 256]),
                          writes=[("stw", slot)], dma=True)
                        conv("act" if ic % 4 == 3 else "dve",
                             lambda slot=slot: wdb[:, slot, :],
                             lambda slot=slot: stage_w[:, slot, 0:256],
                             reads=[("stw", slot)], writes=[("wdb", slot)])
                        for j in range(2):
                            pslot = (fp % 2) * 2 + j
                            p("pe", lambda t, slot=slot, j=j, ic=ic, pslot=pslot:
                                t.matmul(ps[pslot][:, :], wdb[:, slot, j * 128:(j + 1) * 128],
                                         h_sb[:, ic, :], start=(ic == 0), stop=(ic == NI - 1)),
                              reads=[("wdb", slot), ("hs", ic)], writes=[("ps", pslot)])
                    for j in range(2):
                        foc = fp * 2 + j
                        pslot = (fp % 2) * 2 + j
                        p("act", lambda e, pslot=pslot, j=j, foc=foc:
                            e.activation(out=fb[:, j, :], in_=ps[pslot][:, :], func=AF.Copy,
                                         bias=0.0, scale=ad_t[:, foc:foc + 1]),
                          reads=[("ps", pslot), ("ad",)], writes=[("fb", j)])
                        p("sync", lambda s, foc=foc, tq=tq, j=j:
                            s.dma_start(out=tb[:, j, :],
                                        in_=rbuf[foc * 128:(foc + 1) * 128,
                                                 tq * 512:(tq + 1) * 512]),
                          writes=[("tb", j)], dma=True)
                        p("dve", lambda v, j=j, tq=tq:
                            v.tensor_tensor(out=tb[:, j, :], in0=tb[:, j, :],
                                            in1=mean_bc[:, tq * 512:(tq + 1) * 512],
                                            op=ALU.subtract),
                          reads=[("tb", j), ("meanbc",)], writes=[("tb", j)])
                        p("dve", lambda v, j=j, tq=tq:
                            v.tensor_tensor(out=tb[:, j, :], in0=tb[:, j, :],
                                            in1=rstd_bc[:, tq * 512:(tq + 1) * 512],
                                            op=ALU.mult),
                          reads=[("tb", j), ("rstdbc",)], writes=[("tb", j)])
                        r2s = (foc * 2 + tq) % 4
                        p("dve", lambda v, j=j, r2s=r2s:
                            v.tensor_tensor(out=r2c[:, r2s, :], in0=tb[:, j, :],
                                            in1=fb[:, j, :], op=ALU.add),
                          reads=[("tb", j), ("fb", j)], writes=[("r2c", r2s)])
                        p("act", lambda s, foc=foc, tq=tq, r2s=r2s:
                            s.dma_start(out=r2buf[foc * 128:(foc + 1) * 128,
                                                  tq * 512:(tq + 1) * 512],
                                        in_=r2c[:, r2s, :]),
                          reads=[("r2c", r2s)], dma=True)
                        p("act", lambda e, r2s=r2s, j=j: e.copy(out=r2b[:, j, :], in_=r2c[:, r2s, :]),
                          reads=[("r2c", r2s)], writes=[("r2b", j)])
                        p("pe", lambda t, j=j, foc=foc:
                            t.matmul(ps[4][:, :], ones16[:, :], r2b[:, j, :],
                                     start=(foc == 0), stop=(foc == NKC - 1)),
                          reads=[("r2b", j), ("ones16",)], writes=[("ps", 4)])
                        p("act", lambda e, j=j: e.square(out=sq2[:, j, :], in_=r2b[:, j, :]),
                          reads=[("r2b", j)], writes=[("sq2", j)])
                        p("pe", lambda t, j=j, foc=foc:
                            t.matmul(ps[5][:, :], ones16[:, :], sq2[:, j, :],
                                     start=(foc == 0), stop=(foc == NKC - 1)),
                          reads=[("sq2", j), ("ones16",)], writes=[("ps", 5)])
                p("act", lambda e, tq=tq: e.activation(out=m2_raw[:, tq * 512:(tq + 1) * 512],
                                                       in_=ps[4][:, :], func=AF.Copy,
                                                       bias=0.0, scale=1.0 / D),
                  reads=[("ps", 4)], writes=[("m2raw",)])
                p("act", lambda e, tq=tq: e.activation(out=msq2_raw[:, tq * 512:(tq + 1) * 512],
                                                       in_=ps[5][:, :], func=AF.Copy,
                                                       bias=0.0, scale=1.0 / D),
                  reads=[("ps", 5)], writes=[("msq2raw",)])
            prog.flush(block)
        prog.barrier()
        prog.flush(block)

        with ExitStack() as sE:
            y2t = sb(sE, "y2t", [128, 4, D], F32)
            y2c = sb(sE, "y2c", [128, 2, 512], F32)
            p("dve", lambda v: v.tensor_copy(out=mean_bc[:, :], in_=m2_raw[:, :]),
              reads=[("m2raw",)], writes=[("meanbc",)])
            p("dve", lambda v: v.tensor_tensor(out=rstd_bc[:, :], in0=mean_bc[:, :],
                                               in1=mean_bc[:, :], op=ALU.mult),
              reads=[("meanbc",)], writes=[("rstdbc",)])
            p("dve", lambda v: v.tensor_tensor(out=msq2_raw[:, :], in0=msq2_raw[:, :],
                                               in1=rstd_bc[:, :], op=ALU.subtract),
              reads=[("msq2raw",), ("rstdbc",)], writes=[("msq2raw",)])
            p("act", lambda e: e.activation(out=msq2_raw[:, :], in_=msq2_raw[:, :], func=AF.Sqrt,
                                            bias=eps_t[:, :], scale=1.0),
              reads=[("msq2raw",), ("eps",)], writes=[("msq2raw",)])
            p("dve", lambda v: v.reciprocal(out=rstd_bc[:, :], in_=msq2_raw[:, :]),
              reads=[("msq2raw",)], writes=[("rstdbc",)])
            for th in range(2):
                for foc in range(NKC):
                    cs = foc % 2
                    p("sync", lambda s, foc=foc, th=th, cs=cs:
                        s.dma_start(out=y2c[:, cs, :],
                                    in_=r2buf[foc * 128:(foc + 1) * 128,
                                              th * 512:(th + 1) * 512]),
                      writes=[("y2c", cs)], dma=True)
                    p("dve", lambda v, cs=cs, th=th:
                        v.tensor_tensor(out=y2c[:, cs, :], in0=y2c[:, cs, :],
                                        in1=mean_bc[:, th * 512:(th + 1) * 512],
                                        op=ALU.subtract),
                      reads=[("y2c", cs), ("meanbc",)], writes=[("y2c", cs)])
                    p("dve", lambda v, cs=cs, th=th:
                        v.tensor_tensor(out=y2c[:, cs, :], in0=y2c[:, cs, :],
                                        in1=rstd_bc[:, th * 512:(th + 1) * 512], op=ALU.mult),
                      reads=[("y2c", cs), ("rstdbc",)], writes=[("y2c", cs)])
                    for i in range(4):
                        p("pe", lambda t, cs=cs, i=i:
                            t.transpose(ps[6][:, 0:128], y2c[:, cs, i * 128:(i + 1) * 128],
                                        ident32[:, :]),
                          reads=[("y2c", cs), ("id32",)], writes=[("ps", 6)])
                        p("act", lambda e, i=i, foc=foc:
                            e.copy(out=y2t[:, i, foc * 128:(foc + 1) * 128], in_=ps[6][:, 0:128]),
                          reads=[("ps", 6)], writes=[("y2t", i)])
                for i in range(4):
                    tt = th * 4 + i
                    p("sync", lambda s, tt=tt, i=i:
                        s.dma_start(out=out[tt * 128:(tt + 1) * 128, :], in_=y2t[:, i, :]),
                      reads=[("y2t", i)], dma=True)
            prog.flush(block)
        prog.barrier()
        prog.flush(block)

    return nc, prog


def _build(debug=False):
    _, prog1 = build_core_program(mark=None, debug=debug)
    mark = {s: sorted(v) for s, v in prog1.used.items()}
    nc, _ = build_core_program(mark=mark, debug=debug)
    return nc


def _host_stage(L0_attn_hidden_states, L0_attn_Wqkv, L0_attn_Wo, L0_ffn_Wg, L0_ffn_Wu,
                L0_ffn_Wd, L0_ffn_ag, L0_ffn_au, L0_ffn_ad):
    x = np.asarray(L0_attn_hidden_states, dtype=np.float32)
    wqkv = np.ascontiguousarray(np.asarray(L0_attn_Wqkv, dtype=np.float32))
    wo_ = np.ascontiguousarray(np.asarray(L0_attn_Wo, dtype=np.float32))
    wgt = np.ascontiguousarray(np.asarray(L0_ffn_Wg, dtype=np.float32).T)
    wut = np.ascontiguousarray(np.asarray(L0_ffn_Wu, dtype=np.float32).T)
    wdt = np.ascontiguousarray(np.asarray(L0_ffn_Wd, dtype=np.float32).T)
    ag = np.ascontiguousarray(np.asarray(L0_ffn_ag, dtype=np.float32).reshape(NI, 128).T)
    au = np.ascontiguousarray(np.asarray(L0_ffn_au, dtype=np.float32).reshape(NI, 128).T)
    ad = np.ascontiguousarray(np.asarray(L0_ffn_ad, dtype=np.float32).reshape(NKC, 128).T)

    in_maps = []
    dk = np.arange(128)
    for core in range(8):
        b, half = core // 2, core % 2
        xb = x[b].T
        own = xb[:, half * TOK:(half + 1) * TOK]
        oth = xb[:, (1 - half) * TOK:(2 - half) * TOK]
        xtc = np.ascontiguousarray(np.concatenate([own, oth], axis=1))

        alibi = np.zeros((128, NH * 2 * 16), dtype=np.float32)
        own_base = half * TOK
        oth_base = (1 - half) * TOK
        for h in range(NH):
            for qt in range(2):
                q0 = own_base + qt * 512
                for kt in range(16):
                    col = (h * 2 + qt) * 16 + kt
                    if kt < 8:
                        kpos = own_base + kt * 128 + dk
                    else:
                        kpos = oth_base + (kt - 8) * 128 + dk
                    if kpos.min() > q0 + 511:
                        alibi[:, col] = NEG
                    else:
                        alibi[:, col] = SLOPES[h] * (kpos - q0)
        in_maps.append({
            "xt": xtc, "wqkv": wqkv, "wo": wo_, "wgt": wgt, "wut": wut, "wdt": wdt,
            "ag": ag, "au": au, "ad": ad, "alibi": alibi,
        })
    return in_maps


def kernel(**inputs):
    in_maps = _host_stage(**inputs)
    nc = _build()
    res = run_bass_kernel_spmd(nc, in_maps, core_ids=list(range(8)))
    outp = np.empty((B, S, D), dtype=np.float32)
    for core in range(8):
        b, half = core // 2, core % 2
        outp[b, half * TOK:(half + 1) * TOK, :] = res.results[core]["out"]
    return outp


# version 07
# speedup vs baseline: 1.2430x; 1.2430x over previous
import math
import numpy as np
from contextlib import ExitStack

import concourse.bass as bass
import concourse.mybir as mybir
from concourse.bass_utils import run_bass_kernel_spmd

F32 = mybir.dt.float32
BF16 = mybir.dt.bfloat16
I32 = mybir.dt.int32
AF = mybir.ActivationFunctionType
ALU = mybir.AluOpType
AX = mybir.AxisListType

B, S, D = 4, 2048, 4096
NH, HD = 32, 128
INTER = 11008
NI = INTER // 128
NKC = D // 128
TOK = 1024
SCALE = HD ** -0.5
EPS = 1e-5
MARGIN = 4.0
SLOPES = [2.0 ** (-0.25 * (h + 1)) for h in range(NH)]
NEG = -1e30
NO_COLVEC_H = 11

ENGS = ("pe", "act", "dve", "pool", "sync")
ENG_ATTR = {"pe": "tensor", "act": "scalar", "dve": "vector", "pool": "gpsimd", "sync": "sync"}
DMA_ENGS = ("sync", "act")
NLANES = 12


class Prog:

    def __init__(self, nc, ctx, mark=None):
        import bisect
        self._bisect = bisect
        self.nc = nc
        self.collect = mark is None
        self.mark = mark
        self.sems = {}
        for e in ENGS:
            self.sems[e] = ctx.enter_context(nc.semaphore(f"s_{e}"))
        for e in DMA_ENGS:
            for ln in range(NLANES):
                self.sems[f"{e}_dma{ln}"] = ctx.enter_context(nc.semaphore(f"s_{e}d{ln}"))
        self.dma_rr = {e: 0 for e in DMA_ENGS}
        self.cnt = {s: 0 for s in self.sems}
        self.last_w = {}
        self.readers = {}
        self.waited = {e: {} for e in ENGS}
        self.used = {s: set() for s in self.sems}
        self.phase_ops = {e: [] for e in ENGS}

    def _remap_val(self, sem, val):
        if "_dma" in sem:
            return val
        lst = self.mark[sem]
        i = self._bisect.bisect_left(lst, val)
        assert i < len(lst), f"remap miss {sem} {val}"
        return i + 1

    def _is_marked(self, sem, val):
        if "_dma" in sem:
            return True
        lst = self.mark[sem]
        i = self._bisect.bisect_left(lst, val)
        return i < len(lst) and lst[i] == val

    def op(self, eng, fn, reads=(), writes=(), dma=False, extra=()):
        if dma:
            ln = self.dma_rr[eng] % NLANES
            self.dma_rr[eng] += 1
            sem = f"{eng}_dma{ln}"
            extra = list(extra) + [(sem, self.cnt[sem])]
        else:
            sem = eng
        toks = {}

        def add(tok):
            if tok is None:
                return
            s, v = tok
            if toks.get(s, 0) < v:
                toks[s] = v

        for k in reads:
            add(self.last_w.get(k))
        for k in writes:
            add(self.last_w.get(k))
            for t in self.readers.get(k, ()):
                add(t)
        for t in extra:
            add(t)

        waits = []
        for s, v in sorted(toks.items()):
            if v <= 0:
                continue
            if s == "pe" and eng == "pe" and not dma:
                continue
            if self.waited[eng].get(s, 0) >= v:
                continue
            self.waited[eng][s] = v
            waits.append((s, v))
            if self.collect:
                self.used[s].add(v)

        self.cnt[sem] += 16 if dma else 1
        tok = (sem, self.cnt[sem])
        for k in reads:
            self.readers.setdefault(k, []).append(tok)
        for k in writes:
            self.last_w[k] = tok
            self.readers[k] = []

        if not self.collect:
            do_inc = self._is_marked(sem, self.cnt[sem])
            rw = [(self.sems[s], self._remap_val(s, v)) for s, v in waits]
            my_sem = self.sems[sem]
            inc_amt = 16 if dma else 1

            def emit(engine, rw=rw, do_inc=do_inc, fn=fn, my_sem=my_sem, inc_amt=inc_amt):
                for sh, v in rw:
                    engine.wait_ge(sh, v)
                instr = fn(engine)
                if do_inc:
                    instr.then_inc(my_sem, inc_amt)

            self.phase_ops[eng].append(emit)
        return tok

    def barrier(self):
        snap = [(s, self.cnt[s]) for s in self.sems if self.cnt[s] > 0]
        for e in ENGS:
            ex = [(s, v) for s, v in snap if s != e]
            self.op(e, lambda eng: eng.nop(), extra=ex)

    def flush(self, block):
        if self.collect or block is None:
            for e in ENGS:
                self.phase_ops[e] = []
            return
        for e in ENGS:
            ops = self.phase_ops[e]
            if not ops:
                continue
            self.phase_ops[e] = []
            section = getattr(block, ENG_ATTR[e])

            def body(engine, ops=ops):
                for o in ops:
                    o(engine)

            section(body)


def build_core_program(mark=None, debug=False):
    nc = bass.Bass("TRN2", target_bir_lowering=False)

    xt = nc.dram_tensor("xt", [D, S], F32, kind="ExternalInput")
    wqkv = nc.dram_tensor("wqkv", [D, 4352], F32, kind="ExternalInput")
    wo = nc.dram_tensor("wo", [D, D], F32, kind="ExternalInput")
    wgt = nc.dram_tensor("wgt", [D, INTER], F32, kind="ExternalInput")
    wut = nc.dram_tensor("wut", [D, INTER], F32, kind="ExternalInput")
    wdt = nc.dram_tensor("wdt", [INTER, D], F32, kind="ExternalInput")
    ag_in = nc.dram_tensor("ag", [128, NI], F32, kind="ExternalInput")
    au_in = nc.dram_tensor("au", [128, NI], F32, kind="ExternalInput")
    ad_in = nc.dram_tensor("ad", [128, NKC], F32, kind="ExternalInput")
    alibi_in = nc.dram_tensor("alibi", [128, NH * 2 * 16], F32, kind="ExternalInput")
    out = nc.dram_tensor("out", [TOK, D], F32, kind="ExternalOutput")

    spill_kind = "ExternalOutput" if debug else "Internal"
    attn_spill = nc.dram_tensor("attn_spill", [D, TOK], BF16, kind=spill_kind)
    rbuf = nc.dram_tensor("rbuf", [D, TOK], F32, kind=spill_kind)
    hbuf = nc.dram_tensor("hbuf", [INTER, TOK], BF16, kind=spill_kind)
    r2buf = nc.dram_tensor("r2buf", [D, TOK], F32, kind=spill_kind)
    if debug:
        dbg_kmax = nc.dram_tensor("dbg_kmax", [128, 1], F32, kind="ExternalOutput")
        dbg_kfm = nc.dram_tensor("dbg_kfm", [128, 2048], BF16, kind="ExternalOutput")
        dbg_bias = nc.dram_tensor("dbg_bias", [128, 64], F32, kind="ExternalOutput")
        dbg_cvec = nc.dram_tensor("dbg_cvec", [128, 4], F32, kind="ExternalOutput")
        dbg_qn2 = nc.dram_tensor("dbg_qn2", [128, 4], F32, kind="ExternalOutput")
        dbg_qsb = nc.dram_tensor("dbg_qsb", [128, 2048], BF16, kind="ExternalOutput")
        dbg_acc = nc.dram_tensor("dbg_acc", [128, 2048], F32, kind="ExternalOutput")
        dbg_dqf = nc.dram_tensor("dbg_dqf", [128, 512], F32, kind="ExternalOutput")
        dbg_xown = nc.dram_tensor("dbg_xown", [128, NKC * 1024], BF16, kind="ExternalOutput")

    with ExitStack() as top:
        prog = Prog(nc, top, mark=mark)
        p = prog.op
        block = None if mark is None else top.enter_context(nc.Block())

        def sb(stack, name, shape, dtype):
            return stack.enter_context(nc.sbuf_tensor(name, shape, dtype))

        ps = [top.enter_context(nc.psum_tensor(f"ps{i}", [128, 512], F32)) for i in range(7)]
        ps_t16 = top.enter_context(nc.psum_tensor("ps_t16", [128, 128], BF16))

        eps_t = sb(top, "eps_t", [128, 1], F32)
        ones16 = sb(top, "ones16", [128, 128], BF16)
        ones32f = sb(top, "ones32f", [128, 128], F32)
        ident16 = sb(top, "ident16", [128, 128], BF16)
        ident32 = sb(top, "ident32", [128, 128], F32)
        dq_i = sb(top, "dq_i", [128, 512], I32)
        dq_f = sb(top, "dq_f", [128, 512], F32)
        alibi_t = sb(top, "alibi_t", [128, NH * 2 * 16], F32)
        ag_t = sb(top, "ag_t", [128, NI], F32)
        au_t = sb(top, "au_t", [128, NI], F32)
        ad_t = sb(top, "ad_t", [128, NKC], F32)
        kmax2 = sb(top, "kmax2", [128, 1], F32)
        kn_tmp = sb(top, "kn_tmp", [128, 1], F32)
        stage_w = sb(top, "stage_w", [128, 2, 2048], F32)
        stage_b = sb(top, "stage_b", [128, 2, 1024], F32)
        mean_bc = sb(top, "mean_bc", [128, 1024], F32)
        rstd_bc = sb(top, "rstd_bc", [128, 1024], F32)
        bias_all = sb(top, "bias_all", [128, 4, 16], F32)
        cvec = sb(top, "cvec", [128, 4], F32)
        maskb = sb(top, "maskb", [128, 4, 512], F32)
        qn2 = sb(top, "qn2", [128, 4], F32)

        def conv(eng, dst_ap_fn, src_ap_fn, reads, writes):
            if eng == "act":
                p("act", lambda e: e.copy(out=dst_ap_fn(), in_=src_ap_fn()),
                  reads=reads, writes=writes)
            else:
                p("dve", lambda e: e.tensor_copy(out=dst_ap_fn(), in_=src_ap_fn()),
                  reads=reads, writes=writes)

        p("pool", lambda g: g.memset(eps_t[:, :], EPS), writes=[("eps",)])
        p("pool", lambda g: g.memset(ones16[:, :], 1.0), writes=[("ones16",)])
        p("pool", lambda g: g.memset(ones32f[:, :], 1.0), writes=[("ones32f",)])
        p("pool", lambda g: g.memset(ident32[:, :], 0.0), writes=[("id32",)])
        p("pool", lambda g: g.affine_select(out=ident32[:, :], in_=ident32[:, :],
                                            compare_op=ALU.not_equal, fill=1.0, base=0,
                                            pattern=[[-1, 128]], channel_multiplier=1),
          reads=[("id32",)], writes=[("id32",)])
        p("act", lambda e: e.copy(out=ident16[:, :], in_=ident32[:, :]),
          reads=[("id32",)], writes=[("id16",)])
        p("pool", lambda g: g.iota(out=dq_i[:, :], pattern=[[1, 512]], base=0,
                                   channel_multiplier=0), writes=[("dqi",)])
        for pi in range(4):
            p("pool", lambda g, pi=pi: g.memset(maskb[:, pi, :], 0.0), writes=[("mask", pi)])
            p("pool", lambda g, pi=pi: g.affine_select(out=maskb[:, pi, :], in_=maskb[:, pi, :],
                                                       compare_op=ALU.is_ge, fill=NEG,
                                                       base=-128 * pi, pattern=[[1, 512]],
                                                       channel_multiplier=-1),
              reads=[("mask", pi)], writes=[("mask", pi)])
        p("dve", lambda v: v.tensor_copy(out=dq_f[:, :], in_=dq_i[:, :]),
          reads=[("dqi",)], writes=[("dqf",)])
        p("sync", lambda s: s.dma_start(out=alibi_t[:, :], in_=alibi_in[:, :]),
          writes=[("alibi",)], dma=True)
        p("sync", lambda s: s.dma_start(out=ag_t[:, :], in_=ag_in[:, :]), writes=[("ag",)], dma=True)
        p("sync", lambda s: s.dma_start(out=au_t[:, :], in_=au_in[:, :]), writes=[("au",)], dma=True)
        p("sync", lambda s: s.dma_start(out=ad_t[:, :], in_=ad_in[:, :]), writes=[("ad",)], dma=True)
        prog.flush(block)

        with ExitStack() as sA:
            x_own = sb(sA, "x_own", [128, NKC, 1024], BF16)
            k_fm = sb(sA, "k_fm", [128, 2048], BF16)
            v_tok = sb(sA, "v_tok", [128, 2048], BF16)

            with ExitStack() as sA1:
                x_oth = sb(sA1, "x_oth", [128, NKC, 1024], BF16)
                v_fm = sb(sA1, "v_fm", [128, 2048], BF16)
                wkv_bf = sb(sA1, "wkv_bf", [128, 2, 4, 128], BF16)
                ksq = sb(sA1, "ksq", [128, 512], BF16)

                for j in range(2 * NKC):
                    kc, half = j % NKC, j // NKC
                    dst = x_own if half == 0 else x_oth
                    slot = j % 2
                    p("sync", lambda s, kc=kc, half=half, slot=slot:
                        s.dma_start(out=stage_b[:, slot, :],
                                    in_=xt[kc * 128:(kc + 1) * 128,
                                           half * 1024:(half + 1) * 1024]),
                      writes=[("stb", slot)], dma=True)
                    conv("act" if j % 4 == 0 else "dve",
                         lambda dst=dst, kc=kc: dst[:, kc, :],
                         lambda slot=slot: stage_b[:, slot, :],
                         reads=[("stb", slot)], writes=[("x", half, kc)])

                for ocp in range(2):
                    for kc4 in range(NKC // 4):
                        slot = kc4 % 2
                        p("sync", lambda s, ocp=ocp, kc4=kc4, slot=slot:
                            s.dma_start(out=stage_w[:, slot, 0:512]
                                        .rearrange("p (f c) -> p f c", f=4),
                                        in_=wqkv[kc4 * 512:(kc4 + 1) * 512,
                                                 4096 + ocp * 128:4096 + (ocp + 1) * 128]
                                        .rearrange("(f p) c -> p f c", p=128)),
                          writes=[("stw", slot)], dma=True)
                        conv("dve",
                             lambda slot=slot: wkv_bf[:, slot, :, :]
                             .rearrange("p f c -> p (f c)"),
                             lambda slot=slot: stage_w[:, slot, 0:512],
                             reads=[("stw", slot)], writes=[("wkv", slot)])
                        for ff in range(4):
                            kc = kc4 * 4 + ff
                            for tq in range(4):
                                half, col = tq // 2, (tq % 2) * 512
                                src = x_own if half == 0 else x_oth
                                p("pe", lambda t, kc=kc, slot=slot, ff=ff, tq=tq, src=src, col=col:
                                    t.matmul(ps[tq][:, :], wkv_bf[:, slot, ff, :],
                                             src[:, kc, col:col + 512],
                                             start=(kc == 0), stop=(kc == NKC - 1)),
                                  reads=[("wkv", slot), ("x", half, kc)], writes=[("ps", tq)])
                    dstt = k_fm if ocp == 0 else v_fm
                    for tq in range(4):
                        p("act", lambda e, tq=tq, dstt=dstt:
                            e.copy(out=dstt[:, tq * 512:(tq + 1) * 512], in_=ps[tq][:, :]),
                          reads=[("ps", tq)], writes=[("kv", ocp, tq)])
                        if ocp == 0:
                            p("act", lambda e, tq=tq:
                                e.square(out=ksq[:, :], in_=k_fm[:, tq * 512:(tq + 1) * 512]),
                              reads=[("kv", 0, tq)], writes=[("ksq",)])
                            p("pe", lambda t: t.matmul(ps[4][:, :], ones16[:, :], ksq[:, :],
                                                       start=True, stop=True),
                              reads=[("ksq",), ("ones16",)], writes=[("ps", 4)])
                            if tq == 0:
                                p("dve", lambda v: v.reduce_max(out=kmax2[:, :], in_=ps[4][:, :],
                                                                axis=AX.X),
                                  reads=[("ps", 4)], writes=[("kmax2",)])
                            else:
                                p("dve", lambda v: v.reduce_max(out=kn_tmp[:, :], in_=ps[4][:, :],
                                                                axis=AX.X),
                                  reads=[("ps", 4)], writes=[("kntmp",)])
                                p("dve", lambda v: v.tensor_tensor(out=kmax2[:, :], in0=kmax2[:, :],
                                                                   in1=kn_tmp[:, :], op=ALU.max),
                                  reads=[("kntmp",), ("kmax2",)], writes=[("kmax2",)])
                if debug:
                    p("sync", lambda s: s.dma_start(out=dbg_xown[:, :],
                                                    in_=x_own[:, :, :].rearrange("p a b -> p (a b)")),
                      reads=[("x", 0, 0), ("x", 0, 31)], dma=True)
                    p("sync", lambda s: s.dma_start(out=dbg_kmax[:, :], in_=kmax2[:, :]),
                      reads=[("kmax2",)], dma=True)
                    p("sync", lambda s: s.dma_start(out=dbg_kfm[:, :], in_=k_fm[:, :]),
                      reads=[("kv", 0, 0), ("kv", 0, 1), ("kv", 0, 2), ("kv", 0, 3)], dma=True)
                    p("sync", lambda s: s.dma_start(out=dbg_dqf[:, :], in_=dq_f[:, :]),
                      reads=[("dqf",)], dma=True)
                for kt in range(16):
                    p("pe", lambda t, kt=kt: t.transpose(ps_t16[:, :],
                                                         v_fm[:, kt * 128:(kt + 1) * 128],
                                                         ident16[:, :]),
                      reads=[("kv", 1, kt // 4), ("id16",)], writes=[("pst16",)])
                    p("act", lambda e, kt=kt: e.copy(out=v_tok[:, kt * 128:(kt + 1) * 128],
                                                     in_=ps_t16[:, :]),
                      reads=[("pst16",)], writes=[("vtok", kt)])
                prog.flush(block)
            prog.barrier()
            prog.flush(block)

            with ExitStack() as sA2:
                qblk = sb(sA2, "qblk", [128, 2, 16, 512], BF16)
                q_sb = sb(sA2, "q_sb", [128, 4, 512], BF16)
                qsq = sb(sA2, "qsq", [128, 4, 512], BF16)
                colvec = sb(sA2, "colvec", [128, 4, 512], F32)
                acc_e = sb(sA2, "acc_e", [128, 4, 512], F32)
                sc_b = sb(sA2, "sc_b", [128, 2, 512], F32)
                e_b = sb(sA2, "e_b", [128, 8, 512], BF16)
                linv = sb(sA2, "linv", [128, 2, 512], F32)
                attn_o = sb(sA2, "attn_o", [128, 4, 512], BF16)

                for hg in range(8):
                    for kc4 in range(NKC // 4):
                        slot = kc4 % 2
                        p("sync", lambda s, kc4=kc4, slot=slot, hg=hg:
                            s.dma_start(out=stage_w[:, slot, :]
                                        .rearrange("p (f c) -> p f c", f=4),
                                        in_=wqkv[kc4 * 512:(kc4 + 1) * 512,
                                                 hg * 512:(hg + 1) * 512]
                                        .rearrange("(f p) c -> p f c", p=128)),
                          writes=[("stw", slot)], dma=True)
                        half, kk0 = kc4 // 4, (kc4 % 4) * 4
                        conv("act" if kc4 % 4 == 3 else "dve",
                             lambda half=half, kk0=kk0: qblk[:, half, kk0:kk0 + 4, :]
                             .rearrange("p f c -> p (f c)"),
                             lambda slot=slot: stage_w[:, slot, :],
                             reads=[("stw", slot)], writes=[("qblk", kc4)])
                    for hp in range(4):
                        h = hg * 4 + hp
                        if h < NO_COLVEC_H:
                            p("dve", lambda v, hp=hp, h=h:
                                v.tensor_scalar_mul(colvec[:, hp, :], dq_f[:, :], -SLOPES[h]),
                              reads=[("dqf",)], writes=[("colvec", hp)])
                    for qt in range(2):
                        for hp in range(4):
                            for kc in range(NKC):
                                half, kk = kc // 16, kc % 16
                                rd = ([("qblk", g) for g in range(8)]
                                      if kc == 0 else [("qblk", kc // 4)])
                                p("pe", lambda t, hp=hp, kc=kc, half=half, kk=kk, qt=qt:
                                    t.matmul(ps[2 + hp][:, :],
                                             qblk[:, half, kk, hp * 128:(hp + 1) * 128],
                                             x_own[:, kc, qt * 512:(qt + 1) * 512],
                                             start=(kc == 0), stop=(kc == NKC - 1)),
                                  reads=rd + [("x", 0, kc)],
                                  writes=[("ps", 2 + hp)])
                            p("act", lambda e, hp=hp:
                                e.activation(out=q_sb[:, hp, :], in_=ps[2 + hp][:, :],
                                             func=AF.Copy, bias=0.0, scale=SCALE),
                              reads=[("ps", 2 + hp)], writes=[("qsb", hp)])
                        for hp in range(4):
                            p("act", lambda e, hp=hp: e.square(out=qsq[:, hp, :], in_=q_sb[:, hp, :]),
                              reads=[("qsb", hp)], writes=[("qsq", hp)])
                            p("pe", lambda t, hp=hp: t.matmul(ps[hp % 2][:, :], ones16[:, :],
                                                              qsq[:, hp, :], start=True, stop=True),
                              reads=[("qsq", hp), ("ones16",)], writes=[("ps", hp % 2)])
                            p("dve", lambda v, hp=hp:
                                v.reduce_max(out=qn2[:, hp:hp + 1], in_=ps[hp % 2][:, :], axis=AX.X),
                              reads=[("ps", hp % 2)], writes=[("qn2", hp)])
                            p("dve", lambda v, hp=hp:
                                v.tensor_tensor(out=cvec[:, hp:hp + 1], in0=qn2[:, hp:hp + 1],
                                                in1=kmax2[:, :], op=ALU.mult),
                              reads=[("qn2", hp), ("kmax2",)], writes=[("cv", hp)])
                            p("act", lambda e, hp=hp:
                                e.activation(out=cvec[:, hp:hp + 1], in_=cvec[:, hp:hp + 1],
                                             func=AF.Sqrt, bias=0.0, scale=1.0),
                              reads=[("cv", hp)], writes=[("cv", hp)])
                            p("dve", lambda v, hp=hp:
                                v.tensor_scalar_add(cvec[:, hp:hp + 1], cvec[:, hp:hp + 1], MARGIN),
                              reads=[("cv", hp)], writes=[("cv", hp)])
                            h = hg * 4 + hp
                            col0 = (h * 2 + qt) * 16
                            p("dve", lambda v, hp=hp, col0=col0:
                                v.tensor_scalar(out=bias_all[:, hp, :],
                                                in0=alibi_t[:, col0:col0 + 16],
                                                scalar1=cvec[:, hp:hp + 1], scalar2=0.0,
                                                op0=ALU.subtract, op1=ALU.subtract),
                              reads=[("alibi",), ("cv", hp)], writes=[("bias", hp)])
                        if debug and hg == 0 and qt == 0:
                            p("sync", lambda s: s.dma_start(out=dbg_qsb[:, :],
                                                            in_=q_sb[:, :, :].rearrange("p a b -> p (a b)")),
                              reads=[("qsb", 0), ("qsb", 1), ("qsb", 2), ("qsb", 3)], dma=True)
                            p("sync", lambda s: s.dma_start(out=dbg_qn2[:, :], in_=qn2[:, :]),
                              reads=[("qn2", 0), ("qn2", 1), ("qn2", 2), ("qn2", 3)], dma=True)
                            p("sync", lambda s: s.dma_start(out=dbg_cvec[:, :], in_=cvec[:, :]),
                              reads=[("cv", 0), ("cv", 1), ("cv", 2), ("cv", 3)], dma=True)
                            p("sync", lambda s: s.dma_start(out=dbg_bias[:, :],
                                                            in_=bias_all[:, :, :].rearrange("p a b -> p (a b)")),
                              reads=[("bias", 0), ("bias", 1), ("bias", 2), ("bias", 3)], dma=True)
                        def emit_av(kt):
                            for hp in range(4):
                                ebs = (kt % 2) * 4 + hp
                                p("pe", lambda t, kt=kt, ebs=ebs, hp=hp:
                                    t.matmul(ps[2 + hp][:, :], v_tok[:, kt * 128:(kt + 1) * 128],
                                             e_b[:, ebs, :], start=(kt == 0), stop=(kt == 15)),
                                  reads=[("vtok", kt), ("eb", ebs)], writes=[("ps", 2 + hp)])

                        for kt in range(16):
                            partial_i = kt - 4 * qt
                            for hp in range(4):
                                h = hg * 4 + hp
                                spx = [0, 1, 6][(kt * 4 + hp) % 3]
                                scs = (kt * 4 + hp) % 2
                                ebs = (kt % 2) * 4 + hp
                                p("pe", lambda t, kt=kt, hp=hp, spx=spx:
                                    t.matmul(ps[spx][:, :], k_fm[:, kt * 128:(kt + 1) * 128],
                                             q_sb[:, hp, :], start=True, stop=True),
                                  reads=[("kv", 0, kt // 4), ("qsb", hp)], writes=[("ps", spx)])
                                partial = 0 <= partial_i < 4
                                if h < NO_COLVEC_H:
                                    p("dve", lambda v, spx=spx, scs=scs, hp=hp:
                                        v.tensor_tensor(out=sc_b[:, scs, :], in0=ps[spx][:, :],
                                                        in1=colvec[:, hp, :], op=ALU.add),
                                      reads=[("ps", spx), ("colvec", hp)], writes=[("sc", scs)])
                                    if partial:
                                        p("dve", lambda v, scs=scs, pi=partial_i:
                                            v.tensor_tensor(out=sc_b[:, scs, :], in0=sc_b[:, scs, :],
                                                            in1=maskb[:, pi, :], op=ALU.add),
                                          reads=[("sc", scs), ("mask", pi)], writes=[("sc", scs)])
                                    p("act", lambda e, scs=scs, ebs=ebs, hp=hp, kt=kt:
                                        e.activation(out=e_b[:, ebs, :], in_=sc_b[:, scs, :],
                                                     func=AF.Exp, bias=bias_all[:, hp, kt:kt + 1],
                                                     scale=1.0),
                                      reads=[("sc", scs), ("bias", hp)], writes=[("eb", ebs)])
                                elif partial:
                                    p("dve", lambda v, spx=spx, scs=scs, pi=partial_i:
                                        v.tensor_tensor(out=sc_b[:, scs, :], in0=ps[spx][:, :],
                                                        in1=maskb[:, pi, :], op=ALU.add),
                                      reads=[("ps", spx), ("mask", pi)], writes=[("sc", scs)])
                                    p("act", lambda e, scs=scs, ebs=ebs, hp=hp, kt=kt:
                                        e.activation(out=e_b[:, ebs, :], in_=sc_b[:, scs, :],
                                                     func=AF.Exp, bias=bias_all[:, hp, kt:kt + 1],
                                                     scale=1.0),
                                      reads=[("sc", scs), ("bias", hp)], writes=[("eb", ebs)])
                                else:
                                    p("act", lambda e, spx=spx, ebs=ebs, hp=hp, kt=kt:
                                        e.activation(out=e_b[:, ebs, :], in_=ps[spx][:, :],
                                                     func=AF.Exp, bias=bias_all[:, hp, kt:kt + 1],
                                                     scale=1.0),
                                      reads=[("ps", spx), ("bias", hp)], writes=[("eb", ebs)])
                                if kt == 0:
                                    p("dve", lambda v, ebs=ebs, hp=hp:
                                        v.tensor_copy(out=acc_e[:, hp, :], in_=e_b[:, ebs, :]),
                                      reads=[("eb", ebs)], writes=[("acc", hp)])
                                else:
                                    p("dve", lambda v, ebs=ebs, hp=hp:
                                        v.tensor_tensor(out=acc_e[:, hp, :], in0=acc_e[:, hp, :],
                                                        in1=e_b[:, ebs, :], op=ALU.add),
                                      reads=[("eb", ebs), ("acc", hp)], writes=[("acc", hp)])
                            if kt >= 1:
                                emit_av(kt - 1)
                        emit_av(15)
                        if debug and hg == 0 and qt == 0:
                            p("sync", lambda s: s.dma_start(out=dbg_acc[:, :],
                                                            in_=acc_e[:, :, :].rearrange("p a b -> p (a b)")),
                              reads=[("acc", 0), ("acc", 1), ("acc", 2), ("acc", 3)], dma=True)
                        for hp in range(4):
                            lp = hp % 2
                            p("pe", lambda t, hp=hp, lp=lp:
                                t.matmul(ps[lp][:, :], ones32f[:, :], acc_e[:, hp, :],
                                         start=True, stop=True),
                              reads=[("acc", hp), ("ones32f",)], writes=[("ps", lp)])
                            p("dve", lambda v, lp=lp:
                                v.reciprocal(out=linv[:, lp, :], in_=ps[lp][:, :]),
                              reads=[("ps", lp)], writes=[("linv", lp)])
                            p("dve", lambda v, hp=hp, lp=lp:
                                v.tensor_tensor(out=attn_o[:, hp, :], in0=ps[2 + hp][:, :],
                                                in1=linv[:, lp, :], op=ALU.mult),
                              reads=[("ps", 2 + hp), ("linv", lp)], writes=[("ao", hp)])
                            h = hg * 4 + hp
                            p("act", lambda s, h=h, hp=hp, qt=qt:
                                s.dma_start(out=attn_spill[h * 128:(h + 1) * 128,
                                                           qt * 512:(qt + 1) * 512],
                                            in_=attn_o[:, hp, :]),
                              reads=[("ao", hp)], dma=True)
                prog.flush(block)
        prog.barrier()
        prog.flush(block)

        with ExitStack() as sB:
            attn_sb = sb(sB, "attn_sb", [128, NKC, 1024], BF16)
            woblk = sb(sB, "woblk", [128, 2, NKC, 256], BF16)
            rb = sb(sB, "rb", [128, 2, 2, 512], BF16)
            sqb = sb(sB, "sqb", [128, 2, 2, 512], BF16)
            rc = sb(sB, "rc", [128, 4, 512], F32)
            xs = sb(sB, "xs", [128, 2, 512], F32)
            pend_stats = []
            for h4 in range(NKC // 4):
                p("sync", lambda s, h4=h4: s.dma_start(
                    out=attn_sb[:, h4 * 4:(h4 + 1) * 4, :],
                    in_=attn_spill[h4 * 512:(h4 + 1) * 512, :]
                    .rearrange("(f p) c -> p f c", p=128)),
                  writes=[("at", h4 * 4 + i) for i in range(4)], dma=True)
            for fp in range(16):
                par = fp % 2
                for kc4 in range(NKC // 4):
                    slot = kc4 % 2
                    p("sync", lambda s, fp=fp, kc4=kc4, slot=slot:
                        s.dma_start(out=stage_w[:, slot, 0:1024]
                                    .rearrange("p (f c) -> p f c", f=4),
                                    in_=wo[kc4 * 512:(kc4 + 1) * 512, fp * 256:(fp + 1) * 256]
                                    .rearrange("(f p) c -> p f c", p=128)),
                      writes=[("stw", slot)], dma=True)
                    conv("act" if kc4 % 4 == 3 else "dve",
                         lambda par=par, kc4=kc4: woblk[:, par, kc4 * 4:(kc4 + 1) * 4, :]
                         .rearrange("p f c -> p (f c)"),
                         lambda slot=slot: stage_w[:, slot, 0:1024],
                         reads=[("stw", slot)], writes=[("wob", par, kc4)])
                for tq in range(2):
                    grp = fp * 2 + tq
                    gp = grp % 2
                    p("sync", lambda s, fp=fp, tq=tq:
                        s.dma_start(out=xs[:, :, :],
                                    in_=xt[fp * 256:(fp + 1) * 256,
                                           tq * 512:(tq + 1) * 512]
                                    .rearrange("(f p) c -> p f c", p=128)),
                      writes=[("xs", 0), ("xs", 1)], dma=True)
                    for j in range(2):
                        for kc in range(NKC):
                            rd = ([("wob", par, k2) for k2 in range(NKC // 4)] if kc == 0
                                  else [("wob", par, kc // 4)])
                            p("pe", lambda t, par=par, kc=kc, j=j, tq=tq:
                                t.matmul(ps[j][:, :], woblk[:, par, kc, j * 128:(j + 1) * 128],
                                         attn_sb[:, kc, tq * 512:(tq + 1) * 512],
                                         start=(kc == 0), stop=(kc == NKC - 1)),
                              reads=rd + [("at", kc)], writes=[("ps", j)])
                    for th in pend_stats:
                        th()
                    pend_stats = []
                    for j in range(2):
                        foc = fp * 2 + j
                        rslot = (foc * 2 + tq) % 4
                        p("dve", lambda v, j=j, rslot=rslot:
                            v.tensor_tensor(out=rc[:, rslot, :], in0=ps[j][:, :],
                                            in1=xs[:, j, :], op=ALU.add),
                          reads=[("ps", j), ("xs", j)], writes=[("rc", rslot)])
                        p("act", lambda s, foc=foc, tq=tq, rslot=rslot:
                            s.dma_start(out=rbuf[foc * 128:(foc + 1) * 128,
                                                 tq * 512:(tq + 1) * 512],
                                        in_=rc[:, rslot, :]),
                          reads=[("rc", rslot)], dma=True)
                        p("act", lambda e, rslot=rslot, gp=gp, j=j:
                            e.copy(out=rb[:, gp, j, :], in_=rc[:, rslot, :]),
                          reads=[("rc", rslot)], writes=[("rb", gp, j)])
                        p("act", lambda e, gp=gp, j=j: e.square(out=sqb[:, gp, j, :],
                                                                in_=rb[:, gp, j, :]),
                          reads=[("rb", gp, j)], writes=[("sqb", gp, j)])

                        def mk_stats(j=j, tq=tq, foc=foc, gp=gp):
                            def th():
                                p("pe", lambda t: t.matmul(ps[4 + tq][:, :], ones16[:, :],
                                                           rb[:, gp, j, :],
                                                           start=(foc == 0), stop=(foc == NKC - 1)),
                                  reads=[("rb", gp, j), ("ones16",)], writes=[("ps", 4 + tq)])
                                p("pe", lambda t: t.matmul(ps[2 + tq][:, :], ones16[:, :],
                                                           sqb[:, gp, j, :],
                                                           start=(foc == 0), stop=(foc == NKC - 1)),
                                  reads=[("sqb", gp, j), ("ones16",)], writes=[("ps", 2 + tq)])
                            return th
                        pend_stats.append(mk_stats())
            for th in pend_stats:
                th()
            pend_stats = []
            prog.flush(block)
        prog.barrier()
        prog.flush(block)

        with ExitStack() as sC:
            y1 = sb(sC, "y1", [128, NKC, 1024], BF16)

            with ExitStack() as sC1:
                varb = sb(sC1, "varb", [128, 1024], F32)
                for tq in range(2):
                    p("act", lambda e, tq=tq:
                        e.activation(out=mean_bc[:, tq * 512:(tq + 1) * 512],
                                     in_=ps[4 + tq][:, :], func=AF.Copy, bias=0.0, scale=1.0 / D),
                      reads=[("ps", 4 + tq)], writes=[("meanbc",)])
                    p("act", lambda e, tq=tq:
                        e.activation(out=varb[:, tq * 512:(tq + 1) * 512],
                                     in_=ps[2 + tq][:, :], func=AF.Copy, bias=0.0, scale=1.0 / D),
                      reads=[("ps", 2 + tq)], writes=[("varb",)])
                p("dve", lambda v: v.tensor_tensor(out=rstd_bc[:, :], in0=mean_bc[:, :],
                                                   in1=mean_bc[:, :], op=ALU.mult),
                  reads=[("meanbc",)], writes=[("rstdbc",)])
                p("dve", lambda v: v.tensor_tensor(out=varb[:, :], in0=varb[:, :],
                                                   in1=rstd_bc[:, :], op=ALU.subtract),
                  reads=[("varb",), ("rstdbc",)], writes=[("varb",)])
                p("act", lambda e: e.activation(out=varb[:, :], in_=varb[:, :], func=AF.Sqrt,
                                                bias=eps_t[:, :], scale=1.0),
                  reads=[("varb",), ("eps",)], writes=[("varb",)])
                p("dve", lambda v: v.reciprocal(out=rstd_bc[:, :], in_=varb[:, :]),
                  reads=[("varb",)], writes=[("rstdbc",)])
                for kc in range(NKC):
                    slot = kc % 2
                    p("sync", lambda s, kc=kc, slot=slot:
                        s.dma_start(out=stage_b[:, slot, :],
                                    in_=rbuf[kc * 128:(kc + 1) * 128, :]),
                      writes=[("stb", slot)], dma=True)
                    p("dve", lambda v, slot=slot:
                        v.tensor_tensor(out=stage_b[:, slot, :], in0=stage_b[:, slot, :],
                                        in1=mean_bc[:, :], op=ALU.subtract),
                      reads=[("stb", slot), ("meanbc",)], writes=[("stb", slot)])
                    p("dve", lambda v, kc=kc, slot=slot:
                        v.tensor_tensor(out=y1[:, kc, :], in0=stage_b[:, slot, :],
                                        in1=rstd_bc[:, :], op=ALU.mult),
                      reads=[("stb", slot), ("rstdbc",)], writes=[("y1", kc)])
                prog.flush(block)
            prog.barrier()
            prog.flush(block)

            with ExitStack() as sC2:
                wgblk = sb(sC2, "wgblk", [128, 2, NKC, 256], BF16)
                wublk = sb(sC2, "wublk", [128, 2, NKC, 256], BF16)
                sgb = sb(sC2, "sgb", [128, 2, 512], F32)
                sub_ = sb(sC2, "sub", [128, 2, 512], F32)
                hb = sb(sC2, "hb", [128, 4, 512], BF16)
                for ip in range(43):
                    par = ip % 2
                    for kc4 in range(NKC // 4):
                        slot = kc4 % 2
                        p("sync", lambda s, ip=ip, kc4=kc4, slot=slot:
                            s.dma_start(out=stage_w[:, slot, 0:1024]
                                        .rearrange("p (f c) -> p f c", f=4),
                                        in_=wgt[kc4 * 512:(kc4 + 1) * 512,
                                                ip * 256:(ip + 1) * 256]
                                        .rearrange("(f p) c -> p f c", p=128)),
                          writes=[("stw", slot)], dma=True)
                        conv("act" if kc4 % 4 == 3 else "dve",
                             lambda par=par, kc4=kc4: wgblk[:, par, kc4 * 4:(kc4 + 1) * 4, :]
                             .rearrange("p f c -> p (f c)"),
                             lambda slot=slot: stage_w[:, slot, 0:1024],
                             reads=[("stw", slot)], writes=[("wgb", par, kc4)])
                        p("sync", lambda s, ip=ip, kc4=kc4, slot=slot:
                            s.dma_start(out=stage_w[:, slot, 1024:2048]
                                        .rearrange("p (f c) -> p f c", f=4),
                                        in_=wut[kc4 * 512:(kc4 + 1) * 512,
                                                ip * 256:(ip + 1) * 256]
                                        .rearrange("(f p) c -> p f c", p=128)),
                          writes=[("stw2", slot)], dma=True)
                        conv("dve" if kc4 % 4 != 1 else "act",
                             lambda par=par, kc4=kc4: wublk[:, par, kc4 * 4:(kc4 + 1) * 4, :]
                             .rearrange("p f c -> p (f c)"),
                             lambda slot=slot: stage_w[:, slot, 1024:2048],
                             reads=[("stw2", slot)], writes=[("wub", par, kc4)])
                    for jj in range(2):
                        ioc = ip * 2 + jj
                        for tq in range(2):
                            pg = 2 * ((jj + tq) % 2)
                            pu = pg + 1
                            for kc in range(NKC):
                                rdg = ([("wgb", par, k2) for k2 in range(NKC // 4)] if kc == 0
                                       else [("wgb", par, kc // 4)])
                                rdu = ([("wub", par, k2) for k2 in range(NKC // 4)] if kc == 0
                                       else [("wub", par, kc // 4)])
                                p("pe", lambda t, par=par, kc=kc, jj=jj, tq=tq, pg=pg:
                                    t.matmul(ps[pg][:, :],
                                             wgblk[:, par, kc, jj * 128:(jj + 1) * 128],
                                             y1[:, kc, tq * 512:(tq + 1) * 512],
                                             start=(kc == 0), stop=(kc == NKC - 1)),
                                  reads=rdg + [("y1", kc)], writes=[("ps", pg)])
                                p("pe", lambda t, par=par, kc=kc, jj=jj, tq=tq, pu=pu:
                                    t.matmul(ps[pu][:, :],
                                             wublk[:, par, kc, jj * 128:(jj + 1) * 128],
                                             y1[:, kc, tq * 512:(tq + 1) * 512],
                                             start=(kc == 0), stop=(kc == NKC - 1)),
                                  reads=rdu + [("y1", kc)], writes=[("ps", pu)])
                            p("act", lambda e, pg=pg, tq=tq, ioc=ioc:
                                e.activation(out=sgb[:, tq, :], in_=ps[pg][:, :], func=AF.Silu,
                                             bias=0.0, scale=ag_t[:, ioc:ioc + 1]),
                              reads=[("ps", pg), ("ag",)], writes=[("sg", tq)])
                            p("act", lambda e, pu=pu, tq=tq, ioc=ioc:
                                e.activation(out=sub_[:, tq, :], in_=ps[pu][:, :], func=AF.Copy,
                                             bias=0.0, scale=au_t[:, ioc:ioc + 1]),
                              reads=[("ps", pu), ("au",)], writes=[("su", tq)])
                            hs = (ioc * 2 + tq) % 4
                            p("dve", lambda v, tq=tq, hs=hs:
                                v.tensor_tensor(out=hb[:, hs, :], in0=sgb[:, tq, :],
                                                in1=sub_[:, tq, :], op=ALU.mult),
                              reads=[("sg", tq), ("su", tq)], writes=[("hb", hs)])
                            p("act", lambda s, ioc=ioc, tq=tq, hs=hs:
                                s.dma_start(out=hbuf[ioc * 128:(ioc + 1) * 128,
                                                     tq * 512:(tq + 1) * 512],
                                            in_=hb[:, hs, :]),
                              reads=[("hb", hs)], dma=True)
                prog.flush(block)
        prog.barrier()
        prog.flush(block)

        with ExitStack() as sDE:
          m2_raw = sb(sDE, "m2_raw", [128, 1024], F32)
          msq2_raw = sb(sDE, "msq2_raw", [128, 1024], F32)
          with ExitStack() as sD:
            h_sb = sb(sD, "h_sb", [128, NI, 512], BF16)
            wdb = sb(sD, "wdb", [128, 2, 4, 256], BF16)
            fb = sb(sD, "fb", [128, 2, 512], F32)
            tb = sb(sD, "tb", [128, 2, 512], F32)
            r2c = sb(sD, "r2c", [128, 4, 512], F32)
            r2b = sb(sD, "r2b", [128, 2, 2, 512], BF16)
            sq2 = sb(sD, "sq2", [128, 2, 2, 512], BF16)
            IC_GROUPS = [(g * 4, min(4, NI - g * 4)) for g in range((NI + 3) // 4)]
            pend2 = []
            for tq in range(2):
                for ig in range((NI + 3) // 4):
                    ic0, gn = IC_GROUPS[ig]
                    p("sync", lambda s, ic0=ic0, gn=gn, tq=tq:
                        s.dma_start(out=h_sb[:, ic0:ic0 + gn, :],
                                    in_=hbuf[ic0 * 128:(ic0 + gn) * 128,
                                             tq * 512:(tq + 1) * 512]
                                    .rearrange("(f p) c -> p f c", p=128)),
                      writes=[("hs", ic0 + i) for i in range(gn)], dma=True)
                for fp in range(16):
                    for ig in range((NI + 3) // 4):
                        ic0, gn = IC_GROUPS[ig]
                        slot = ig % 2
                        p("sync", lambda s, fp=fp, ic0=ic0, gn=gn, slot=slot:
                            s.dma_start(out=stage_w[:, slot, 0:gn * 256]
                                        .rearrange("p (f c) -> p f c", f=gn),
                                        in_=wdt[ic0 * 128:(ic0 + gn) * 128,
                                                fp * 256:(fp + 1) * 256]
                                        .rearrange("(f p) c -> p f c", p=128)),
                          writes=[("stw", slot)], dma=True)
                        conv("act" if ig % 4 == 3 else "dve",
                             lambda slot=slot, gn=gn: wdb[:, slot, 0:gn, :]
                             .rearrange("p f c -> p (f c)"),
                             lambda slot=slot, gn=gn: stage_w[:, slot, 0:gn * 256],
                             reads=[("stw", slot)], writes=[("wdb", slot)])
                        for i in range(gn):
                            ic = ic0 + i
                            for j in range(2):
                                pslot = (fp % 2) * 2 + j
                                p("pe", lambda t, slot=slot, i=i, j=j, ic=ic, pslot=pslot:
                                    t.matmul(ps[pslot][:, :],
                                             wdb[:, slot, i, j * 128:(j + 1) * 128],
                                             h_sb[:, ic, :], start=(ic == 0), stop=(ic == NI - 1)),
                                  reads=[("wdb", slot), ("hs", ic)], writes=[("ps", pslot)])
                    for th in pend2:
                        th()
                    pend2 = []
                    gp = fp % 2
                    p("sync", lambda s, fp=fp, tq=tq:
                        s.dma_start(out=tb[:, :, :],
                                    in_=rbuf[fp * 256:(fp + 1) * 256,
                                             tq * 512:(tq + 1) * 512]
                                    .rearrange("(f p) c -> p f c", p=128)),
                      writes=[("tb", 0), ("tb", 1)], dma=True)
                    for j in range(2):
                        foc = fp * 2 + j
                        pslot = (fp % 2) * 2 + j
                        p("act", lambda e, pslot=pslot, j=j, foc=foc:
                            e.activation(out=fb[:, j, :], in_=ps[pslot][:, :], func=AF.Copy,
                                         bias=0.0, scale=ad_t[:, foc:foc + 1]),
                          reads=[("ps", pslot), ("ad",)], writes=[("fb", j)])
                        p("dve", lambda v, j=j, tq=tq:
                            v.tensor_tensor(out=tb[:, j, :], in0=tb[:, j, :],
                                            in1=mean_bc[:, tq * 512:(tq + 1) * 512],
                                            op=ALU.subtract),
                          reads=[("tb", j), ("meanbc",)], writes=[("tb", j)])
                        p("dve", lambda v, j=j, tq=tq:
                            v.tensor_tensor(out=tb[:, j, :], in0=tb[:, j, :],
                                            in1=rstd_bc[:, tq * 512:(tq + 1) * 512],
                                            op=ALU.mult),
                          reads=[("tb", j), ("rstdbc",)], writes=[("tb", j)])
                        r2s = (foc * 2 + tq) % 4
                        p("dve", lambda v, j=j, r2s=r2s:
                            v.tensor_tensor(out=r2c[:, r2s, :], in0=tb[:, j, :],
                                            in1=fb[:, j, :], op=ALU.add),
                          reads=[("tb", j), ("fb", j)], writes=[("r2c", r2s)])
                        p("act", lambda s, foc=foc, tq=tq, r2s=r2s:
                            s.dma_start(out=r2buf[foc * 128:(foc + 1) * 128,
                                                  tq * 512:(tq + 1) * 512],
                                        in_=r2c[:, r2s, :]),
                          reads=[("r2c", r2s)], dma=True)
                        p("act", lambda e, r2s=r2s, gp=gp, j=j:
                            e.copy(out=r2b[:, gp, j, :], in_=r2c[:, r2s, :]),
                          reads=[("r2c", r2s)], writes=[("r2b", gp, j)])
                        p("act", lambda e, gp=gp, j=j: e.square(out=sq2[:, gp, j, :],
                                                                in_=r2b[:, gp, j, :]),
                          reads=[("r2b", gp, j)], writes=[("sq2", gp, j)])

                        def mk_stats2(j=j, foc=foc, gp=gp):
                            def th():
                                p("pe", lambda t: t.matmul(ps[4][:, :], ones16[:, :],
                                                           r2b[:, gp, j, :],
                                                           start=(foc == 0), stop=(foc == NKC - 1)),
                                  reads=[("r2b", gp, j), ("ones16",)], writes=[("ps", 4)])
                                p("pe", lambda t: t.matmul(ps[5][:, :], ones16[:, :],
                                                           sq2[:, gp, j, :],
                                                           start=(foc == 0), stop=(foc == NKC - 1)),
                                  reads=[("sq2", gp, j), ("ones16",)], writes=[("ps", 5)])
                            return th
                        pend2.append(mk_stats2())
                for th in pend2:
                    th()
                pend2 = []
                p("act", lambda e, tq=tq: e.activation(out=m2_raw[:, tq * 512:(tq + 1) * 512],
                                                       in_=ps[4][:, :], func=AF.Copy,
                                                       bias=0.0, scale=1.0 / D),
                  reads=[("ps", 4)], writes=[("m2raw",)])
                p("act", lambda e, tq=tq: e.activation(out=msq2_raw[:, tq * 512:(tq + 1) * 512],
                                                       in_=ps[5][:, :], func=AF.Copy,
                                                       bias=0.0, scale=1.0 / D),
                  reads=[("ps", 5)], writes=[("msq2raw",)])
            prog.flush(block)
          prog.barrier()
          prog.flush(block)

          with ExitStack() as sE:
              y2t = sb(sE, "y2t", [128, 4, D], F32)
              y2c = sb(sE, "y2c", [128, 2, 512], F32)
              p("dve", lambda v: v.tensor_copy(out=mean_bc[:, :], in_=m2_raw[:, :]),
                reads=[("m2raw",)], writes=[("meanbc",)])
              p("dve", lambda v: v.tensor_tensor(out=rstd_bc[:, :], in0=mean_bc[:, :],
                                                 in1=mean_bc[:, :], op=ALU.mult),
                reads=[("meanbc",)], writes=[("rstdbc",)])
              p("dve", lambda v: v.tensor_tensor(out=msq2_raw[:, :], in0=msq2_raw[:, :],
                                                 in1=rstd_bc[:, :], op=ALU.subtract),
                reads=[("msq2raw",), ("rstdbc",)], writes=[("msq2raw",)])
              p("act", lambda e: e.activation(out=msq2_raw[:, :], in_=msq2_raw[:, :], func=AF.Sqrt,
                                              bias=eps_t[:, :], scale=1.0),
                reads=[("msq2raw",), ("eps",)], writes=[("msq2raw",)])
              p("dve", lambda v: v.reciprocal(out=rstd_bc[:, :], in_=msq2_raw[:, :]),
                reads=[("msq2raw",)], writes=[("rstdbc",)])
              for th in range(2):
                  for foc in range(NKC):
                      cs = foc % 2
                      p("sync", lambda s, foc=foc, th=th, cs=cs:
                          s.dma_start(out=y2c[:, cs, :],
                                      in_=r2buf[foc * 128:(foc + 1) * 128,
                                                th * 512:(th + 1) * 512]),
                        writes=[("y2c", cs)], dma=True)
                      p("dve", lambda v, cs=cs, th=th:
                          v.tensor_tensor(out=y2c[:, cs, :], in0=y2c[:, cs, :],
                                          in1=mean_bc[:, th * 512:(th + 1) * 512],
                                          op=ALU.subtract),
                        reads=[("y2c", cs), ("meanbc",)], writes=[("y2c", cs)])
                      p("dve", lambda v, cs=cs, th=th:
                          v.tensor_tensor(out=y2c[:, cs, :], in0=y2c[:, cs, :],
                                          in1=rstd_bc[:, th * 512:(th + 1) * 512], op=ALU.mult),
                        reads=[("y2c", cs), ("rstdbc",)], writes=[("y2c", cs)])
                      for i in range(4):
                          p("pe", lambda t, cs=cs, i=i:
                              t.transpose(ps[6][:, 0:128], y2c[:, cs, i * 128:(i + 1) * 128],
                                          ident32[:, :]),
                            reads=[("y2c", cs), ("id32",)], writes=[("ps", 6)])
                          p("act", lambda e, i=i, foc=foc:
                              e.copy(out=y2t[:, i, foc * 128:(foc + 1) * 128], in_=ps[6][:, 0:128]),
                            reads=[("ps", 6)], writes=[("y2t", i)])
                  for i in range(4):
                      tt = th * 4 + i
                      p("sync", lambda s, tt=tt, i=i:
                          s.dma_start(out=out[tt * 128:(tt + 1) * 128, :], in_=y2t[:, i, :]),
                        reads=[("y2t", i)], dma=True)
              prog.flush(block)
        prog.barrier()
        prog.flush(block)

    return nc, prog


def _build(debug=False):
    _, prog1 = build_core_program(mark=None, debug=debug)
    mark = {s: sorted(v) for s, v in prog1.used.items()}
    nc, _ = build_core_program(mark=mark, debug=debug)
    return nc


def _host_stage(L0_attn_hidden_states, L0_attn_Wqkv, L0_attn_Wo, L0_ffn_Wg, L0_ffn_Wu,
                L0_ffn_Wd, L0_ffn_ag, L0_ffn_au, L0_ffn_ad):
    x = np.asarray(L0_attn_hidden_states, dtype=np.float32)
    wqkv = np.ascontiguousarray(np.asarray(L0_attn_Wqkv, dtype=np.float32))
    wo_ = np.ascontiguousarray(np.asarray(L0_attn_Wo, dtype=np.float32))
    wgt = np.ascontiguousarray(np.asarray(L0_ffn_Wg, dtype=np.float32).T)
    wut = np.ascontiguousarray(np.asarray(L0_ffn_Wu, dtype=np.float32).T)
    wdt = np.ascontiguousarray(np.asarray(L0_ffn_Wd, dtype=np.float32).T)
    ag = np.ascontiguousarray(np.asarray(L0_ffn_ag, dtype=np.float32).reshape(NI, 128).T)
    au = np.ascontiguousarray(np.asarray(L0_ffn_au, dtype=np.float32).reshape(NI, 128).T)
    ad = np.ascontiguousarray(np.asarray(L0_ffn_ad, dtype=np.float32).reshape(NKC, 128).T)

    in_maps = []
    dk = np.arange(128)
    for core in range(8):
        b, half = core // 2, core % 2
        xb = x[b].T
        own = xb[:, half * TOK:(half + 1) * TOK]
        oth = xb[:, (1 - half) * TOK:(2 - half) * TOK]
        xtc = np.ascontiguousarray(np.concatenate([own, oth], axis=1))

        alibi = np.zeros((128, NH * 2 * 16), dtype=np.float32)
        own_base = half * TOK
        oth_base = (1 - half) * TOK
        for h in range(NH):
            for qt in range(2):
                q0 = own_base + qt * 512
                for kt in range(16):
                    col = (h * 2 + qt) * 16 + kt
                    if kt < 8:
                        kpos = own_base + kt * 128 + dk
                    else:
                        kpos = oth_base + (kt - 8) * 128 + dk
                    if kpos.min() > q0 + 511:
                        alibi[:, col] = NEG
                    else:
                        alibi[:, col] = SLOPES[h] * (kpos - q0)
        in_maps.append({
            "xt": xtc, "wqkv": wqkv, "wo": wo_, "wgt": wgt, "wut": wut, "wdt": wdt,
            "ag": ag, "au": au, "ad": ad, "alibi": alibi,
        })
    return in_maps


def kernel(**inputs):
    in_maps = _host_stage(**inputs)
    nc = _build()
    res = run_bass_kernel_spmd(nc, in_maps, core_ids=list(range(8)))
    outp = np.empty((B, S, D), dtype=np.float32)
    for core in range(8):
        b, half = core // 2, core % 2
        outp[b, half * TOK:(half + 1) * TOK, :] = res.results[core]["out"]
    return outp


# version 08 (final)
# speedup vs baseline: 1.3553x; 1.0904x over previous
import math
import numpy as np
from contextlib import ExitStack

import concourse.bass as bass
import concourse.mybir as mybir
from concourse.bass_utils import run_bass_kernel_spmd

F32 = mybir.dt.float32
BF16 = mybir.dt.bfloat16
I32 = mybir.dt.int32
AF = mybir.ActivationFunctionType
ALU = mybir.AluOpType
AX = mybir.AxisListType

B, S, D = 4, 2048, 4096
NH, HD = 32, 128
INTER = 11008
NI = INTER // 128
NKC = D // 128
TOK = 1024
SCALE = HD ** -0.5
EPS = 1e-5
MARGIN = 4.0
SLOPES = [2.0 ** (-0.25 * (h + 1)) for h in range(NH)]
NEG = -1e30
NO_COLVEC_H = 11

ENGS = ("pe", "act", "dve", "pool", "sync")
ENG_ATTR = {"pe": "tensor", "act": "scalar", "dve": "vector", "pool": "gpsimd", "sync": "sync"}
DMA_ENGS = ("sync", "act")
NLANES = 12


class Prog:

    def __init__(self, nc, ctx, mark=None):
        import bisect
        self._bisect = bisect
        self.nc = nc
        self.collect = mark is None
        self.mark = mark
        self.sems = {}
        for e in ENGS:
            self.sems[e] = ctx.enter_context(nc.semaphore(f"s_{e}"))
        for e in DMA_ENGS:
            for ln in range(NLANES):
                self.sems[f"{e}_dma{ln}"] = ctx.enter_context(nc.semaphore(f"s_{e}d{ln}"))
        self.dma_rr = {e: 0 for e in DMA_ENGS}
        self.cnt = {s: 0 for s in self.sems}
        self.last_w = {}
        self.readers = {}
        self.waited = {e: {} for e in ENGS}
        self.used = {s: set() for s in self.sems}
        self.phase_ops = {e: [] for e in ENGS}

    def _remap_val(self, sem, val):
        if "_dma" in sem:
            return val
        lst = self.mark[sem]
        i = self._bisect.bisect_left(lst, val)
        assert i < len(lst), f"remap miss {sem} {val}"
        return i + 1

    def _is_marked(self, sem, val):
        if "_dma" in sem:
            return True
        lst = self.mark[sem]
        i = self._bisect.bisect_left(lst, val)
        return i < len(lst) and lst[i] == val

    def op(self, eng, fn, reads=(), writes=(), dma=False, extra=()):
        if dma:
            ln = self.dma_rr[eng] % NLANES
            self.dma_rr[eng] += 1
            sem = f"{eng}_dma{ln}"
            extra = list(extra) + [(sem, self.cnt[sem])]
        else:
            sem = eng
        toks = {}

        def add(tok):
            if tok is None:
                return
            s, v = tok
            if toks.get(s, 0) < v:
                toks[s] = v

        for k in reads:
            add(self.last_w.get(k))
        for k in writes:
            add(self.last_w.get(k))
            for t in self.readers.get(k, ()):
                add(t)
        for t in extra:
            add(t)

        waits = []
        for s, v in sorted(toks.items()):
            if v <= 0:
                continue
            if s == "pe" and eng == "pe" and not dma:
                continue
            if self.waited[eng].get(s, 0) >= v:
                continue
            self.waited[eng][s] = v
            waits.append((s, v))
            if self.collect:
                self.used[s].add(v)

        self.cnt[sem] += 16 if dma else 1
        tok = (sem, self.cnt[sem])
        for k in reads:
            self.readers.setdefault(k, []).append(tok)
        for k in writes:
            self.last_w[k] = tok
            self.readers[k] = []

        if not self.collect:
            do_inc = self._is_marked(sem, self.cnt[sem])
            rw = [(self.sems[s], self._remap_val(s, v)) for s, v in waits]
            my_sem = self.sems[sem]
            inc_amt = 16 if dma else 1

            def emit(engine, rw=rw, do_inc=do_inc, fn=fn, my_sem=my_sem, inc_amt=inc_amt):
                for sh, v in rw:
                    engine.wait_ge(sh, v)
                instr = fn(engine)
                if do_inc:
                    instr.then_inc(my_sem, inc_amt)

            self.phase_ops[eng].append(emit)
        return tok

    def barrier(self):
        snap = [(s, self.cnt[s]) for s in self.sems if self.cnt[s] > 0]
        for e in ENGS:
            ex = [(s, v) for s, v in snap if s != e]
            self.op(e, lambda eng: eng.nop(), extra=ex)

    def flush(self, block):
        if self.collect or block is None:
            for e in ENGS:
                self.phase_ops[e] = []
            return
        for e in ENGS:
            ops = self.phase_ops[e]
            if not ops:
                continue
            self.phase_ops[e] = []
            section = getattr(block, ENG_ATTR[e])

            def body(engine, ops=ops):
                for o in ops:
                    o(engine)

            section(body)


def build_core_program(mark=None, debug=False):
    nc = bass.Bass("TRN2", target_bir_lowering=False)

    xt = nc.dram_tensor("xt", [D, S], F32, kind="ExternalInput")
    wqkv = nc.dram_tensor("wqkv", [D, 4352], F32, kind="ExternalInput")
    wo = nc.dram_tensor("wo", [D, D], F32, kind="ExternalInput")
    wgt = nc.dram_tensor("wgt", [D, INTER], F32, kind="ExternalInput")
    wut = nc.dram_tensor("wut", [D, INTER], F32, kind="ExternalInput")
    wdt = nc.dram_tensor("wdt", [INTER, D], F32, kind="ExternalInput")
    ag_in = nc.dram_tensor("ag", [128, NI], F32, kind="ExternalInput")
    au_in = nc.dram_tensor("au", [128, NI], F32, kind="ExternalInput")
    ad_in = nc.dram_tensor("ad", [128, NKC], F32, kind="ExternalInput")
    alibi_in = nc.dram_tensor("alibi", [128, NH * 2 * 16], F32, kind="ExternalInput")
    out = nc.dram_tensor("out", [TOK, D], F32, kind="ExternalOutput")

    spill_kind = "ExternalOutput" if debug else "Internal"
    attn_spill = nc.dram_tensor("attn_spill", [D, TOK], BF16, kind=spill_kind)
    rbuf = nc.dram_tensor("rbuf", [D, TOK], F32, kind=spill_kind)
    hbuf = nc.dram_tensor("hbuf", [INTER, TOK], BF16, kind=spill_kind)
    r2buf = nc.dram_tensor("r2buf", [D, TOK], F32, kind=spill_kind)
    if debug:
        dbg_kmax = nc.dram_tensor("dbg_kmax", [128, 1], F32, kind="ExternalOutput")
        dbg_kfm = nc.dram_tensor("dbg_kfm", [128, 2048], BF16, kind="ExternalOutput")
        dbg_bias = nc.dram_tensor("dbg_bias", [128, 64], F32, kind="ExternalOutput")
        dbg_cvec = nc.dram_tensor("dbg_cvec", [128, 4], F32, kind="ExternalOutput")
        dbg_qn2 = nc.dram_tensor("dbg_qn2", [128, 4], F32, kind="ExternalOutput")
        dbg_qsb = nc.dram_tensor("dbg_qsb", [128, 2048], BF16, kind="ExternalOutput")
        dbg_acc = nc.dram_tensor("dbg_acc", [128, 2048], F32, kind="ExternalOutput")
        dbg_dqf = nc.dram_tensor("dbg_dqf", [128, 512], F32, kind="ExternalOutput")
        dbg_xown = nc.dram_tensor("dbg_xown", [128, NKC * 1024], BF16, kind="ExternalOutput")

    with ExitStack() as top:
        prog = Prog(nc, top, mark=mark)
        p = prog.op
        block = None if mark is None else top.enter_context(nc.Block())

        def sb(stack, name, shape, dtype):
            return stack.enter_context(nc.sbuf_tensor(name, shape, dtype))

        ps = [top.enter_context(nc.psum_tensor(f"ps{i}", [128, 512], F32)) for i in range(7)]
        ps_t16 = top.enter_context(nc.psum_tensor("ps_t16", [128, 128], BF16))

        eps_t = sb(top, "eps_t", [128, 1], F32)
        ones16 = sb(top, "ones16", [128, 128], BF16)
        ones32f = sb(top, "ones32f", [128, 128], F32)
        ident16 = sb(top, "ident16", [128, 128], BF16)
        ident32 = sb(top, "ident32", [128, 128], F32)
        dq_i = sb(top, "dq_i", [128, 512], I32)
        dq_f = sb(top, "dq_f", [128, 512], F32)
        alibi_t = sb(top, "alibi_t", [128, NH * 2 * 16], F32)
        ag_t = sb(top, "ag_t", [128, NI], F32)
        au_t = sb(top, "au_t", [128, NI], F32)
        ad_t = sb(top, "ad_t", [128, NKC], F32)
        kmax2 = sb(top, "kmax2", [128, 1], F32)
        kn_tmp = sb(top, "kn_tmp", [128, 1], F32)
        stage_w = sb(top, "stage_w", [128, 2, 2048], F32)
        stage_b = sb(top, "stage_b", [128, 2, 1024], F32)
        mean_bc = sb(top, "mean_bc", [128, 1024], F32)
        rstd_bc = sb(top, "rstd_bc", [128, 1024], F32)
        bias_all = sb(top, "bias_all", [128, 4, 16], F32)
        cvec = sb(top, "cvec", [128, 4], F32)
        maskb = sb(top, "maskb", [128, 4, 512], F32)
        qn2 = sb(top, "qn2", [128, 4], F32)

        def conv(eng, dst_ap_fn, src_ap_fn, reads, writes):
            p("pool", lambda e: e.tensor_copy(out=dst_ap_fn(), in_=src_ap_fn()),
              reads=reads, writes=writes)

        p("pool", lambda g: g.memset(eps_t[:, :], EPS), writes=[("eps",)])
        p("pool", lambda g: g.memset(ones16[:, :], 1.0), writes=[("ones16",)])
        p("pool", lambda g: g.memset(ones32f[:, :], 1.0), writes=[("ones32f",)])
        p("pool", lambda g: g.memset(ident32[:, :], 0.0), writes=[("id32",)])
        p("pool", lambda g: g.affine_select(out=ident32[:, :], in_=ident32[:, :],
                                            compare_op=ALU.not_equal, fill=1.0, base=0,
                                            pattern=[[-1, 128]], channel_multiplier=1),
          reads=[("id32",)], writes=[("id32",)])
        p("act", lambda e: e.copy(out=ident16[:, :], in_=ident32[:, :]),
          reads=[("id32",)], writes=[("id16",)])
        p("pool", lambda g: g.iota(out=dq_i[:, :], pattern=[[1, 512]], base=0,
                                   channel_multiplier=0), writes=[("dqi",)])
        for pi in range(4):
            p("pool", lambda g, pi=pi: g.memset(maskb[:, pi, :], 0.0), writes=[("mask", pi)])
            p("pool", lambda g, pi=pi: g.affine_select(out=maskb[:, pi, :], in_=maskb[:, pi, :],
                                                       compare_op=ALU.is_ge, fill=NEG,
                                                       base=-128 * pi, pattern=[[1, 512]],
                                                       channel_multiplier=-1),
              reads=[("mask", pi)], writes=[("mask", pi)])
        p("dve", lambda v: v.tensor_copy(out=dq_f[:, :], in_=dq_i[:, :]),
          reads=[("dqi",)], writes=[("dqf",)])
        p("sync", lambda s: s.dma_start(out=alibi_t[:, :], in_=alibi_in[:, :]),
          writes=[("alibi",)], dma=True)
        p("sync", lambda s: s.dma_start(out=ag_t[:, :], in_=ag_in[:, :]), writes=[("ag",)], dma=True)
        p("sync", lambda s: s.dma_start(out=au_t[:, :], in_=au_in[:, :]), writes=[("au",)], dma=True)
        p("sync", lambda s: s.dma_start(out=ad_t[:, :], in_=ad_in[:, :]), writes=[("ad",)], dma=True)
        prog.flush(block)

        with ExitStack() as sA:
            x_own = sb(sA, "x_own", [128, NKC, 1024], BF16)
            k_fm = sb(sA, "k_fm", [128, 2048], BF16)
            v_tok = sb(sA, "v_tok", [128, 2048], BF16)

            with ExitStack() as sA1:
                x_oth = sb(sA1, "x_oth", [128, NKC, 1024], BF16)
                v_fm = sb(sA1, "v_fm", [128, 2048], BF16)
                wkv_bf = sb(sA1, "wkv_bf", [128, 2, 4, 128], BF16)
                ksq = sb(sA1, "ksq", [128, 512], BF16)

                for j in range(2 * NKC):
                    kc, half = j % NKC, j // NKC
                    dst = x_own if half == 0 else x_oth
                    slot = j % 2
                    p("sync", lambda s, kc=kc, half=half, slot=slot:
                        s.dma_start(out=stage_b[:, slot, :],
                                    in_=xt[kc * 128:(kc + 1) * 128,
                                           half * 1024:(half + 1) * 1024]),
                      writes=[("stb", slot)], dma=True)
                    conv("act" if j % 4 == 0 else "dve",
                         lambda dst=dst, kc=kc: dst[:, kc, :],
                         lambda slot=slot: stage_b[:, slot, :],
                         reads=[("stb", slot)], writes=[("x", half, kc)])

                for ocp in range(2):
                    for kc4 in range(NKC // 4):
                        slot = kc4 % 2
                        p("sync", lambda s, ocp=ocp, kc4=kc4, slot=slot:
                            s.dma_start(out=stage_w[:, slot, 0:512]
                                        .rearrange("p (f c) -> p f c", f=4),
                                        in_=wqkv[kc4 * 512:(kc4 + 1) * 512,
                                                 4096 + ocp * 128:4096 + (ocp + 1) * 128]
                                        .rearrange("(f p) c -> p f c", p=128)),
                          writes=[("stw", slot)], dma=True)
                        conv("dve",
                             lambda slot=slot: wkv_bf[:, slot, :, :]
                             .rearrange("p f c -> p (f c)"),
                             lambda slot=slot: stage_w[:, slot, 0:512],
                             reads=[("stw", slot)], writes=[("wkv", slot)])
                        for ff in range(4):
                            kc = kc4 * 4 + ff
                            for tq in range(4):
                                half, col = tq // 2, (tq % 2) * 512
                                src = x_own if half == 0 else x_oth
                                p("pe", lambda t, kc=kc, slot=slot, ff=ff, tq=tq, src=src, col=col:
                                    t.matmul(ps[tq][:, :], wkv_bf[:, slot, ff, :],
                                             src[:, kc, col:col + 512],
                                             start=(kc == 0), stop=(kc == NKC - 1)),
                                  reads=[("wkv", slot), ("x", half, kc)], writes=[("ps", tq)])
                    dstt = k_fm if ocp == 0 else v_fm
                    for tq in range(4):
                        p("act", lambda e, tq=tq, dstt=dstt:
                            e.copy(out=dstt[:, tq * 512:(tq + 1) * 512], in_=ps[tq][:, :]),
                          reads=[("ps", tq)], writes=[("kv", ocp, tq)])
                        if ocp == 0:
                            p("act", lambda e, tq=tq:
                                e.square(out=ksq[:, :], in_=k_fm[:, tq * 512:(tq + 1) * 512]),
                              reads=[("kv", 0, tq)], writes=[("ksq",)])
                            p("pe", lambda t: t.matmul(ps[4][:, :], ones16[:, :], ksq[:, :],
                                                       start=True, stop=True),
                              reads=[("ksq",), ("ones16",)], writes=[("ps", 4)])
                            if tq == 0:
                                p("dve", lambda v: v.reduce_max(out=kmax2[:, :], in_=ps[4][:, :],
                                                                axis=AX.X),
                                  reads=[("ps", 4)], writes=[("kmax2",)])
                            else:
                                p("dve", lambda v: v.reduce_max(out=kn_tmp[:, :], in_=ps[4][:, :],
                                                                axis=AX.X),
                                  reads=[("ps", 4)], writes=[("kntmp",)])
                                p("dve", lambda v: v.tensor_tensor(out=kmax2[:, :], in0=kmax2[:, :],
                                                                   in1=kn_tmp[:, :], op=ALU.max),
                                  reads=[("kntmp",), ("kmax2",)], writes=[("kmax2",)])
                if debug:
                    p("sync", lambda s: s.dma_start(out=dbg_xown[:, :],
                                                    in_=x_own[:, :, :].rearrange("p a b -> p (a b)")),
                      reads=[("x", 0, 0), ("x", 0, 31)], dma=True)
                    p("sync", lambda s: s.dma_start(out=dbg_kmax[:, :], in_=kmax2[:, :]),
                      reads=[("kmax2",)], dma=True)
                    p("sync", lambda s: s.dma_start(out=dbg_kfm[:, :], in_=k_fm[:, :]),
                      reads=[("kv", 0, 0), ("kv", 0, 1), ("kv", 0, 2), ("kv", 0, 3)], dma=True)
                    p("sync", lambda s: s.dma_start(out=dbg_dqf[:, :], in_=dq_f[:, :]),
                      reads=[("dqf",)], dma=True)
                for kt in range(16):
                    p("pe", lambda t, kt=kt: t.transpose(ps_t16[:, :],
                                                         v_fm[:, kt * 128:(kt + 1) * 128],
                                                         ident16[:, :]),
                      reads=[("kv", 1, kt // 4), ("id16",)], writes=[("pst16",)])
                    p("act", lambda e, kt=kt: e.copy(out=v_tok[:, kt * 128:(kt + 1) * 128],
                                                     in_=ps_t16[:, :]),
                      reads=[("pst16",)], writes=[("vtok", kt)])
                prog.flush(block)
            prog.barrier()
            prog.flush(block)

            with ExitStack() as sA2:
                qblk = sb(sA2, "qblk", [128, 2, 16, 512], BF16)
                q_sb = sb(sA2, "q_sb", [128, 4, 512], BF16)
                qsq = sb(sA2, "qsq", [128, 4, 512], BF16)
                colvec = sb(sA2, "colvec", [128, 4, 512], F32)
                acc_e = sb(sA2, "acc_e", [128, 4, 512], F32)
                sc_b = sb(sA2, "sc_b", [128, 4, 512], F32)
                e_b = sb(sA2, "e_b", [128, 8, 512], BF16)
                linv = sb(sA2, "linv", [128, 2, 512], F32)
                attn_o = sb(sA2, "attn_o", [128, 4, 512], BF16)

                for hg in range(8):
                    for kc4 in range(NKC // 4):
                        slot = kc4 % 2
                        p("sync", lambda s, kc4=kc4, slot=slot, hg=hg:
                            s.dma_start(out=stage_w[:, slot, :]
                                        .rearrange("p (f c) -> p f c", f=4),
                                        in_=wqkv[kc4 * 512:(kc4 + 1) * 512,
                                                 hg * 512:(hg + 1) * 512]
                                        .rearrange("(f p) c -> p f c", p=128)),
                          writes=[("stw", slot)], dma=True)
                        half, kk0 = kc4 // 4, (kc4 % 4) * 4
                        conv("act" if kc4 % 4 == 3 else "dve",
                             lambda half=half, kk0=kk0: qblk[:, half, kk0:kk0 + 4, :]
                             .rearrange("p f c -> p (f c)"),
                             lambda slot=slot: stage_w[:, slot, :],
                             reads=[("stw", slot)], writes=[("qblk", kc4)])
                    for hp in range(4):
                        h = hg * 4 + hp
                        if h < NO_COLVEC_H:
                            p("dve", lambda v, hp=hp, h=h:
                                v.tensor_scalar_mul(colvec[:, hp, :], dq_f[:, :], -SLOPES[h]),
                              reads=[("dqf",)], writes=[("colvec", hp)])
                    for qt in range(2):
                        for hp in range(4):
                            for kc in range(NKC):
                                half, kk = kc // 16, kc % 16
                                rd = ([("qblk", g) for g in range(8)]
                                      if kc == 0 else [("qblk", kc // 4)])
                                p("pe", lambda t, hp=hp, kc=kc, half=half, kk=kk, qt=qt:
                                    t.matmul(ps[2 + hp][:, :],
                                             qblk[:, half, kk, hp * 128:(hp + 1) * 128],
                                             x_own[:, kc, qt * 512:(qt + 1) * 512],
                                             start=(kc == 0), stop=(kc == NKC - 1)),
                                  reads=rd + [("x", 0, kc)],
                                  writes=[("ps", 2 + hp)])
                            p("act", lambda e, hp=hp:
                                e.activation(out=q_sb[:, hp, :], in_=ps[2 + hp][:, :],
                                             func=AF.Copy, bias=0.0, scale=SCALE),
                              reads=[("ps", 2 + hp)], writes=[("qsb", hp)])
                        for hp in range(4):
                            p("pool", lambda g, hp=hp:
                                g.tensor_tensor(out=qsq[:, hp, :], in0=q_sb[:, hp, :],
                                                in1=q_sb[:, hp, :], op=ALU.mult),
                              reads=[("qsb", hp)], writes=[("qsq", hp)])
                            p("pe", lambda t, hp=hp: t.matmul(ps[hp % 2][:, :], ones16[:, :],
                                                              qsq[:, hp, :], start=True, stop=True),
                              reads=[("qsq", hp), ("ones16",)], writes=[("ps", hp % 2)])
                            p("dve", lambda v, hp=hp:
                                v.reduce_max(out=qn2[:, hp:hp + 1], in_=ps[hp % 2][:, :], axis=AX.X),
                              reads=[("ps", hp % 2)], writes=[("qn2", hp)])
                            p("dve", lambda v, hp=hp:
                                v.tensor_tensor(out=cvec[:, hp:hp + 1], in0=qn2[:, hp:hp + 1],
                                                in1=kmax2[:, :], op=ALU.mult),
                              reads=[("qn2", hp), ("kmax2",)], writes=[("cv", hp)])
                            p("act", lambda e, hp=hp:
                                e.activation(out=cvec[:, hp:hp + 1], in_=cvec[:, hp:hp + 1],
                                             func=AF.Sqrt, bias=0.0, scale=1.0),
                              reads=[("cv", hp)], writes=[("cv", hp)])
                            p("dve", lambda v, hp=hp:
                                v.tensor_scalar_add(cvec[:, hp:hp + 1], cvec[:, hp:hp + 1], MARGIN),
                              reads=[("cv", hp)], writes=[("cv", hp)])
                            h = hg * 4 + hp
                            col0 = (h * 2 + qt) * 16
                            p("dve", lambda v, hp=hp, col0=col0:
                                v.tensor_scalar(out=bias_all[:, hp, :],
                                                in0=alibi_t[:, col0:col0 + 16],
                                                scalar1=cvec[:, hp:hp + 1], scalar2=0.0,
                                                op0=ALU.subtract, op1=ALU.subtract),
                              reads=[("alibi",), ("cv", hp)], writes=[("bias", hp)])
                        if debug and hg == 0 and qt == 0:
                            p("sync", lambda s: s.dma_start(out=dbg_qsb[:, :],
                                                            in_=q_sb[:, :, :].rearrange("p a b -> p (a b)")),
                              reads=[("qsb", 0), ("qsb", 1), ("qsb", 2), ("qsb", 3)], dma=True)
                            p("sync", lambda s: s.dma_start(out=dbg_qn2[:, :], in_=qn2[:, :]),
                              reads=[("qn2", 0), ("qn2", 1), ("qn2", 2), ("qn2", 3)], dma=True)
                            p("sync", lambda s: s.dma_start(out=dbg_cvec[:, :], in_=cvec[:, :]),
                              reads=[("cv", 0), ("cv", 1), ("cv", 2), ("cv", 3)], dma=True)
                            p("sync", lambda s: s.dma_start(out=dbg_bias[:, :],
                                                            in_=bias_all[:, :, :].rearrange("p a b -> p (a b)")),
                              reads=[("bias", 0), ("bias", 1), ("bias", 2), ("bias", 3)], dma=True)
                        def emit_av(kt):
                            for hp in range(4):
                                ebs = (kt % 2) * 4 + hp
                                p("pe", lambda t, kt=kt, ebs=ebs, hp=hp:
                                    t.matmul(ps[2 + hp][:, :], v_tok[:, kt * 128:(kt + 1) * 128],
                                             e_b[:, ebs, :], start=(kt == 0), stop=(kt == 15)),
                                  reads=[("vtok", kt), ("eb", ebs)], writes=[("ps", 2 + hp)])

                        def emit_acc(kt):
                            for hp in range(4):
                                ebs = (kt % 2) * 4 + hp
                                if kt == 0:
                                    p("dve", lambda v, ebs=ebs, hp=hp:
                                        v.tensor_copy(out=acc_e[:, hp, :], in_=e_b[:, ebs, :]),
                                      reads=[("eb", ebs)], writes=[("acc", hp)])
                                else:
                                    p("dve", lambda v, ebs=ebs, hp=hp:
                                        v.tensor_tensor(out=acc_e[:, hp, :], in0=acc_e[:, hp, :],
                                                        in1=e_b[:, ebs, :], op=ALU.add),
                                      reads=[("eb", ebs), ("acc", hp)], writes=[("acc", hp)])

                        for kt in range(16):
                            partial_i = kt - 4 * qt
                            if kt >= 1:
                                emit_acc(kt - 1)
                            for hp in range(4):
                                h = hg * 4 + hp
                                spx = [0, 1, 6][(kt * 4 + hp) % 3]
                                scs = (kt * 4 + hp) % 4
                                ebs = (kt % 2) * 4 + hp
                                p("pe", lambda t, kt=kt, hp=hp, spx=spx:
                                    t.matmul(ps[spx][:, :], k_fm[:, kt * 128:(kt + 1) * 128],
                                             q_sb[:, hp, :], start=True, stop=True),
                                  reads=[("kv", 0, kt // 4), ("qsb", hp)], writes=[("ps", spx)])
                                partial = 0 <= partial_i < 4
                                if h < NO_COLVEC_H:
                                    p("dve", lambda v, spx=spx, scs=scs, hp=hp:
                                        v.tensor_tensor(out=sc_b[:, scs, :], in0=ps[spx][:, :],
                                                        in1=colvec[:, hp, :], op=ALU.add),
                                      reads=[("ps", spx), ("colvec", hp)], writes=[("sc", scs)])
                                    if partial:
                                        p("dve", lambda v, scs=scs, pi=partial_i:
                                            v.tensor_tensor(out=sc_b[:, scs, :], in0=sc_b[:, scs, :],
                                                            in1=maskb[:, pi, :], op=ALU.add),
                                          reads=[("sc", scs), ("mask", pi)], writes=[("sc", scs)])
                                    p("act", lambda e, scs=scs, ebs=ebs, hp=hp, kt=kt:
                                        e.activation(out=e_b[:, ebs, :], in_=sc_b[:, scs, :],
                                                     func=AF.Exp, bias=bias_all[:, hp, kt:kt + 1],
                                                     scale=1.0),
                                      reads=[("sc", scs), ("bias", hp)], writes=[("eb", ebs)])
                                elif partial:
                                    p("dve", lambda v, spx=spx, scs=scs, pi=partial_i:
                                        v.tensor_tensor(out=sc_b[:, scs, :], in0=ps[spx][:, :],
                                                        in1=maskb[:, pi, :], op=ALU.add),
                                      reads=[("ps", spx), ("mask", pi)], writes=[("sc", scs)])
                                    p("act", lambda e, scs=scs, ebs=ebs, hp=hp, kt=kt:
                                        e.activation(out=e_b[:, ebs, :], in_=sc_b[:, scs, :],
                                                     func=AF.Exp, bias=bias_all[:, hp, kt:kt + 1],
                                                     scale=1.0),
                                      reads=[("sc", scs), ("bias", hp)], writes=[("eb", ebs)])
                                else:
                                    p("act", lambda e, spx=spx, ebs=ebs, hp=hp, kt=kt:
                                        e.activation(out=e_b[:, ebs, :], in_=ps[spx][:, :],
                                                     func=AF.Exp, bias=bias_all[:, hp, kt:kt + 1],
                                                     scale=1.0),
                                      reads=[("ps", spx), ("bias", hp)], writes=[("eb", ebs)])
                            if kt >= 1:
                                emit_av(kt - 1)
                        emit_acc(15)
                        emit_av(15)
                        if debug and hg == 0 and qt == 0:
                            p("sync", lambda s: s.dma_start(out=dbg_acc[:, :],
                                                            in_=acc_e[:, :, :].rearrange("p a b -> p (a b)")),
                              reads=[("acc", 0), ("acc", 1), ("acc", 2), ("acc", 3)], dma=True)
                        for hp in range(4):
                            lp = hp % 2
                            p("pe", lambda t, hp=hp, lp=lp:
                                t.matmul(ps[lp][:, :], ones32f[:, :], acc_e[:, hp, :],
                                         start=True, stop=True),
                              reads=[("acc", hp), ("ones32f",)], writes=[("ps", lp)])
                            p("dve", lambda v, lp=lp:
                                v.reciprocal(out=linv[:, lp, :], in_=ps[lp][:, :]),
                              reads=[("ps", lp)], writes=[("linv", lp)])
                            p("dve", lambda v, hp=hp, lp=lp:
                                v.tensor_tensor(out=attn_o[:, hp, :], in0=ps[2 + hp][:, :],
                                                in1=linv[:, lp, :], op=ALU.mult),
                              reads=[("ps", 2 + hp), ("linv", lp)], writes=[("ao", hp)])
                            h = hg * 4 + hp
                            p("act", lambda s, h=h, hp=hp, qt=qt:
                                s.dma_start(out=attn_spill[h * 128:(h + 1) * 128,
                                                           qt * 512:(qt + 1) * 512],
                                            in_=attn_o[:, hp, :]),
                              reads=[("ao", hp)], dma=True)
                prog.flush(block)
        prog.barrier()
        prog.flush(block)

        with ExitStack() as sB:
            attn_sb = sb(sB, "attn_sb", [128, NKC, 1024], BF16)
            woblk = sb(sB, "woblk", [128, 2, NKC, 256], BF16)
            rb = sb(sB, "rb", [128, 2, 2, 512], BF16)
            sqb = sb(sB, "sqb", [128, 2, 2, 512], BF16)
            rc = sb(sB, "rc", [128, 4, 512], F32)
            xs = sb(sB, "xs", [128, 2, 512], F32)
            pend_stats = []
            for h4 in range(NKC // 4):
                p("sync", lambda s, h4=h4: s.dma_start(
                    out=attn_sb[:, h4 * 4:(h4 + 1) * 4, :],
                    in_=attn_spill[h4 * 512:(h4 + 1) * 512, :]
                    .rearrange("(f p) c -> p f c", p=128)),
                  writes=[("at", h4 * 4 + i) for i in range(4)], dma=True)
            for fp in range(16):
                par = fp % 2
                for kc4 in range(NKC // 4):
                    slot = kc4 % 2
                    p("sync", lambda s, fp=fp, kc4=kc4, slot=slot:
                        s.dma_start(out=stage_w[:, slot, 0:1024]
                                    .rearrange("p (f c) -> p f c", f=4),
                                    in_=wo[kc4 * 512:(kc4 + 1) * 512, fp * 256:(fp + 1) * 256]
                                    .rearrange("(f p) c -> p f c", p=128)),
                      writes=[("stw", slot)], dma=True)
                    conv("act" if kc4 % 4 == 3 else "dve",
                         lambda par=par, kc4=kc4: woblk[:, par, kc4 * 4:(kc4 + 1) * 4, :]
                         .rearrange("p f c -> p (f c)"),
                         lambda slot=slot: stage_w[:, slot, 0:1024],
                         reads=[("stw", slot)], writes=[("wob", par, kc4)])
                for tq in range(2):
                    grp = fp * 2 + tq
                    gp = grp % 2
                    p("sync", lambda s, fp=fp, tq=tq:
                        s.dma_start(out=xs[:, :, :],
                                    in_=xt[fp * 256:(fp + 1) * 256,
                                           tq * 512:(tq + 1) * 512]
                                    .rearrange("(f p) c -> p f c", p=128)),
                      writes=[("xs", 0), ("xs", 1)], dma=True)
                    for j in range(2):
                        for kc in range(NKC):
                            rd = ([("wob", par, k2) for k2 in range(NKC // 4)] if kc == 0
                                  else [("wob", par, kc // 4)])
                            p("pe", lambda t, par=par, kc=kc, j=j, tq=tq:
                                t.matmul(ps[j][:, :], woblk[:, par, kc, j * 128:(j + 1) * 128],
                                         attn_sb[:, kc, tq * 512:(tq + 1) * 512],
                                         start=(kc == 0), stop=(kc == NKC - 1)),
                              reads=rd + [("at", kc)], writes=[("ps", j)])
                    for th in pend_stats:
                        th()
                    pend_stats = []
                    for j in range(2):
                        foc = fp * 2 + j
                        rslot = (foc * 2 + tq) % 4
                        p("dve", lambda v, j=j, rslot=rslot:
                            v.tensor_tensor(out=rc[:, rslot, :], in0=ps[j][:, :],
                                            in1=xs[:, j, :], op=ALU.add),
                          reads=[("ps", j), ("xs", j)], writes=[("rc", rslot)])
                        p("act", lambda s, foc=foc, tq=tq, rslot=rslot:
                            s.dma_start(out=rbuf[foc * 128:(foc + 1) * 128,
                                                 tq * 512:(tq + 1) * 512],
                                        in_=rc[:, rslot, :]),
                          reads=[("rc", rslot)], dma=True)
                        p("act", lambda e, rslot=rslot, gp=gp, j=j:
                            e.copy(out=rb[:, gp, j, :], in_=rc[:, rslot, :]),
                          reads=[("rc", rslot)], writes=[("rb", gp, j)])
                        p("act", lambda e, gp=gp, j=j: e.square(out=sqb[:, gp, j, :],
                                                                in_=rb[:, gp, j, :]),
                          reads=[("rb", gp, j)], writes=[("sqb", gp, j)])

                        def mk_stats(j=j, tq=tq, foc=foc, gp=gp):
                            def th():
                                p("pe", lambda t: t.matmul(ps[4 + tq][:, :], ones16[:, :],
                                                           rb[:, gp, j, :],
                                                           start=(foc == 0), stop=(foc == NKC - 1)),
                                  reads=[("rb", gp, j), ("ones16",)], writes=[("ps", 4 + tq)])
                                p("pe", lambda t: t.matmul(ps[2 + tq][:, :], ones16[:, :],
                                                           sqb[:, gp, j, :],
                                                           start=(foc == 0), stop=(foc == NKC - 1)),
                                  reads=[("sqb", gp, j), ("ones16",)], writes=[("ps", 2 + tq)])
                            return th
                        pend_stats.append(mk_stats())
            for th in pend_stats:
                th()
            pend_stats = []
            prog.flush(block)
        prog.barrier()
        prog.flush(block)

        with ExitStack() as sC:
            y1 = sb(sC, "y1", [128, NKC, 1024], BF16)

            with ExitStack() as sC1:
                varb = sb(sC1, "varb", [128, 1024], F32)
                for tq in range(2):
                    p("act", lambda e, tq=tq:
                        e.activation(out=mean_bc[:, tq * 512:(tq + 1) * 512],
                                     in_=ps[4 + tq][:, :], func=AF.Copy, bias=0.0, scale=1.0 / D),
                      reads=[("ps", 4 + tq)], writes=[("meanbc",)])
                    p("act", lambda e, tq=tq:
                        e.activation(out=varb[:, tq * 512:(tq + 1) * 512],
                                     in_=ps[2 + tq][:, :], func=AF.Copy, bias=0.0, scale=1.0 / D),
                      reads=[("ps", 2 + tq)], writes=[("varb",)])
                p("dve", lambda v: v.tensor_tensor(out=rstd_bc[:, :], in0=mean_bc[:, :],
                                                   in1=mean_bc[:, :], op=ALU.mult),
                  reads=[("meanbc",)], writes=[("rstdbc",)])
                p("dve", lambda v: v.tensor_tensor(out=varb[:, :], in0=varb[:, :],
                                                   in1=rstd_bc[:, :], op=ALU.subtract),
                  reads=[("varb",), ("rstdbc",)], writes=[("varb",)])
                p("act", lambda e: e.activation(out=varb[:, :], in_=varb[:, :], func=AF.Sqrt,
                                                bias=eps_t[:, :], scale=1.0),
                  reads=[("varb",), ("eps",)], writes=[("varb",)])
                p("dve", lambda v: v.reciprocal(out=rstd_bc[:, :], in_=varb[:, :]),
                  reads=[("varb",)], writes=[("rstdbc",)])
                for kc in range(NKC):
                    slot = kc % 2
                    p("sync", lambda s, kc=kc, slot=slot:
                        s.dma_start(out=stage_b[:, slot, :],
                                    in_=rbuf[kc * 128:(kc + 1) * 128, :]),
                      writes=[("stb", slot)], dma=True)
                    p("dve", lambda v, slot=slot:
                        v.tensor_tensor(out=stage_b[:, slot, :], in0=stage_b[:, slot, :],
                                        in1=mean_bc[:, :], op=ALU.subtract),
                      reads=[("stb", slot), ("meanbc",)], writes=[("stb", slot)])
                    p("dve", lambda v, kc=kc, slot=slot:
                        v.tensor_tensor(out=y1[:, kc, :], in0=stage_b[:, slot, :],
                                        in1=rstd_bc[:, :], op=ALU.mult),
                      reads=[("stb", slot), ("rstdbc",)], writes=[("y1", kc)])
                prog.flush(block)
            prog.barrier()
            prog.flush(block)

            with ExitStack() as sC2:
                wgblk = sb(sC2, "wgblk", [128, 2, NKC, 256], BF16)
                wublk = sb(sC2, "wublk", [128, 2, NKC, 256], BF16)
                sgb = sb(sC2, "sgb", [128, 2, 512], F32)
                sub_ = sb(sC2, "sub", [128, 2, 512], F32)
                hb = sb(sC2, "hb", [128, 4, 512], BF16)
                for ip in range(43):
                    par = ip % 2
                    for kc4 in range(NKC // 4):
                        slot = kc4 % 2
                        p("sync", lambda s, ip=ip, kc4=kc4, slot=slot:
                            s.dma_start(out=stage_w[:, slot, 0:1024]
                                        .rearrange("p (f c) -> p f c", f=4),
                                        in_=wgt[kc4 * 512:(kc4 + 1) * 512,
                                                ip * 256:(ip + 1) * 256]
                                        .rearrange("(f p) c -> p f c", p=128)),
                          writes=[("stw", slot)], dma=True)
                        conv("act" if kc4 % 4 == 3 else "dve",
                             lambda par=par, kc4=kc4: wgblk[:, par, kc4 * 4:(kc4 + 1) * 4, :]
                             .rearrange("p f c -> p (f c)"),
                             lambda slot=slot: stage_w[:, slot, 0:1024],
                             reads=[("stw", slot)], writes=[("wgb", par, kc4)])
                        p("sync", lambda s, ip=ip, kc4=kc4, slot=slot:
                            s.dma_start(out=stage_w[:, slot, 1024:2048]
                                        .rearrange("p (f c) -> p f c", f=4),
                                        in_=wut[kc4 * 512:(kc4 + 1) * 512,
                                                ip * 256:(ip + 1) * 256]
                                        .rearrange("(f p) c -> p f c", p=128)),
                          writes=[("stw2", slot)], dma=True)
                        conv("dve" if kc4 % 4 != 1 else "act",
                             lambda par=par, kc4=kc4: wublk[:, par, kc4 * 4:(kc4 + 1) * 4, :]
                             .rearrange("p f c -> p (f c)"),
                             lambda slot=slot: stage_w[:, slot, 1024:2048],
                             reads=[("stw2", slot)], writes=[("wub", par, kc4)])
                    for jj in range(2):
                        ioc = ip * 2 + jj
                        for tq in range(2):
                            pg = 2 * ((jj + tq) % 2)
                            pu = pg + 1
                            for kc in range(NKC):
                                rdg = ([("wgb", par, k2) for k2 in range(NKC // 4)] if kc == 0
                                       else [("wgb", par, kc // 4)])
                                rdu = ([("wub", par, k2) for k2 in range(NKC // 4)] if kc == 0
                                       else [("wub", par, kc // 4)])
                                p("pe", lambda t, par=par, kc=kc, jj=jj, tq=tq, pg=pg:
                                    t.matmul(ps[pg][:, :],
                                             wgblk[:, par, kc, jj * 128:(jj + 1) * 128],
                                             y1[:, kc, tq * 512:(tq + 1) * 512],
                                             start=(kc == 0), stop=(kc == NKC - 1)),
                                  reads=rdg + [("y1", kc)], writes=[("ps", pg)])
                                p("pe", lambda t, par=par, kc=kc, jj=jj, tq=tq, pu=pu:
                                    t.matmul(ps[pu][:, :],
                                             wublk[:, par, kc, jj * 128:(jj + 1) * 128],
                                             y1[:, kc, tq * 512:(tq + 1) * 512],
                                             start=(kc == 0), stop=(kc == NKC - 1)),
                                  reads=rdu + [("y1", kc)], writes=[("ps", pu)])
                            p("act", lambda e, pg=pg, tq=tq, ioc=ioc:
                                e.activation(out=sgb[:, tq, :], in_=ps[pg][:, :], func=AF.Silu,
                                             bias=0.0, scale=ag_t[:, ioc:ioc + 1]),
                              reads=[("ps", pg), ("ag",)], writes=[("sg", tq)])
                            p("act", lambda e, pu=pu, tq=tq, ioc=ioc:
                                e.activation(out=sub_[:, tq, :], in_=ps[pu][:, :], func=AF.Copy,
                                             bias=0.0, scale=au_t[:, ioc:ioc + 1]),
                              reads=[("ps", pu), ("au",)], writes=[("su", tq)])
                            hs = (ioc * 2 + tq) % 4
                            p("dve", lambda v, tq=tq, hs=hs:
                                v.tensor_tensor(out=hb[:, hs, :], in0=sgb[:, tq, :],
                                                in1=sub_[:, tq, :], op=ALU.mult),
                              reads=[("sg", tq), ("su", tq)], writes=[("hb", hs)])
                            p("act", lambda s, ioc=ioc, tq=tq, hs=hs:
                                s.dma_start(out=hbuf[ioc * 128:(ioc + 1) * 128,
                                                     tq * 512:(tq + 1) * 512],
                                            in_=hb[:, hs, :]),
                              reads=[("hb", hs)], dma=True)
                prog.flush(block)
        prog.barrier()
        prog.flush(block)

        with ExitStack() as sDE:
          m2_raw = sb(sDE, "m2_raw", [128, 1024], F32)
          msq2_raw = sb(sDE, "msq2_raw", [128, 1024], F32)
          with ExitStack() as sD:
            h_sb = sb(sD, "h_sb", [128, NI, 512], BF16)
            wdb = sb(sD, "wdb", [128, 2, 4, 256], BF16)
            fb = sb(sD, "fb", [128, 2, 512], F32)
            tb = sb(sD, "tb", [128, 2, 512], F32)
            r2c = sb(sD, "r2c", [128, 4, 512], F32)
            r2b = sb(sD, "r2b", [128, 2, 2, 512], BF16)
            sq2 = sb(sD, "sq2", [128, 2, 2, 512], BF16)
            IC_GROUPS = [(g * 4, min(4, NI - g * 4)) for g in range((NI + 3) // 4)]
            pend2 = []
            for tq in range(2):
                for ig in range((NI + 3) // 4):
                    ic0, gn = IC_GROUPS[ig]
                    p("sync", lambda s, ic0=ic0, gn=gn, tq=tq:
                        s.dma_start(out=h_sb[:, ic0:ic0 + gn, :],
                                    in_=hbuf[ic0 * 128:(ic0 + gn) * 128,
                                             tq * 512:(tq + 1) * 512]
                                    .rearrange("(f p) c -> p f c", p=128)),
                      writes=[("hs", ic0 + i) for i in range(gn)], dma=True)
                for fp in range(16):
                    for ig in range((NI + 3) // 4):
                        ic0, gn = IC_GROUPS[ig]
                        slot = ig % 2
                        p("sync", lambda s, fp=fp, ic0=ic0, gn=gn, slot=slot:
                            s.dma_start(out=stage_w[:, slot, 0:gn * 256]
                                        .rearrange("p (f c) -> p f c", f=gn),
                                        in_=wdt[ic0 * 128:(ic0 + gn) * 128,
                                                fp * 256:(fp + 1) * 256]
                                        .rearrange("(f p) c -> p f c", p=128)),
                          writes=[("stw", slot)], dma=True)
                        conv("act" if ig % 4 == 3 else "dve",
                             lambda slot=slot, gn=gn: wdb[:, slot, 0:gn, :]
                             .rearrange("p f c -> p (f c)"),
                             lambda slot=slot, gn=gn: stage_w[:, slot, 0:gn * 256],
                             reads=[("stw", slot)], writes=[("wdb", slot)])
                        for i in range(gn):
                            ic = ic0 + i
                            for j in range(2):
                                pslot = (fp % 2) * 2 + j
                                p("pe", lambda t, slot=slot, i=i, j=j, ic=ic, pslot=pslot:
                                    t.matmul(ps[pslot][:, :],
                                             wdb[:, slot, i, j * 128:(j + 1) * 128],
                                             h_sb[:, ic, :], start=(ic == 0), stop=(ic == NI - 1)),
                                  reads=[("wdb", slot), ("hs", ic)], writes=[("ps", pslot)])
                    for th in pend2:
                        th()
                    pend2 = []
                    gp = fp % 2
                    p("sync", lambda s, fp=fp, tq=tq:
                        s.dma_start(out=tb[:, :, :],
                                    in_=rbuf[fp * 256:(fp + 1) * 256,
                                             tq * 512:(tq + 1) * 512]
                                    .rearrange("(f p) c -> p f c", p=128)),
                      writes=[("tb", 0), ("tb", 1)], dma=True)
                    for j in range(2):
                        foc = fp * 2 + j
                        pslot = (fp % 2) * 2 + j
                        p("act", lambda e, pslot=pslot, j=j, foc=foc:
                            e.activation(out=fb[:, j, :], in_=ps[pslot][:, :], func=AF.Copy,
                                         bias=0.0, scale=ad_t[:, foc:foc + 1]),
                          reads=[("ps", pslot), ("ad",)], writes=[("fb", j)])
                        p("dve", lambda v, j=j, tq=tq:
                            v.tensor_tensor(out=tb[:, j, :], in0=tb[:, j, :],
                                            in1=mean_bc[:, tq * 512:(tq + 1) * 512],
                                            op=ALU.subtract),
                          reads=[("tb", j), ("meanbc",)], writes=[("tb", j)])
                        p("dve", lambda v, j=j, tq=tq:
                            v.tensor_tensor(out=tb[:, j, :], in0=tb[:, j, :],
                                            in1=rstd_bc[:, tq * 512:(tq + 1) * 512],
                                            op=ALU.mult),
                          reads=[("tb", j), ("rstdbc",)], writes=[("tb", j)])
                        r2s = (foc * 2 + tq) % 4
                        p("dve", lambda v, j=j, r2s=r2s:
                            v.tensor_tensor(out=r2c[:, r2s, :], in0=tb[:, j, :],
                                            in1=fb[:, j, :], op=ALU.add),
                          reads=[("tb", j), ("fb", j)], writes=[("r2c", r2s)])
                        p("act", lambda s, foc=foc, tq=tq, r2s=r2s:
                            s.dma_start(out=r2buf[foc * 128:(foc + 1) * 128,
                                                  tq * 512:(tq + 1) * 512],
                                        in_=r2c[:, r2s, :]),
                          reads=[("r2c", r2s)], dma=True)
                        p("act", lambda e, r2s=r2s, gp=gp, j=j:
                            e.copy(out=r2b[:, gp, j, :], in_=r2c[:, r2s, :]),
                          reads=[("r2c", r2s)], writes=[("r2b", gp, j)])
                        p("act", lambda e, gp=gp, j=j: e.square(out=sq2[:, gp, j, :],
                                                                in_=r2b[:, gp, j, :]),
                          reads=[("r2b", gp, j)], writes=[("sq2", gp, j)])

                        def mk_stats2(j=j, foc=foc, gp=gp):
                            def th():
                                p("pe", lambda t: t.matmul(ps[4][:, :], ones16[:, :],
                                                           r2b[:, gp, j, :],
                                                           start=(foc == 0), stop=(foc == NKC - 1)),
                                  reads=[("r2b", gp, j), ("ones16",)], writes=[("ps", 4)])
                                p("pe", lambda t: t.matmul(ps[5][:, :], ones16[:, :],
                                                           sq2[:, gp, j, :],
                                                           start=(foc == 0), stop=(foc == NKC - 1)),
                                  reads=[("sq2", gp, j), ("ones16",)], writes=[("ps", 5)])
                            return th
                        pend2.append(mk_stats2())
                for th in pend2:
                    th()
                pend2 = []
                p("act", lambda e, tq=tq: e.activation(out=m2_raw[:, tq * 512:(tq + 1) * 512],
                                                       in_=ps[4][:, :], func=AF.Copy,
                                                       bias=0.0, scale=1.0 / D),
                  reads=[("ps", 4)], writes=[("m2raw",)])
                p("act", lambda e, tq=tq: e.activation(out=msq2_raw[:, tq * 512:(tq + 1) * 512],
                                                       in_=ps[5][:, :], func=AF.Copy,
                                                       bias=0.0, scale=1.0 / D),
                  reads=[("ps", 5)], writes=[("msq2raw",)])
            prog.flush(block)
          prog.barrier()
          prog.flush(block)

          with ExitStack() as sE:
              y2t = sb(sE, "y2t", [128, 4, D], F32)
              y2c = sb(sE, "y2c", [128, 2, 512], F32)
              p("dve", lambda v: v.tensor_copy(out=mean_bc[:, :], in_=m2_raw[:, :]),
                reads=[("m2raw",)], writes=[("meanbc",)])
              p("dve", lambda v: v.tensor_tensor(out=rstd_bc[:, :], in0=mean_bc[:, :],
                                                 in1=mean_bc[:, :], op=ALU.mult),
                reads=[("meanbc",)], writes=[("rstdbc",)])
              p("dve", lambda v: v.tensor_tensor(out=msq2_raw[:, :], in0=msq2_raw[:, :],
                                                 in1=rstd_bc[:, :], op=ALU.subtract),
                reads=[("msq2raw",), ("rstdbc",)], writes=[("msq2raw",)])
              p("act", lambda e: e.activation(out=msq2_raw[:, :], in_=msq2_raw[:, :], func=AF.Sqrt,
                                              bias=eps_t[:, :], scale=1.0),
                reads=[("msq2raw",), ("eps",)], writes=[("msq2raw",)])
              p("dve", lambda v: v.reciprocal(out=rstd_bc[:, :], in_=msq2_raw[:, :]),
                reads=[("msq2raw",)], writes=[("rstdbc",)])
              for th in range(2):
                  for foc in range(NKC):
                      cs = foc % 2
                      p("sync", lambda s, foc=foc, th=th, cs=cs:
                          s.dma_start(out=y2c[:, cs, :],
                                      in_=r2buf[foc * 128:(foc + 1) * 128,
                                                th * 512:(th + 1) * 512]),
                        writes=[("y2c", cs)], dma=True)
                      p("dve", lambda v, cs=cs, th=th:
                          v.tensor_tensor(out=y2c[:, cs, :], in0=y2c[:, cs, :],
                                          in1=mean_bc[:, th * 512:(th + 1) * 512],
                                          op=ALU.subtract),
                        reads=[("y2c", cs), ("meanbc",)], writes=[("y2c", cs)])
                      p("dve", lambda v, cs=cs, th=th:
                          v.tensor_tensor(out=y2c[:, cs, :], in0=y2c[:, cs, :],
                                          in1=rstd_bc[:, th * 512:(th + 1) * 512], op=ALU.mult),
                        reads=[("y2c", cs), ("rstdbc",)], writes=[("y2c", cs)])
                      for i in range(4):
                          p("pe", lambda t, cs=cs, i=i:
                              t.transpose(ps[6][:, 0:128], y2c[:, cs, i * 128:(i + 1) * 128],
                                          ident32[:, :]),
                            reads=[("y2c", cs), ("id32",)], writes=[("ps", 6)])
                          p("act", lambda e, i=i, foc=foc:
                              e.copy(out=y2t[:, i, foc * 128:(foc + 1) * 128], in_=ps[6][:, 0:128]),
                            reads=[("ps", 6)], writes=[("y2t", i)])
                  for i in range(4):
                      tt = th * 4 + i
                      p("sync", lambda s, tt=tt, i=i:
                          s.dma_start(out=out[tt * 128:(tt + 1) * 128, :], in_=y2t[:, i, :]),
                        reads=[("y2t", i)], dma=True)
              prog.flush(block)
        prog.barrier()
        prog.flush(block)

    return nc, prog


def _build(debug=False):
    _, prog1 = build_core_program(mark=None, debug=debug)
    mark = {s: sorted(v) for s, v in prog1.used.items()}
    nc, _ = build_core_program(mark=mark, debug=debug)
    return nc


def _host_stage(L0_attn_hidden_states, L0_attn_Wqkv, L0_attn_Wo, L0_ffn_Wg, L0_ffn_Wu,
                L0_ffn_Wd, L0_ffn_ag, L0_ffn_au, L0_ffn_ad):
    x = np.asarray(L0_attn_hidden_states, dtype=np.float32)
    wqkv = np.ascontiguousarray(np.asarray(L0_attn_Wqkv, dtype=np.float32))
    wo_ = np.ascontiguousarray(np.asarray(L0_attn_Wo, dtype=np.float32))
    wgt = np.ascontiguousarray(np.asarray(L0_ffn_Wg, dtype=np.float32).T)
    wut = np.ascontiguousarray(np.asarray(L0_ffn_Wu, dtype=np.float32).T)
    wdt = np.ascontiguousarray(np.asarray(L0_ffn_Wd, dtype=np.float32).T)
    ag = np.ascontiguousarray(np.asarray(L0_ffn_ag, dtype=np.float32).reshape(NI, 128).T)
    au = np.ascontiguousarray(np.asarray(L0_ffn_au, dtype=np.float32).reshape(NI, 128).T)
    ad = np.ascontiguousarray(np.asarray(L0_ffn_ad, dtype=np.float32).reshape(NKC, 128).T)

    in_maps = []
    dk = np.arange(128)
    for core in range(8):
        b, half = core // 2, core % 2
        xb = x[b].T
        own = xb[:, half * TOK:(half + 1) * TOK]
        oth = xb[:, (1 - half) * TOK:(2 - half) * TOK]
        xtc = np.ascontiguousarray(np.concatenate([own, oth], axis=1))

        alibi = np.zeros((128, NH * 2 * 16), dtype=np.float32)
        own_base = half * TOK
        oth_base = (1 - half) * TOK
        for h in range(NH):
            for qt in range(2):
                q0 = own_base + qt * 512
                for kt in range(16):
                    col = (h * 2 + qt) * 16 + kt
                    if kt < 8:
                        kpos = own_base + kt * 128 + dk
                    else:
                        kpos = oth_base + (kt - 8) * 128 + dk
                    if kpos.min() > q0 + 511:
                        alibi[:, col] = NEG
                    else:
                        alibi[:, col] = SLOPES[h] * (kpos - q0)
        in_maps.append({
            "xt": xtc, "wqkv": wqkv, "wo": wo_, "wgt": wgt, "wut": wut, "wdt": wdt,
            "ag": ag, "au": au, "ad": ad, "alibi": alibi,
        })
    return in_maps


def kernel(**inputs):
    in_maps = _host_stage(**inputs)
    nc = _build()
    res = run_bass_kernel_spmd(nc, in_maps, core_ids=list(range(8)))
    outp = np.empty((B, S, D), dtype=np.float32)
    for core in range(8):
        b, half = core // 2, core % 2
        outp[b, half * TOK:(half + 1) * TOK, :] = res.results[core]["out"]
    return outp


# version 11
# speedup vs baseline: 1.3826x; 1.0201x over previous
import math
import numpy as np
from contextlib import ExitStack

import concourse.bass as bass
import concourse.mybir as mybir
from concourse.bass_utils import run_bass_kernel_spmd

F32 = mybir.dt.float32
BF16 = mybir.dt.bfloat16
I32 = mybir.dt.int32
AF = mybir.ActivationFunctionType
ALU = mybir.AluOpType
AX = mybir.AxisListType

B, S, D = 4, 2048, 4096
NH, HD = 32, 128
INTER = 11008
NI = INTER // 128
NKC = D // 128
TOK = 1024
SCALE = HD ** -0.5
EPS = 1e-5
MARGIN = 4.0
SLOPES = [2.0 ** (-0.25 * (h + 1)) for h in range(NH)]
NEG = -1e30
NO_COLVEC_H = 11
ICG = [(g * 4, min(4, INTER // 128 - g * 4)) for g in range((INTER // 128 + 3) // 4)]

ENGS = ("pe", "act", "dve", "pool", "sync")
ENG_ATTR = {"pe": "tensor", "act": "scalar", "dve": "vector", "pool": "gpsimd", "sync": "sync"}
DMA_ENGS = ("sync", "act")
NLANES = 12


class Prog:

    def __init__(self, nc, ctx, mark=None):
        import bisect
        self._bisect = bisect
        self.nc = nc
        self.collect = mark is None
        self.mark = mark
        self.sems = {}
        for e in ENGS:
            self.sems[e] = ctx.enter_context(nc.semaphore(f"s_{e}"))
        for e in DMA_ENGS:
            for ln in range(NLANES):
                self.sems[f"{e}_dma{ln}"] = ctx.enter_context(nc.semaphore(f"s_{e}d{ln}"))
        self.dma_rr = {e: 0 for e in DMA_ENGS}
        self.cnt = {s: 0 for s in self.sems}
        self.last_w = {}
        self.readers = {}
        self.waited = {e: {} for e in ENGS}
        self.used = {s: set() for s in self.sems}
        self.phase_ops = {e: [] for e in ENGS}

    def _remap_val(self, sem, val):
        if "_dma" in sem:
            return val
        lst = self.mark[sem]
        i = self._bisect.bisect_left(lst, val)
        assert i < len(lst), f"remap miss {sem} {val}"
        return i + 1

    def _is_marked(self, sem, val):
        if "_dma" in sem:
            return True
        lst = self.mark[sem]
        i = self._bisect.bisect_left(lst, val)
        return i < len(lst) and lst[i] == val

    def op(self, eng, fn, reads=(), writes=(), dma=False, extra=()):
        if dma:
            ln = self.dma_rr[eng] % NLANES
            self.dma_rr[eng] += 1
            sem = f"{eng}_dma{ln}"
            extra = list(extra) + [(sem, self.cnt[sem])]
        else:
            sem = eng
        toks = {}

        def add(tok):
            if tok is None:
                return
            s, v = tok
            if toks.get(s, 0) < v:
                toks[s] = v

        for k in reads:
            add(self.last_w.get(k))
        for k in writes:
            add(self.last_w.get(k))
            for t in self.readers.get(k, ()):
                add(t)
        for t in extra:
            add(t)

        waits = []
        for s, v in sorted(toks.items()):
            if v <= 0:
                continue
            if s == "pe" and eng == "pe" and not dma:
                continue
            if self.waited[eng].get(s, 0) >= v:
                continue
            self.waited[eng][s] = v
            waits.append((s, v))
            if self.collect:
                self.used[s].add(v)

        self.cnt[sem] += 16 if dma else 1
        tok = (sem, self.cnt[sem])
        for k in reads:
            self.readers.setdefault(k, []).append(tok)
        for k in writes:
            self.last_w[k] = tok
            self.readers[k] = []

        if not self.collect:
            do_inc = self._is_marked(sem, self.cnt[sem])
            rw = [(self.sems[s], self._remap_val(s, v)) for s, v in waits]
            my_sem = self.sems[sem]
            inc_amt = 16 if dma else 1

            def emit(engine, rw=rw, do_inc=do_inc, fn=fn, my_sem=my_sem, inc_amt=inc_amt):
                for sh, v in rw:
                    engine.wait_ge(sh, v)
                instr = fn(engine)
                if do_inc:
                    instr.then_inc(my_sem, inc_amt)

            self.phase_ops[eng].append(emit)
        return tok

    def barrier(self):
        snap = [(s, self.cnt[s]) for s in self.sems if self.cnt[s] > 0]
        for e in ENGS:
            ex = [(s, v) for s, v in snap if s != e]
            self.op(e, lambda eng: eng.nop(), extra=ex)

    def flush(self, block):
        if self.collect or block is None:
            for e in ENGS:
                self.phase_ops[e] = []
            return
        for e in ENGS:
            ops = self.phase_ops[e]
            if not ops:
                continue
            self.phase_ops[e] = []
            section = getattr(block, ENG_ATTR[e])

            def body(engine, ops=ops):
                for o in ops:
                    o(engine)

            section(body)


def build_core_program(mark=None, debug=False):
    nc = bass.Bass("TRN2", target_bir_lowering=False)

    xt = nc.dram_tensor("xt", [D, S], F32, kind="ExternalInput")
    wqkv = nc.dram_tensor("wqkv", [D, 4352], F32, kind="ExternalInput")
    wo = nc.dram_tensor("wo", [D, D], F32, kind="ExternalInput")
    wgt = nc.dram_tensor("wgt", [D, INTER], F32, kind="ExternalInput")
    wut = nc.dram_tensor("wut", [D, INTER], F32, kind="ExternalInput")
    wdt = nc.dram_tensor("wdt", [INTER, D], F32, kind="ExternalInput")
    ag_in = nc.dram_tensor("ag", [128, NI], F32, kind="ExternalInput")
    au_in = nc.dram_tensor("au", [128, NI], F32, kind="ExternalInput")
    ad_in = nc.dram_tensor("ad", [128, NKC], F32, kind="ExternalInput")
    alibi_in = nc.dram_tensor("alibi", [128, NH * 2 * 16], F32, kind="ExternalInput")
    out = nc.dram_tensor("out", [TOK, D], F32, kind="ExternalOutput")

    spill_kind = "ExternalOutput" if debug else "Internal"
    attn_spill = nc.dram_tensor("attn_spill", [D, TOK], BF16, kind=spill_kind)
    rbuf = nc.dram_tensor("rbuf", [D, TOK], F32, kind=spill_kind)
    hbuf = nc.dram_tensor("hbuf", [INTER, TOK], BF16, kind=spill_kind)
    r2buf = nc.dram_tensor("r2buf", [D, TOK], F32, kind=spill_kind)
    wo16 = nc.dram_tensor("wo16", [16, 128, 8192], BF16, kind="Internal")
    wd16 = nc.dram_tensor("wd16", [16, 128, NI * 256], BF16, kind="Internal")
    if debug:
        dbg_kmax = nc.dram_tensor("dbg_kmax", [128, 1], F32, kind="ExternalOutput")
        dbg_kfm = nc.dram_tensor("dbg_kfm", [128, 2048], BF16, kind="ExternalOutput")
        dbg_bias = nc.dram_tensor("dbg_bias", [128, 64], F32, kind="ExternalOutput")
        dbg_cvec = nc.dram_tensor("dbg_cvec", [128, 4], F32, kind="ExternalOutput")
        dbg_qn2 = nc.dram_tensor("dbg_qn2", [128, 4], F32, kind="ExternalOutput")
        dbg_qsb = nc.dram_tensor("dbg_qsb", [128, 2048], BF16, kind="ExternalOutput")
        dbg_acc = nc.dram_tensor("dbg_acc", [128, 2048], F32, kind="ExternalOutput")
        dbg_dqf = nc.dram_tensor("dbg_dqf", [128, 512], F32, kind="ExternalOutput")
        dbg_xown = nc.dram_tensor("dbg_xown", [128, NKC * 1024], BF16, kind="ExternalOutput")

    with ExitStack() as top:
        prog = Prog(nc, top, mark=mark)
        p = prog.op
        block = None if mark is None else top.enter_context(nc.Block())

        def sb(stack, name, shape, dtype):
            return stack.enter_context(nc.sbuf_tensor(name, shape, dtype))

        ps = [top.enter_context(nc.psum_tensor(f"ps{i}", [128, 512], F32)) for i in range(7)]
        ps_t16 = top.enter_context(nc.psum_tensor("ps_t16", [128, 128], BF16))

        eps_t = sb(top, "eps_t", [128, 1], F32)
        ones16 = sb(top, "ones16", [128, 128], BF16)
        ones32f = sb(top, "ones32f", [128, 128], F32)
        ident16 = sb(top, "ident16", [128, 128], BF16)
        ident32 = sb(top, "ident32", [128, 128], F32)
        dq_i = sb(top, "dq_i", [128, 512], I32)
        dq_f = sb(top, "dq_f", [128, 512], F32)
        alibi_t = sb(top, "alibi_t", [128, NH * 2 * 16], F32)
        ag_t = sb(top, "ag_t", [128, NI], F32)
        au_t = sb(top, "au_t", [128, NI], F32)
        ad_t = sb(top, "ad_t", [128, NKC], F32)
        kmax2 = sb(top, "kmax2", [128, 1], F32)
        kn_tmp = sb(top, "kn_tmp", [128, 1], F32)
        stage_w = sb(top, "stage_w", [128, 2, 2048], F32)
        mean_bc = sb(top, "mean_bc", [128, 1024], F32)
        rstd_bc = sb(top, "rstd_bc", [128, 1024], F32)
        bias_all = sb(top, "bias_all", [128, 4, 16], F32)
        cvec = sb(top, "cvec", [128, 4], F32)
        maskb = sb(top, "maskb", [128, 4, 512], F32)
        qn2 = sb(top, "qn2", [128, 4], F32)

        def conv(eng, dst_ap_fn, src_ap_fn, reads, writes):
            p("pool", lambda e: e.tensor_copy(out=dst_ap_fn(), in_=src_ap_fn()),
              reads=reads, writes=writes)

        p("pool", lambda g: g.memset(eps_t[:, :], EPS), writes=[("eps",)])
        p("pool", lambda g: g.memset(ones16[:, :], 1.0), writes=[("ones16",)])
        p("pool", lambda g: g.memset(ones32f[:, :], 1.0), writes=[("ones32f",)])
        p("pool", lambda g: g.memset(ident32[:, :], 0.0), writes=[("id32",)])
        p("pool", lambda g: g.affine_select(out=ident32[:, :], in_=ident32[:, :],
                                            compare_op=ALU.not_equal, fill=1.0, base=0,
                                            pattern=[[-1, 128]], channel_multiplier=1),
          reads=[("id32",)], writes=[("id32",)])
        p("act", lambda e: e.copy(out=ident16[:, :], in_=ident32[:, :]),
          reads=[("id32",)], writes=[("id16",)])
        p("pool", lambda g: g.iota(out=dq_i[:, :], pattern=[[1, 512]], base=0,
                                   channel_multiplier=0), writes=[("dqi",)])
        for pi in range(4):
            p("pool", lambda g, pi=pi: g.memset(maskb[:, pi, :], 0.0), writes=[("mask", pi)])
            p("pool", lambda g, pi=pi: g.affine_select(out=maskb[:, pi, :], in_=maskb[:, pi, :],
                                                       compare_op=ALU.is_ge, fill=NEG,
                                                       base=-128 * pi, pattern=[[1, 512]],
                                                       channel_multiplier=-1),
              reads=[("mask", pi)], writes=[("mask", pi)])
        p("dve", lambda v: v.tensor_copy(out=dq_f[:, :], in_=dq_i[:, :]),
          reads=[("dqi",)], writes=[("dqf",)])
        p("sync", lambda s: s.dma_start(out=alibi_t[:, :], in_=alibi_in[:, :]),
          writes=[("alibi",)], dma=True)
        p("sync", lambda s: s.dma_start(out=ag_t[:, :], in_=ag_in[:, :]), writes=[("ag",)], dma=True)
        p("sync", lambda s: s.dma_start(out=au_t[:, :], in_=au_in[:, :]), writes=[("au",)], dma=True)
        p("sync", lambda s: s.dma_start(out=ad_t[:, :], in_=ad_in[:, :]), writes=[("ad",)], dma=True)
        prog.flush(block)

        with ExitStack() as sA:
            x_own = sb(sA, "x_own", [128, NKC, 1024], BF16)
            k_fm = sb(sA, "k_fm", [128, 2048], BF16)
            v_tok = sb(sA, "v_tok", [128, 2048], BF16)

            with ExitStack() as sA1:
                stage_b = sb(sA1, "stage_b", [128, 2, 1024], F32)
                cvt16 = sb(sA1, "cvt16", [128, 2, 2048], BF16)
                x_oth = sb(sA1, "x_oth", [128, NKC, 1024], BF16)
                v_fm = sb(sA1, "v_fm", [128, 2048], BF16)
                wkv_bf = sb(sA1, "wkv_bf", [128, 2, 4, 128], BF16)
                ksq = sb(sA1, "ksq", [128, 512], BF16)

                for j in range(2 * NKC):
                    kc, half = j % NKC, j // NKC
                    dst = x_own if half == 0 else x_oth
                    slot = j % 2
                    p("sync", lambda s, kc=kc, half=half, slot=slot:
                        s.dma_start(out=stage_b[:, slot, :],
                                    in_=xt[kc * 128:(kc + 1) * 128,
                                           half * 1024:(half + 1) * 1024]),
                      writes=[("stb", slot)], dma=True)
                    conv("act" if j % 4 == 0 else "dve",
                         lambda dst=dst, kc=kc: dst[:, kc, :],
                         lambda slot=slot: stage_b[:, slot, :],
                         reads=[("stb", slot)], writes=[("x", half, kc)])

                for ocp in range(2):
                    for kc4 in range(NKC // 4):
                        slot = kc4 % 2
                        p("sync", lambda s, ocp=ocp, kc4=kc4, slot=slot:
                            s.dma_start(out=stage_w[:, slot, 0:512]
                                        .rearrange("p (f c) -> p f c", f=4),
                                        in_=wqkv[kc4 * 512:(kc4 + 1) * 512,
                                                 4096 + ocp * 128:4096 + (ocp + 1) * 128]
                                        .rearrange("(f p) c -> p f c", p=128)),
                          writes=[("stw", slot)], dma=True)
                        conv("dve",
                             lambda slot=slot: wkv_bf[:, slot, :, :]
                             .rearrange("p f c -> p (f c)"),
                             lambda slot=slot: stage_w[:, slot, 0:512],
                             reads=[("stw", slot)], writes=[("wkv", slot)])
                        for ff in range(4):
                            kc = kc4 * 4 + ff
                            for tq in range(4):
                                half, col = tq // 2, (tq % 2) * 512
                                src = x_own if half == 0 else x_oth
                                p("pe", lambda t, kc=kc, slot=slot, ff=ff, tq=tq, src=src, col=col:
                                    t.matmul(ps[tq][:, :], wkv_bf[:, slot, ff, :],
                                             src[:, kc, col:col + 512],
                                             start=(kc == 0), stop=(kc == NKC - 1)),
                                  reads=[("wkv", slot), ("x", half, kc)], writes=[("ps", tq)])
                    dstt = k_fm if ocp == 0 else v_fm
                    for tq in range(4):
                        p("act", lambda e, tq=tq, dstt=dstt:
                            e.copy(out=dstt[:, tq * 512:(tq + 1) * 512], in_=ps[tq][:, :]),
                          reads=[("ps", tq)], writes=[("kv", ocp, tq)])
                        if ocp == 0:
                            p("act", lambda e, tq=tq:
                                e.square(out=ksq[:, :], in_=k_fm[:, tq * 512:(tq + 1) * 512]),
                              reads=[("kv", 0, tq)], writes=[("ksq",)])
                            p("pe", lambda t: t.matmul(ps[4][:, :], ones16[:, :], ksq[:, :],
                                                       start=True, stop=True),
                              reads=[("ksq",), ("ones16",)], writes=[("ps", 4)])
                            if tq == 0:
                                p("dve", lambda v: v.reduce_max(out=kmax2[:, :], in_=ps[4][:, :],
                                                                axis=AX.X),
                                  reads=[("ps", 4)], writes=[("kmax2",)])
                            else:
                                p("dve", lambda v: v.reduce_max(out=kn_tmp[:, :], in_=ps[4][:, :],
                                                                axis=AX.X),
                                  reads=[("ps", 4)], writes=[("kntmp",)])
                                p("dve", lambda v: v.tensor_tensor(out=kmax2[:, :], in0=kmax2[:, :],
                                                                   in1=kn_tmp[:, :], op=ALU.max),
                                  reads=[("kntmp",), ("kmax2",)], writes=[("kmax2",)])
                for fp in range(16):
                    for kc8 in range(4):
                        slot = (fp * 4 + kc8) % 2
                        tslot = (fp * 4 + kc8) % 2
                        p("sync", lambda s, fp=fp, kc8=kc8, slot=slot:
                            s.dma_start(out=stage_w[:, slot, :]
                                        .rearrange("p (f c) -> p f c", f=8),
                                        in_=wo[kc8 * 1024:(kc8 + 1) * 1024,
                                               fp * 256:(fp + 1) * 256]
                                        .rearrange("(f p) c -> p f c", p=128)),
                          writes=[("stw", slot)], dma=True)
                        p("pool", lambda g, slot=slot, tslot=tslot:
                            g.tensor_copy(out=cvt16[:, tslot, :], in_=stage_w[:, slot, :]),
                          reads=[("stw", slot)], writes=[("cvt", tslot)])
                        p("act", lambda s, fp=fp, kc8=kc8, tslot=tslot:
                            s.dma_start(out=wo16[fp, :, kc8 * 2048:(kc8 + 1) * 2048],
                                        in_=cvt16[:, tslot, :]),
                          reads=[("cvt", tslot)], dma=True)
                if debug:
                    p("sync", lambda s: s.dma_start(out=dbg_xown[:, :],
                                                    in_=x_own[:, :, :].rearrange("p a b -> p (a b)")),
                      reads=[("x", 0, 0), ("x", 0, 31)], dma=True)
                    p("sync", lambda s: s.dma_start(out=dbg_kmax[:, :], in_=kmax2[:, :]),
                      reads=[("kmax2",)], dma=True)
                    p("sync", lambda s: s.dma_start(out=dbg_kfm[:, :], in_=k_fm[:, :]),
                      reads=[("kv", 0, 0), ("kv", 0, 1), ("kv", 0, 2), ("kv", 0, 3)], dma=True)
                    p("sync", lambda s: s.dma_start(out=dbg_dqf[:, :], in_=dq_f[:, :]),
                      reads=[("dqf",)], dma=True)
                for kt in range(16):
                    p("pe", lambda t, kt=kt: t.transpose(ps_t16[:, :],
                                                         v_fm[:, kt * 128:(kt + 1) * 128],
                                                         ident16[:, :]),
                      reads=[("kv", 1, kt // 4), ("id16",)], writes=[("pst16",)])
                    p("act", lambda e, kt=kt: e.copy(out=v_tok[:, kt * 128:(kt + 1) * 128],
                                                     in_=ps_t16[:, :]),
                      reads=[("pst16",)], writes=[("vtok", kt)])
                prog.flush(block)
            prog.barrier()
            prog.flush(block)

            with ExitStack() as sA2:
                qblk = sb(sA2, "qblk", [128, 2, 16, 512], BF16)
                q_sb = sb(sA2, "q_sb", [128, 4, 512], BF16)
                qsq = sb(sA2, "qsq", [128, 4, 512], BF16)
                colvec = sb(sA2, "colvec", [128, 4, 512], F32)
                acc_e = sb(sA2, "acc_e", [128, 4, 512], F32)
                sc_b = sb(sA2, "sc_b", [128, 4, 512], F32)
                e_b = sb(sA2, "e_b", [128, 12, 512], BF16)
                linv = sb(sA2, "linv", [128, 2, 512], F32)
                attn_o = sb(sA2, "attn_o", [128, 4, 512], BF16)

                for hg in range(8):
                    for kc4 in range(NKC // 4):
                        slot = kc4 % 2
                        p("sync", lambda s, kc4=kc4, slot=slot, hg=hg:
                            s.dma_start(out=stage_w[:, slot, :]
                                        .rearrange("p (f c) -> p f c", f=4),
                                        in_=wqkv[kc4 * 512:(kc4 + 1) * 512,
                                                 hg * 512:(hg + 1) * 512]
                                        .rearrange("(f p) c -> p f c", p=128)),
                          writes=[("stw", slot)], dma=True)
                        half, kk0 = kc4 // 4, (kc4 % 4) * 4
                        conv("act" if kc4 % 4 == 3 else "dve",
                             lambda half=half, kk0=kk0: qblk[:, half, kk0:kk0 + 4, :]
                             .rearrange("p f c -> p (f c)"),
                             lambda slot=slot: stage_w[:, slot, :],
                             reads=[("stw", slot)], writes=[("qblk", kc4)])
                    for hp in range(4):
                        h = hg * 4 + hp
                        if h < NO_COLVEC_H:
                            p("dve", lambda v, hp=hp, h=h:
                                v.tensor_scalar_mul(colvec[:, hp, :], dq_f[:, :], -SLOPES[h]),
                              reads=[("dqf",)], writes=[("colvec", hp)])
                    for qt in range(2):
                        for hp in range(4):
                            for kc in range(NKC):
                                half, kk = kc // 16, kc % 16
                                rd = ([("qblk", g) for g in range(8)]
                                      if kc == 0 else [("qblk", kc // 4)])
                                p("pe", lambda t, hp=hp, kc=kc, half=half, kk=kk, qt=qt:
                                    t.matmul(ps[2 + hp][:, :],
                                             qblk[:, half, kk, hp * 128:(hp + 1) * 128],
                                             x_own[:, kc, qt * 512:(qt + 1) * 512],
                                             start=(kc == 0), stop=(kc == NKC - 1)),
                                  reads=rd + [("x", 0, kc)],
                                  writes=[("ps", 2 + hp)])
                            p("act", lambda e, hp=hp:
                                e.activation(out=q_sb[:, hp, :], in_=ps[2 + hp][:, :],
                                             func=AF.Copy, bias=0.0, scale=SCALE),
                              reads=[("ps", 2 + hp)], writes=[("qsb", hp)])
                        for hp in range(4):
                            p("pool", lambda g, hp=hp:
                                g.tensor_tensor(out=qsq[:, hp, :], in0=q_sb[:, hp, :],
                                                in1=q_sb[:, hp, :], op=ALU.mult),
                              reads=[("qsb", hp)], writes=[("qsq", hp)])
                            p("pe", lambda t, hp=hp: t.matmul(ps[hp % 2][:, :], ones16[:, :],
                                                              qsq[:, hp, :], start=True, stop=True),
                              reads=[("qsq", hp), ("ones16",)], writes=[("ps", hp % 2)])
                            p("dve", lambda v, hp=hp:
                                v.reduce_max(out=qn2[:, hp:hp + 1], in_=ps[hp % 2][:, :], axis=AX.X),
                              reads=[("ps", hp % 2)], writes=[("qn2", hp)])
                            p("dve", lambda v, hp=hp:
                                v.tensor_tensor(out=cvec[:, hp:hp + 1], in0=qn2[:, hp:hp + 1],
                                                in1=kmax2[:, :], op=ALU.mult),
                              reads=[("qn2", hp), ("kmax2",)], writes=[("cv", hp)])
                            p("act", lambda e, hp=hp:
                                e.activation(out=cvec[:, hp:hp + 1], in_=cvec[:, hp:hp + 1],
                                             func=AF.Sqrt, bias=0.0, scale=1.0),
                              reads=[("cv", hp)], writes=[("cv", hp)])
                            p("dve", lambda v, hp=hp:
                                v.tensor_scalar_add(cvec[:, hp:hp + 1], cvec[:, hp:hp + 1], MARGIN),
                              reads=[("cv", hp)], writes=[("cv", hp)])
                            h = hg * 4 + hp
                            col0 = (h * 2 + qt) * 16
                            p("dve", lambda v, hp=hp, col0=col0:
                                v.tensor_scalar(out=bias_all[:, hp, :],
                                                in0=alibi_t[:, col0:col0 + 16],
                                                scalar1=cvec[:, hp:hp + 1], scalar2=0.0,
                                                op0=ALU.subtract, op1=ALU.subtract),
                              reads=[("alibi",), ("cv", hp)], writes=[("bias", hp)])
                        if debug and hg == 0 and qt == 0:
                            p("sync", lambda s: s.dma_start(out=dbg_qsb[:, :],
                                                            in_=q_sb[:, :, :].rearrange("p a b -> p (a b)")),
                              reads=[("qsb", 0), ("qsb", 1), ("qsb", 2), ("qsb", 3)], dma=True)
                            p("sync", lambda s: s.dma_start(out=dbg_qn2[:, :], in_=qn2[:, :]),
                              reads=[("qn2", 0), ("qn2", 1), ("qn2", 2), ("qn2", 3)], dma=True)
                            p("sync", lambda s: s.dma_start(out=dbg_cvec[:, :], in_=cvec[:, :]),
                              reads=[("cv", 0), ("cv", 1), ("cv", 2), ("cv", 3)], dma=True)
                            p("sync", lambda s: s.dma_start(out=dbg_bias[:, :],
                                                            in_=bias_all[:, :, :].rearrange("p a b -> p (a b)")),
                              reads=[("bias", 0), ("bias", 1), ("bias", 2), ("bias", 3)], dma=True)
                        def emit_av(kt):
                            for hp in range(4):
                                ebs = (kt % 3) * 4 + hp
                                p("pe", lambda t, kt=kt, ebs=ebs, hp=hp:
                                    t.matmul(ps[2 + hp][:, :], v_tok[:, kt * 128:(kt + 1) * 128],
                                             e_b[:, ebs, :], start=(kt == 0), stop=(kt == 15)),
                                  reads=[("vtok", kt), ("eb", ebs)], writes=[("ps", 2 + hp)])

                        def emit_acc(kt):
                            for hp in range(4):
                                ebs = (kt % 3) * 4 + hp
                                if kt == 0:
                                    p("dve", lambda v, ebs=ebs, hp=hp:
                                        v.tensor_copy(out=acc_e[:, hp, :], in_=e_b[:, ebs, :]),
                                      reads=[("eb", ebs)], writes=[("acc", hp)])
                                else:
                                    p("dve", lambda v, ebs=ebs, hp=hp:
                                        v.tensor_tensor(out=acc_e[:, hp, :], in0=acc_e[:, hp, :],
                                                        in1=e_b[:, ebs, :], op=ALU.add),
                                      reads=[("eb", ebs), ("acc", hp)], writes=[("acc", hp)])

                        for kt in range(16):
                            partial_i = kt - 4 * qt
                            if kt >= 2:
                                emit_av(kt - 2)
                            if kt >= 1:
                                emit_acc(kt - 1)
                            for hp in range(4):
                                h = hg * 4 + hp
                                spx = [0, 1, 6][(kt * 4 + hp) % 3]
                                scs = (kt * 4 + hp) % 4
                                ebs = (kt % 3) * 4 + hp
                                p("pe", lambda t, kt=kt, hp=hp, spx=spx:
                                    t.matmul(ps[spx][:, :], k_fm[:, kt * 128:(kt + 1) * 128],
                                             q_sb[:, hp, :], start=True, stop=True),
                                  reads=[("kv", 0, kt // 4), ("qsb", hp)], writes=[("ps", spx)])
                                partial = 0 <= partial_i < 4
                                if h < NO_COLVEC_H:
                                    p("dve", lambda v, spx=spx, scs=scs, hp=hp:
                                        v.tensor_tensor(out=sc_b[:, scs, :], in0=ps[spx][:, :],
                                                        in1=colvec[:, hp, :], op=ALU.add),
                                      reads=[("ps", spx), ("colvec", hp)], writes=[("sc", scs)])
                                    if partial:
                                        p("dve", lambda v, scs=scs, pi=partial_i:
                                            v.tensor_tensor(out=sc_b[:, scs, :], in0=sc_b[:, scs, :],
                                                            in1=maskb[:, pi, :], op=ALU.add),
                                          reads=[("sc", scs), ("mask", pi)], writes=[("sc", scs)])
                                    p("act", lambda e, scs=scs, ebs=ebs, hp=hp, kt=kt:
                                        e.activation(out=e_b[:, ebs, :], in_=sc_b[:, scs, :],
                                                     func=AF.Exp, bias=bias_all[:, hp, kt:kt + 1],
                                                     scale=1.0),
                                      reads=[("sc", scs), ("bias", hp)], writes=[("eb", ebs)])
                                elif partial:
                                    p("dve", lambda v, spx=spx, scs=scs, pi=partial_i:
                                        v.tensor_tensor(out=sc_b[:, scs, :], in0=ps[spx][:, :],
                                                        in1=maskb[:, pi, :], op=ALU.add),
                                      reads=[("ps", spx), ("mask", pi)], writes=[("sc", scs)])
                                    p("act", lambda e, scs=scs, ebs=ebs, hp=hp, kt=kt:
                                        e.activation(out=e_b[:, ebs, :], in_=sc_b[:, scs, :],
                                                     func=AF.Exp, bias=bias_all[:, hp, kt:kt + 1],
                                                     scale=1.0),
                                      reads=[("sc", scs), ("bias", hp)], writes=[("eb", ebs)])
                                else:
                                    p("act", lambda e, spx=spx, ebs=ebs, hp=hp, kt=kt:
                                        e.activation(out=e_b[:, ebs, :], in_=ps[spx][:, :],
                                                     func=AF.Exp, bias=bias_all[:, hp, kt:kt + 1],
                                                     scale=1.0),
                                      reads=[("ps", spx), ("bias", hp)], writes=[("eb", ebs)])
                        emit_av(14)
                        emit_acc(15)
                        emit_av(15)
                        if debug and hg == 0 and qt == 0:
                            p("sync", lambda s: s.dma_start(out=dbg_acc[:, :],
                                                            in_=acc_e[:, :, :].rearrange("p a b -> p (a b)")),
                              reads=[("acc", 0), ("acc", 1), ("acc", 2), ("acc", 3)], dma=True)
                        for hp in range(4):
                            lp = hp % 2
                            p("pe", lambda t, hp=hp, lp=lp:
                                t.matmul(ps[lp][:, :], ones32f[:, :], acc_e[:, hp, :],
                                         start=True, stop=True),
                              reads=[("acc", hp), ("ones32f",)], writes=[("ps", lp)])
                            p("dve", lambda v, lp=lp:
                                v.reciprocal(out=linv[:, lp, :], in_=ps[lp][:, :]),
                              reads=[("ps", lp)], writes=[("linv", lp)])
                            p("dve", lambda v, hp=hp, lp=lp:
                                v.tensor_tensor(out=attn_o[:, hp, :], in0=ps[2 + hp][:, :],
                                                in1=linv[:, lp, :], op=ALU.mult),
                              reads=[("ps", 2 + hp), ("linv", lp)], writes=[("ao", hp)])
                            h = hg * 4 + hp
                            p("act", lambda s, h=h, hp=hp, qt=qt:
                                s.dma_start(out=attn_spill[h * 128:(h + 1) * 128,
                                                           qt * 512:(qt + 1) * 512],
                                            in_=attn_o[:, hp, :]),
                              reads=[("ao", hp)], dma=True)
                prog.flush(block)
        prog.barrier()
        prog.flush(block)

        with ExitStack() as sB:
            attn_sb = sb(sB, "attn_sb", [128, NKC, 1024], BF16)
            woblk = sb(sB, "woblk", [128, 2, NKC, 256], BF16)
            rb = sb(sB, "rb", [128, 2, 2, 512], BF16)
            sqb = sb(sB, "sqb", [128, 2, 2, 512], BF16)
            rc = sb(sB, "rc", [128, 4, 512], F32)
            xs = sb(sB, "xs", [128, 2, 512], F32)
            cvt5 = sb(sB, "cvt5", [128, 2, 1024], BF16)
            pend_stats = []
            for h4 in range(NKC // 4):
                p("sync", lambda s, h4=h4: s.dma_start(
                    out=attn_sb[:, h4 * 4:(h4 + 1) * 4, :],
                    in_=attn_spill[h4 * 512:(h4 + 1) * 512, :]
                    .rearrange("(f p) c -> p f c", p=128)),
                  writes=[("at", h4 * 4 + i) for i in range(4)], dma=True)
            def load_wo(fp):
                par = fp % 2
                p("sync", lambda s, fp=fp, par=par:
                    s.dma_start(out=woblk[:, par, :, :].rearrange("p f c -> p (f c)"),
                                in_=wo16[fp, :, :]),
                  writes=[("wob", par)], dma=True)

            load_wo(0)
            for fp in range(16):
                par = fp % 2
                if fp + 1 < 16:
                    load_wo(fp + 1)
                for k in range(22):
                    idx = fp * 22 + k
                    fp_d, ig = divmod(idx, 22)
                    ic0, gn = ICG[ig]
                    slot = idx % 2
                    p("sync", lambda s, fp_d=fp_d, ic0=ic0, gn=gn, slot=slot:
                        s.dma_start(out=stage_w[:, slot, 0:gn * 256]
                                    .rearrange("p (f c) -> p f c", f=gn),
                                    in_=wdt[ic0 * 128:(ic0 + gn) * 128,
                                            fp_d * 256:(fp_d + 1) * 256]
                                    .rearrange("(f p) c -> p f c", p=128)),
                      writes=[("stw", slot)], dma=True)
                    p("pool", lambda g, slot=slot, gn=gn:
                        g.tensor_copy(out=cvt5[:, slot, 0:gn * 256],
                                      in_=stage_w[:, slot, 0:gn * 256]),
                      reads=[("stw", slot)], writes=[("cvt5", slot)])
                    p("act", lambda s, fp_d=fp_d, ic0=ic0, gn=gn, slot=slot:
                        s.dma_start(out=wd16[fp_d, :, ic0 * 256:(ic0 + gn) * 256],
                                    in_=cvt5[:, slot, 0:gn * 256]),
                      reads=[("cvt5", slot)], dma=True)
                for tq in range(2):
                    grp = fp * 2 + tq
                    gp = grp % 2
                    p("sync", lambda s, fp=fp, tq=tq:
                        s.dma_start(out=xs[:, :, :],
                                    in_=xt[fp * 256:(fp + 1) * 256,
                                           tq * 512:(tq + 1) * 512]
                                    .rearrange("(f p) c -> p f c", p=128)),
                      writes=[("xs", 0), ("xs", 1)], dma=True)
                    for j in range(2):
                        for kc in range(NKC):
                            rd = [("wob", par)]
                            p("pe", lambda t, par=par, kc=kc, j=j, tq=tq:
                                t.matmul(ps[j][:, :], woblk[:, par, kc, j * 128:(j + 1) * 128],
                                         attn_sb[:, kc, tq * 512:(tq + 1) * 512],
                                         start=(kc == 0), stop=(kc == NKC - 1)),
                              reads=rd + [("at", kc)], writes=[("ps", j)])
                    for th in pend_stats:
                        th()
                    pend_stats = []
                    for j in range(2):
                        foc = fp * 2 + j
                        rslot = (foc * 2 + tq) % 4
                        p("dve", lambda v, j=j, rslot=rslot:
                            v.tensor_tensor(out=rc[:, rslot, :], in0=ps[j][:, :],
                                            in1=xs[:, j, :], op=ALU.add),
                          reads=[("ps", j), ("xs", j)], writes=[("rc", rslot)])
                        p("act", lambda s, foc=foc, tq=tq, rslot=rslot:
                            s.dma_start(out=rbuf[foc * 128:(foc + 1) * 128,
                                                 tq * 512:(tq + 1) * 512],
                                        in_=rc[:, rslot, :]),
                          reads=[("rc", rslot)], dma=True)
                        p("act", lambda e, rslot=rslot, gp=gp, j=j:
                            e.copy(out=rb[:, gp, j, :], in_=rc[:, rslot, :]),
                          reads=[("rc", rslot)], writes=[("rb", gp, j)])
                        p("act", lambda e, gp=gp, j=j: e.square(out=sqb[:, gp, j, :],
                                                                in_=rb[:, gp, j, :]),
                          reads=[("rb", gp, j)], writes=[("sqb", gp, j)])

                        def mk_stats(j=j, tq=tq, foc=foc, gp=gp):
                            def th():
                                p("pe", lambda t: t.matmul(ps[4 + tq][:, :], ones16[:, :],
                                                           rb[:, gp, j, :],
                                                           start=(foc == 0), stop=(foc == NKC - 1)),
                                  reads=[("rb", gp, j), ("ones16",)], writes=[("ps", 4 + tq)])
                                p("pe", lambda t: t.matmul(ps[2 + tq][:, :], ones16[:, :],
                                                           sqb[:, gp, j, :],
                                                           start=(foc == 0), stop=(foc == NKC - 1)),
                                  reads=[("sqb", gp, j), ("ones16",)], writes=[("ps", 2 + tq)])
                            return th
                        pend_stats.append(mk_stats())
            for th in pend_stats:
                th()
            pend_stats = []
            prog.flush(block)
        prog.barrier()
        prog.flush(block)

        with ExitStack() as sC:
            y1 = sb(sC, "y1", [128, NKC, 1024], BF16)

            with ExitStack() as sC1:
                stage_b = sb(sC1, "stage_b6", [128, 2, 1024], F32)
                varb = sb(sC1, "varb", [128, 1024], F32)
                for tq in range(2):
                    p("act", lambda e, tq=tq:
                        e.activation(out=mean_bc[:, tq * 512:(tq + 1) * 512],
                                     in_=ps[4 + tq][:, :], func=AF.Copy, bias=0.0, scale=1.0 / D),
                      reads=[("ps", 4 + tq)], writes=[("meanbc",)])
                    p("act", lambda e, tq=tq:
                        e.activation(out=varb[:, tq * 512:(tq + 1) * 512],
                                     in_=ps[2 + tq][:, :], func=AF.Copy, bias=0.0, scale=1.0 / D),
                      reads=[("ps", 2 + tq)], writes=[("varb",)])
                p("dve", lambda v: v.tensor_tensor(out=rstd_bc[:, :], in0=mean_bc[:, :],
                                                   in1=mean_bc[:, :], op=ALU.mult),
                  reads=[("meanbc",)], writes=[("rstdbc",)])
                p("dve", lambda v: v.tensor_tensor(out=varb[:, :], in0=varb[:, :],
                                                   in1=rstd_bc[:, :], op=ALU.subtract),
                  reads=[("varb",), ("rstdbc",)], writes=[("varb",)])
                p("act", lambda e: e.activation(out=varb[:, :], in_=varb[:, :], func=AF.Sqrt,
                                                bias=eps_t[:, :], scale=1.0),
                  reads=[("varb",), ("eps",)], writes=[("varb",)])
                p("dve", lambda v: v.reciprocal(out=rstd_bc[:, :], in_=varb[:, :]),
                  reads=[("varb",)], writes=[("rstdbc",)])
                for kc in range(NKC):
                    slot = kc % 2
                    p("sync", lambda s, kc=kc, slot=slot:
                        s.dma_start(out=stage_b[:, slot, :],
                                    in_=rbuf[kc * 128:(kc + 1) * 128, :]),
                      writes=[("stb", slot)], dma=True)
                    p("dve", lambda v, slot=slot:
                        v.tensor_tensor(out=stage_b[:, slot, :], in0=stage_b[:, slot, :],
                                        in1=mean_bc[:, :], op=ALU.subtract),
                      reads=[("stb", slot), ("meanbc",)], writes=[("stb", slot)])
                    p("dve", lambda v, kc=kc, slot=slot:
                        v.tensor_tensor(out=y1[:, kc, :], in0=stage_b[:, slot, :],
                                        in1=rstd_bc[:, :], op=ALU.mult),
                      reads=[("stb", slot), ("rstdbc",)], writes=[("y1", kc)])
                prog.flush(block)
            prog.barrier()
            prog.flush(block)

            with ExitStack() as sC2:
                wgblk = sb(sC2, "wgblk", [128, 2, NKC, 256], BF16)
                wublk = sb(sC2, "wublk", [128, 2, NKC, 256], BF16)
                sgb = sb(sC2, "sgb", [128, 2, 512], F32)
                sub_ = sb(sC2, "sub", [128, 2, 512], F32)
                hb = sb(sC2, "hb", [128, 4, 512], BF16)
                for ip in range(43):
                    par = ip % 2
                    for kc4 in range(NKC // 4):
                        slot = kc4 % 2
                        p("sync", lambda s, ip=ip, kc4=kc4, slot=slot:
                            s.dma_start(out=stage_w[:, slot, 0:1024]
                                        .rearrange("p (f c) -> p f c", f=4),
                                        in_=wgt[kc4 * 512:(kc4 + 1) * 512,
                                                ip * 256:(ip + 1) * 256]
                                        .rearrange("(f p) c -> p f c", p=128)),
                          writes=[("stw", slot)], dma=True)
                        p("pool", lambda g, par=par, kc4=kc4, slot=slot:
                            g.tensor_copy(out=wgblk[:, par, kc4 * 4:(kc4 + 1) * 4, :]
                                          .rearrange("p f c -> p (f c)"),
                                          in_=stage_w[:, slot, 0:1024]),
                          reads=[("stw", slot)], writes=[("wgb", par, kc4)])
                        p("sync", lambda s, ip=ip, kc4=kc4, slot=slot:
                            s.dma_start(out=stage_w[:, slot, 1024:2048]
                                        .rearrange("p (f c) -> p f c", f=4),
                                        in_=wut[kc4 * 512:(kc4 + 1) * 512,
                                                ip * 256:(ip + 1) * 256]
                                        .rearrange("(f p) c -> p f c", p=128)),
                          writes=[("stw2", slot)], dma=True)
                        p("pool", lambda g, par=par, kc4=kc4, slot=slot:
                            g.tensor_copy(out=wublk[:, par, kc4 * 4:(kc4 + 1) * 4, :]
                                          .rearrange("p f c -> p (f c)"),
                                          in_=stage_w[:, slot, 1024:2048]),
                          reads=[("stw2", slot)], writes=[("wub", par, kc4)])
                    for jj in range(2):
                        ioc = ip * 2 + jj
                        for tq in range(2):
                            pg = 2 * ((jj + tq) % 2)
                            pu = pg + 1
                            for kc in range(NKC):
                                rdg = ([("wgb", par, k2) for k2 in range(NKC // 4)] if kc == 0
                                       else [("wgb", par, kc // 4)])
                                rdu = ([("wub", par, k2) for k2 in range(NKC // 4)] if kc == 0
                                       else [("wub", par, kc // 4)])
                                p("pe", lambda t, par=par, kc=kc, jj=jj, tq=tq, pg=pg:
                                    t.matmul(ps[pg][:, :],
                                             wgblk[:, par, kc, jj * 128:(jj + 1) * 128],
                                             y1[:, kc, tq * 512:(tq + 1) * 512],
                                             start=(kc == 0), stop=(kc == NKC - 1)),
                                  reads=rdg + [("y1", kc)], writes=[("ps", pg)])
                                p("pe", lambda t, par=par, kc=kc, jj=jj, tq=tq, pu=pu:
                                    t.matmul(ps[pu][:, :],
                                             wublk[:, par, kc, jj * 128:(jj + 1) * 128],
                                             y1[:, kc, tq * 512:(tq + 1) * 512],
                                             start=(kc == 0), stop=(kc == NKC - 1)),
                                  reads=rdu + [("y1", kc)], writes=[("ps", pu)])
                            p("act", lambda e, pg=pg, tq=tq, ioc=ioc:
                                e.activation(out=sgb[:, tq, :], in_=ps[pg][:, :], func=AF.Silu,
                                             bias=0.0, scale=ag_t[:, ioc:ioc + 1]),
                              reads=[("ps", pg), ("ag",)], writes=[("sg", tq)])
                            p("act", lambda e, pu=pu, tq=tq, ioc=ioc:
                                e.activation(out=sub_[:, tq, :], in_=ps[pu][:, :], func=AF.Copy,
                                             bias=0.0, scale=au_t[:, ioc:ioc + 1]),
                              reads=[("ps", pu), ("au",)], writes=[("su", tq)])
                            hs = (ioc * 2 + tq) % 4
                            p("dve", lambda v, tq=tq, hs=hs:
                                v.tensor_tensor(out=hb[:, hs, :], in0=sgb[:, tq, :],
                                                in1=sub_[:, tq, :], op=ALU.mult),
                              reads=[("sg", tq), ("su", tq)], writes=[("hb", hs)])
                            p("act", lambda s, ioc=ioc, tq=tq, hs=hs:
                                s.dma_start(out=hbuf[ioc * 128:(ioc + 1) * 128,
                                                     tq * 512:(tq + 1) * 512],
                                            in_=hb[:, hs, :]),
                              reads=[("hb", hs)], dma=True)
                prog.flush(block)
        prog.barrier()
        prog.flush(block)

        with ExitStack() as sDE:
          m2_raw = sb(sDE, "m2_raw", [128, 1024], F32)
          msq2_raw = sb(sDE, "msq2_raw", [128, 1024], F32)
          with ExitStack() as sD:
            h_sb = sb(sD, "h_sb", [128, NI, 512], BF16)
            wdb = sb(sD, "wdb", [128, 4, 4, 256], BF16)
            fb = sb(sD, "fb", [128, 2, 512], F32)
            tb = sb(sD, "tb", [128, 2, 512], F32)
            r2c = sb(sD, "r2c", [128, 4, 512], F32)
            r2b = sb(sD, "r2b", [128, 2, 2, 512], BF16)
            sq2 = sb(sD, "sq2", [128, 2, 2, 512], BF16)
            IC_GROUPS = [(g * 4, min(4, NI - g * 4)) for g in range((NI + 3) // 4)]
            pend2 = []
            for tq in range(2):
                for ig in range((NI + 3) // 4):
                    ic0, gn = IC_GROUPS[ig]
                    p("sync", lambda s, ic0=ic0, gn=gn, tq=tq:
                        s.dma_start(out=h_sb[:, ic0:ic0 + gn, :],
                                    in_=hbuf[ic0 * 128:(ic0 + gn) * 128,
                                             tq * 512:(tq + 1) * 512]
                                    .rearrange("(f p) c -> p f c", p=128)),
                      writes=[("hs", ic0 + i) for i in range(gn)], dma=True)
                for fp in range(16):
                    for ig in range((NI + 3) // 4):
                        ic0, gn = IC_GROUPS[ig]
                        slot = ig % 4
                        p("sync", lambda s, fp=fp, ic0=ic0, gn=gn, slot=slot:
                            s.dma_start(out=wdb[:, slot, 0:gn, :].rearrange("p f c -> p (f c)"),
                                        in_=wd16[fp, :, ic0 * 256:(ic0 + gn) * 256]),
                          writes=[("wdb", slot)], dma=True)
                        for i in range(gn):
                            ic = ic0 + i
                            for j in range(2):
                                pslot = (fp % 2) * 2 + j
                                p("pe", lambda t, slot=slot, i=i, j=j, ic=ic, pslot=pslot:
                                    t.matmul(ps[pslot][:, :],
                                             wdb[:, slot, i, j * 128:(j + 1) * 128],
                                             h_sb[:, ic, :], start=(ic == 0), stop=(ic == NI - 1)),
                                  reads=[("wdb", slot), ("hs", ic)], writes=[("ps", pslot)])
                    for th in pend2:
                        th()
                    pend2 = []
                    gp = fp % 2
                    p("sync", lambda s, fp=fp, tq=tq:
                        s.dma_start(out=tb[:, :, :],
                                    in_=rbuf[fp * 256:(fp + 1) * 256,
                                             tq * 512:(tq + 1) * 512]
                                    .rearrange("(f p) c -> p f c", p=128)),
                      writes=[("tb", 0), ("tb", 1)], dma=True)
                    for j in range(2):
                        foc = fp * 2 + j
                        pslot = (fp % 2) * 2 + j
                        p("act", lambda e, pslot=pslot, j=j, foc=foc:
                            e.activation(out=fb[:, j, :], in_=ps[pslot][:, :], func=AF.Copy,
                                         bias=0.0, scale=ad_t[:, foc:foc + 1]),
                          reads=[("ps", pslot), ("ad",)], writes=[("fb", j)])
                        p("dve", lambda v, j=j, tq=tq:
                            v.tensor_tensor(out=tb[:, j, :], in0=tb[:, j, :],
                                            in1=mean_bc[:, tq * 512:(tq + 1) * 512],
                                            op=ALU.subtract),
                          reads=[("tb", j), ("meanbc",)], writes=[("tb", j)])
                        p("dve", lambda v, j=j, tq=tq:
                            v.tensor_tensor(out=tb[:, j, :], in0=tb[:, j, :],
                                            in1=rstd_bc[:, tq * 512:(tq + 1) * 512],
                                            op=ALU.mult),
                          reads=[("tb", j), ("rstdbc",)], writes=[("tb", j)])
                        r2s = (foc * 2 + tq) % 4
                        p("dve", lambda v, j=j, r2s=r2s:
                            v.tensor_tensor(out=r2c[:, r2s, :], in0=tb[:, j, :],
                                            in1=fb[:, j, :], op=ALU.add),
                          reads=[("tb", j), ("fb", j)], writes=[("r2c", r2s)])
                        p("act", lambda s, foc=foc, tq=tq, r2s=r2s:
                            s.dma_start(out=r2buf[foc * 128:(foc + 1) * 128,
                                                  tq * 512:(tq + 1) * 512],
                                        in_=r2c[:, r2s, :]),
                          reads=[("r2c", r2s)], dma=True)
                        p("act", lambda e, r2s=r2s, gp=gp, j=j:
                            e.copy(out=r2b[:, gp, j, :], in_=r2c[:, r2s, :]),
                          reads=[("r2c", r2s)], writes=[("r2b", gp, j)])
                        p("act", lambda e, gp=gp, j=j: e.square(out=sq2[:, gp, j, :],
                                                                in_=r2b[:, gp, j, :]),
                          reads=[("r2b", gp, j)], writes=[("sq2", gp, j)])

                        def mk_stats2(j=j, foc=foc, gp=gp):
                            def th():
                                p("pe", lambda t: t.matmul(ps[4][:, :], ones16[:, :],
                                                           r2b[:, gp, j, :],
                                                           start=(foc == 0), stop=(foc == NKC - 1)),
                                  reads=[("r2b", gp, j), ("ones16",)], writes=[("ps", 4)])
                                p("pe", lambda t: t.matmul(ps[5][:, :], ones16[:, :],
                                                           sq2[:, gp, j, :],
                                                           start=(foc == 0), stop=(foc == NKC - 1)),
                                  reads=[("sq2", gp, j), ("ones16",)], writes=[("ps", 5)])
                            return th
                        pend2.append(mk_stats2())
                for th in pend2:
                    th()
                pend2 = []
                p("act", lambda e, tq=tq: e.activation(out=m2_raw[:, tq * 512:(tq + 1) * 512],
                                                       in_=ps[4][:, :], func=AF.Copy,
                                                       bias=0.0, scale=1.0 / D),
                  reads=[("ps", 4)], writes=[("m2raw",)])
                p("act", lambda e, tq=tq: e.activation(out=msq2_raw[:, tq * 512:(tq + 1) * 512],
                                                       in_=ps[5][:, :], func=AF.Copy,
                                                       bias=0.0, scale=1.0 / D),
                  reads=[("ps", 5)], writes=[("msq2raw",)])
            prog.flush(block)
          prog.barrier()
          prog.flush(block)

          with ExitStack() as sE:
              y2t = sb(sE, "y2t", [128, 4, D], F32)
              y2c = sb(sE, "y2c", [128, 2, 512], F32)
              p("dve", lambda v: v.tensor_copy(out=mean_bc[:, :], in_=m2_raw[:, :]),
                reads=[("m2raw",)], writes=[("meanbc",)])
              p("dve", lambda v: v.tensor_tensor(out=rstd_bc[:, :], in0=mean_bc[:, :],
                                                 in1=mean_bc[:, :], op=ALU.mult),
                reads=[("meanbc",)], writes=[("rstdbc",)])
              p("dve", lambda v: v.tensor_tensor(out=msq2_raw[:, :], in0=msq2_raw[:, :],
                                                 in1=rstd_bc[:, :], op=ALU.subtract),
                reads=[("msq2raw",), ("rstdbc",)], writes=[("msq2raw",)])
              p("act", lambda e: e.activation(out=msq2_raw[:, :], in_=msq2_raw[:, :], func=AF.Sqrt,
                                              bias=eps_t[:, :], scale=1.0),
                reads=[("msq2raw",), ("eps",)], writes=[("msq2raw",)])
              p("dve", lambda v: v.reciprocal(out=rstd_bc[:, :], in_=msq2_raw[:, :]),
                reads=[("msq2raw",)], writes=[("rstdbc",)])
              for th in range(2):
                  for foc in range(NKC):
                      cs = foc % 2
                      p("sync", lambda s, foc=foc, th=th, cs=cs:
                          s.dma_start(out=y2c[:, cs, :],
                                      in_=r2buf[foc * 128:(foc + 1) * 128,
                                                th * 512:(th + 1) * 512]),
                        writes=[("y2c", cs)], dma=True)
                      p("dve", lambda v, cs=cs, th=th:
                          v.tensor_tensor(out=y2c[:, cs, :], in0=y2c[:, cs, :],
                                          in1=mean_bc[:, th * 512:(th + 1) * 512],
                                          op=ALU.subtract),
                        reads=[("y2c", cs), ("meanbc",)], writes=[("y2c", cs)])
                      p("dve", lambda v, cs=cs, th=th:
                          v.tensor_tensor(out=y2c[:, cs, :], in0=y2c[:, cs, :],
                                          in1=rstd_bc[:, th * 512:(th + 1) * 512], op=ALU.mult),
                        reads=[("y2c", cs), ("rstdbc",)], writes=[("y2c", cs)])
                      for i in range(4):
                          p("pe", lambda t, cs=cs, i=i:
                              t.transpose(ps[6][:, 0:128], y2c[:, cs, i * 128:(i + 1) * 128],
                                          ident32[:, :]),
                            reads=[("y2c", cs), ("id32",)], writes=[("ps", 6)])
                          p("act", lambda e, i=i, foc=foc:
                              e.copy(out=y2t[:, i, foc * 128:(foc + 1) * 128], in_=ps[6][:, 0:128]),
                            reads=[("ps", 6)], writes=[("y2t", i)])
                  for i in range(4):
                      tt = th * 4 + i
                      p("sync", lambda s, tt=tt, i=i:
                          s.dma_start(out=out[tt * 128:(tt + 1) * 128, :], in_=y2t[:, i, :]),
                        reads=[("y2t", i)], dma=True)
              prog.flush(block)
        prog.barrier()
        prog.flush(block)

    return nc, prog


def _build(debug=False):
    _, prog1 = build_core_program(mark=None, debug=debug)
    mark = {s: sorted(v) for s, v in prog1.used.items()}
    nc, _ = build_core_program(mark=mark, debug=debug)
    return nc


def _host_stage(L0_attn_hidden_states, L0_attn_Wqkv, L0_attn_Wo, L0_ffn_Wg, L0_ffn_Wu,
                L0_ffn_Wd, L0_ffn_ag, L0_ffn_au, L0_ffn_ad):
    x = np.asarray(L0_attn_hidden_states, dtype=np.float32)
    wqkv = np.ascontiguousarray(np.asarray(L0_attn_Wqkv, dtype=np.float32))
    wo_ = np.ascontiguousarray(np.asarray(L0_attn_Wo, dtype=np.float32))
    wgt = np.ascontiguousarray(np.asarray(L0_ffn_Wg, dtype=np.float32).T)
    wut = np.ascontiguousarray(np.asarray(L0_ffn_Wu, dtype=np.float32).T)
    wdt = np.ascontiguousarray(np.asarray(L0_ffn_Wd, dtype=np.float32).T)
    ag = np.ascontiguousarray(np.asarray(L0_ffn_ag, dtype=np.float32).reshape(NI, 128).T)
    au = np.ascontiguousarray(np.asarray(L0_ffn_au, dtype=np.float32).reshape(NI, 128).T)
    ad = np.ascontiguousarray(np.asarray(L0_ffn_ad, dtype=np.float32).reshape(NKC, 128).T)

    in_maps = []
    dk = np.arange(128)
    for core in range(8):
        b, half = core // 2, core % 2
        xb = x[b].T
        own = xb[:, half * TOK:(half + 1) * TOK]
        oth = xb[:, (1 - half) * TOK:(2 - half) * TOK]
        xtc = np.ascontiguousarray(np.concatenate([own, oth], axis=1))

        alibi = np.zeros((128, NH * 2 * 16), dtype=np.float32)
        own_base = half * TOK
        oth_base = (1 - half) * TOK
        for h in range(NH):
            for qt in range(2):
                q0 = own_base + qt * 512
                for kt in range(16):
                    col = (h * 2 + qt) * 16 + kt
                    if kt < 8:
                        kpos = own_base + kt * 128 + dk
                    else:
                        kpos = oth_base + (kt - 8) * 128 + dk
                    if kpos.min() > q0 + 511:
                        alibi[:, col] = NEG
                    else:
                        alibi[:, col] = SLOPES[h] * (kpos - q0)
        in_maps.append({
            "xt": xtc, "wqkv": wqkv, "wo": wo_, "wgt": wgt, "wut": wut, "wdt": wdt,
            "ag": ag, "au": au, "ad": ad, "alibi": alibi,
        })
    return in_maps


def kernel(**inputs):
    in_maps = _host_stage(**inputs)
    nc = _build()
    res = run_bass_kernel_spmd(nc, in_maps, core_ids=list(range(8)))
    outp = np.empty((B, S, D), dtype=np.float32)
    for core in range(8):
        b, half = core // 2, core % 2
        outp[b, half * TOK:(half + 1) * TOK, :] = res.results[core]["out"]
    return outp
